# version 06
# baseline (speedup 1.0000x reference)
import math
import sys
import time
from contextlib import ExitStack

import numpy as np
import concourse.bass as bass
import concourse.mybir as mybir
from concourse.bass_utils import run_bass_kernel_spmd

F32 = mybir.dt.float32
BF16 = mybir.dt.bfloat16
I32 = mybir.dt.int32
AF = mybir.ActivationFunctionType
ALU = mybir.AluOpType
AX = mybir.AxisListType

D = 1024
P = 128
KD = D // P


class Prog:
    def __init__(self, nc, T):
        self.nc = nc
        self.T = T
        self.sb = {}

    def sb_t(self, name, shape, dtype):
        t = self.nc.alloc_sbuf_tensor(name, shape, dtype)
        self.sb[name] = t
        return t



class BankedPsum:

    def __init__(self, st, nc, name, slots, rows, slot_cols, dtype):
        self.slot_cols = slot_cols
        self.t = [st.enter_context(
            nc.psum_tensor(f"{name}_bk{j}", [rows, slot_cols], dtype))
            for j in range(slots)]

    def __getitem__(self, key):
        rows, cols = key
        a = cols.start or 0
        b = cols.stop
        s = a // self.slot_cols
        a2 = a - s * self.slot_cols
        b2 = b - s * self.slot_cols
        assert 0 <= a2 < b2 <= self.slot_cols, (a, b, self.slot_cols)
        return self.t[s][rows, a2:b2]


def fmv(dram_t):
    return dram_t[:, :].rearrange("(k p) t -> p k t", p=P)


def _apname(x):
    t = getattr(x, "tensor", None)
    return getattr(t, "name", None)


class EA:

    def __init__(self, eng):
        self._e = eng
        self._w = set()
        self._r = set()

    def __getattr__(self, name):
        return getattr(self._e, name)

    def _pre(self, outs, ins):
        no = {n for n in (_apname(a) for a in outs) if n}
        ni = {n for n in (_apname(a) for a in ins) if n}
        if ((no | ni) & self._w) or (no & self._r):
            self._e.drain()
            self._w.clear()
            self._r.clear()
        self._w |= no
        self._r |= ni

    def tensor_copy(self, out, in_):
        self._pre([out], [in_])
        return self._e.tensor_copy(out, in_)

    def tensor_tensor(self, out, a, b, op):
        self._pre([out], [a, b])
        return self._e.tensor_tensor(out, a, b, op)

    def tensor_scalar(self, out, in0, s1, s2, op0, op1=ALU.bypass, accum_out=None):
        outs = [out] + ([accum_out] if accum_out is not None else [])
        self._pre(outs, [in0, s1, s2])
        if accum_out is not None:
            return self._e.tensor_scalar(out, in0, s1, s2, op0, op1,
                                         accum_out=accum_out)
        return self._e.tensor_scalar(out, in0, s1, s2, op0, op1)

    def scalar_tensor_tensor(self, out, in0, s, in1, op0, op1, accum_out=None):
        outs = [out] + ([accum_out] if accum_out is not None else [])
        self._pre(outs, [in0, s, in1])
        if accum_out is not None:
            return self._e.scalar_tensor_tensor(out, in0, s, in1, op0, op1,
                                                accum_out=accum_out)
        return self._e.scalar_tensor_tensor(out, in0, s, in1, op0, op1)

    def tensor_tensor_scan(self, out, d0, d1, initial, op0, op1):
        self._pre([out], [d0, d1, initial])
        return self._e.tensor_tensor_scan(out, d0, d1, initial, op0, op1)

    def reciprocal(self, out, in_):
        self._pre([out], [in_])
        return self._e.reciprocal(out, in_)

    def reduce_sum(self, out, in_, axis, negate=None):
        self._pre([out], [in_])
        if negate is None:
            return self._e.reduce_sum(out, in_, axis=axis)
        return self._e.reduce_sum(out, in_, axis=axis, negate=negate)

    def reduce_max(self, out, in_, axis, negate=None):
        self._pre([out], [in_])
        if negate is None:
            return self._e.reduce_max(out, in_, axis=axis)
        return self._e.reduce_max(out, in_, axis=axis, negate=negate)

    def memset(self, ap, c):
        self._pre([ap], [])
        return self._e.memset(ap, c)

    def select(self, out, mask, a, b):
        self._pre([out], [mask, a, b])
        return self._e.select(out, mask, a, b)

    def max_with_indices(self, out_max, out_idx, in_):
        self._pre([out_max, out_idx], [in_])
        return self._e.max_with_indices(out_max, out_idx, in_)

    def iota(self, out, pattern, base=None, channel_multiplier=None):
        self._pre([out], [])
        kw = {}
        if base is not None:
            kw["base"] = base
        if channel_multiplier is not None:
            kw["channel_multiplier"] = channel_multiplier
        return self._e.iota(out, pattern, **kw)

    def activation(self, out, in_, func, bias=None, scale=None, accum_out=None):
        outs = [out] + ([accum_out] if accum_out is not None else [])
        self._pre(outs, [in_, bias, scale])
        kw = {}
        if bias is not None:
            kw["bias"] = bias
        if scale is not None:
            kw["scale"] = scale
        if accum_out is not None:
            kw["accum_out"] = accum_out
        return self._e.activation(out, in_, func, **kw)

    def wait_ge(self, sem, val):
        return self._e.wait_ge(sem, val)

    def engine_nop(self):
        return self._e.engine_nop()

    def drain(self):
        self._w.clear()
        self._r.clear()
        return self._e.drain()

    def dma_start(self, out, in_, **kw):
        return self._e.dma_start(out, in_, **kw)


def emit_consts(p: Prog):
    nc = p.nc
    p.sb_t("ident_bf", [P, P], BF16)
    p.sb_t("ident_f32", [P, P], F32)
    p.sb_t("ones_col_bf", [P, 1], BF16)
    p.sb_t("ones_col_f32", [P, 1], F32)
    p.sb_t("ones_row_bf", [1, P], BF16)
    p.sb_t("ones_row_f32", [1, P], F32)
    p.sb_t("sel16", [16, 16, P], BF16)
    p.sb_t("swap_bf", [P, P], BF16)
    p.sb_t("bones_bf", [P, 2], BF16)
    p.sb_t("bones2_f32", [2, P], F32)
    with nc.sbuf_tensor("iota_a", [P, P], I32) as ia, \
         nc.sbuf_tensor("iota_b", [P, P], I32) as ib, \
         nc.sbuf_tensor("iota_c", [16, 16, P], I32) as ic, \
         nc.sbuf_tensor("iota_d", [16, 16, P], I32) as idd, \
         nc.sbuf_tensor("iota_e", [2, P], I32) as ie, \
         nc.sbuf_tensor("iota_f", [2, P], I32) as iff:
        with nc.Block("consts1") as blk:
            @blk.gpsimd
            def _(g):
                g.iota(ia[:, :], [[1, P]], channel_multiplier=0)
                g.iota(ib[:, :], [[0, P]], channel_multiplier=1)
                g.iota(ic[:, :, :], [[1, 16], [0, P]], channel_multiplier=0)
                g.iota(idd[:, :, :], [[0, 16], [0, P]], channel_multiplier=1)
                g.iota(ie[:, :], [[1, P]], channel_multiplier=0)
                g.iota(iff[:, :], [[0, P]], channel_multiplier=1)
                g.memset(p.sb["ones_col_bf"][:, :], 1.0)
                g.memset(p.sb["ones_col_f32"][:, :], 1.0)
                g.memset(p.sb["ones_row_bf"][:, :], 1.0)
                g.memset(p.sb["ones_row_f32"][:, :], 1.0)
        with nc.Block("consts2") as blk:
            @blk.vector
            def _(v):
                v.tensor_tensor(p.sb["ident_bf"][:, :], ia[:, :], ib[:, :], ALU.is_equal)
                v.tensor_tensor(p.sb["ident_f32"][:, :], ia[:, :], ib[:, :], ALU.is_equal)
                v.tensor_tensor(p.sb["sel16"][:, :, :], ic[:, :, :], idd[:, :, :],
                                ALU.is_equal)
                v.drain()
                v.tensor_scalar(ia[:, :], ia[:, :], 32, None, ALU.bitwise_xor)
                v.drain()
                v.tensor_tensor(p.sb["swap_bf"][:, :], ia[:, :], ib[:, :],
                                ALU.is_equal)
                v.tensor_scalar(p.sb["bones_bf"][:, 1:2], ib[:, 0:1], 64, None,
                                ALU.is_ge)
                v.tensor_scalar(p.sb["bones_bf"][:, 0:1], ib[:, 0:1], 64, None,
                                ALU.is_lt)
                v.tensor_scalar(ie[:, :], ie[:, :], 64, None, ALU.is_ge)
                v.drain()
                v.tensor_tensor(p.sb["bones2_f32"][:, :], ie[:, :], iff[:, :],
                                ALU.is_equal)



class LnFm:
    def __init__(self, p: Prog, st: ExitStack, tag, C):
        nc = p.nc
        self.p, self.C = p, C
        E = st.enter_context
        self.xb = E(nc.sbuf_tensor(f"{tag}_xb", [P, KD, C], BF16))
        self.x2 = E(nc.sbuf_tensor(f"{tag}_x2", [P, KD, C], BF16))
        self.xc = E(nc.sbuf_tensor(f"{tag}_xc", [P, KD, C], F32))
        self.rows = E(nc.sbuf_tensor(f"{tag}_rows", [1, 2 * C], F32))
        self.tmpr = E(nc.sbuf_tensor(f"{tag}_tmpr", [1, 2 * C], F32))
        self.ps_stat = E(nc.psum_tensor(f"{tag}_ps_stat", [1, 2 * C], F32))
        self.ps_bc = E(nc.psum_tensor(f"{tag}_ps_bc", [P, 2 * C], F32))
        self.sa = nc.alloc_semaphore(f"{tag}_sa")
        self.sp = nc.alloc_semaphore(f"{tag}_sp")
        self.sv = nc.alloc_semaphore(f"{tag}_sv")
        self.sq = nc.alloc_semaphore(f"{tag}_sq")

    def act_emit(self, act, i, xc_sb):
        if i > 0:
            act.wait_ge(self.sv, 2 * i)
        act.activation(self.xb[:, :, :], xc_sb, AF.Copy)
        act.activation(self.x2[:, :, :], xc_sb, AF.Square).then_inc(self.sa, 1)
        act.wait_ge(self.sv, 2 * i + 1)
        act.activation(self.rows[0:1, self.C:2 * self.C],
                       self.tmpr[0:1, 0:self.C], AF.Sqrt).then_inc(self.sq, 1)

    def pe_emit(self, te, i, ones_col_bf, ones_row_f32):
        C = self.C
        te.wait_ge(self.sa, i + 1)
        if i > 0:
            te.wait_ge(self.sv, 2 * i - 1)
        for k in range(KD):
            te.matmul(self.ps_stat[0:1, 0:C], ones_col_bf, self.xb[:, k, :],
                      start=(k == 0), stop=(k == KD - 1))
        last = None
        for k in range(KD):
            last = te.matmul(self.ps_stat[0:1, C:2 * C], ones_col_bf,
                             self.x2[:, k, :], start=(k == 0), stop=(k == KD - 1))
        last.then_inc(self.sp, 1)
        te.wait_ge(self.sq, i + 1)
        if i > 0:
            te.wait_ge(self.sv, 2 * i)
        te.matmul(self.ps_bc[:, 0:C], ones_row_f32, self.rows[0:1, 0:C],
                  start=True, stop=True)
        te.matmul(self.ps_bc[:, C:2 * C], ones_row_f32,
                  self.rows[0:1, C:2 * C], start=True, stop=True).then_inc(self.sp, 1)

    def v_emit(self, v, i, xc_sb, out_ap, out2_ap=None, eps=1e-5):
        C = self.C
        v.wait_ge(self.sp, 2 * i + 1)
        v.tensor_scalar(self.rows[0:1, 0:C], self.ps_stat[0:1, 0:C],
                        1.0 / D, None, ALU.mult)
        v.tensor_scalar(self.tmpr[0:1, C:2 * C], self.ps_stat[0:1, C:2 * C],
                        1.0 / D, None, ALU.mult)
        v.tensor_tensor(self.tmpr[0:1, 0:C], self.rows[0:1, 0:C],
                        self.rows[0:1, 0:C], ALU.mult)
        v.drain()
        v.scalar_tensor_tensor(self.tmpr[0:1, C:2 * C], self.tmpr[0:1, C:2 * C],
                               eps, self.tmpr[0:1, 0:C], ALU.add, ALU.subtract)
        v.drain()
        v.reciprocal(self.tmpr[0:1, 0:C],
                     self.tmpr[0:1, C:2 * C]).then_inc(self.sv, 1)
        v.wait_ge(self.sp, 2 * i + 2)
        for k in range(KD):
            v.tensor_tensor(self.xc[:, k, :], xc_sb[:, k, :],
                            self.ps_bc[:, 0:C], ALU.subtract)
        last = None
        for k in range(KD):
            last = v.scalar_tensor_tensor(out_ap[:, k, :], self.xc[:, k, :], 1.0,
                                          self.ps_bc[:, C:2 * C], ALU.mult, ALU.mult)
        if out2_ap is not None:
            for k in range(KD):
                last = v.scalar_tensor_tensor(out2_ap[:, k, :], self.xc[:, k, :],
                                              1.0, self.ps_bc[:, C:2 * C],
                                              ALU.mult, ALU.mult)
        last.then_inc(self.sv, 1)



def emit_weight_prep(p: Prog, st: ExitStack, tag, jobs, block):
    nc = p.nc
    E = st.enter_context
    maxcols = max(j[2] for j in jobs)
    any_int = any(j[1] for j in jobs)
    wrow_b = E(nc.sbuf_tensor(f"{tag}_wrow_b", [P, 2, maxcols], BF16))
    wrow_i = E(nc.sbuf_tensor(f"{tag}_wrow_i", [P, 2, maxcols], I32)) if any_int else None
    tp_ps = BankedPsum(st, nc, f"{tag}_tp_ps", 2, P, P, BF16)
    wl = [nc.alloc_semaphore(f"{tag}_wl0"), nc.alloc_semaphore(f"{tag}_wl1")]
    wc = nc.alloc_semaphore(f"{tag}_wc")
    wt = nc.alloc_semaphore(f"{tag}_wt")
    we = nc.alloc_semaphore(f"{tag}_we")
    ident_bf = p.sb["ident_bf"]

    chunk_of_job = [0]
    for (_, _, cols, _) in jobs:
        chunk_of_job.append(chunk_of_job[-1] + cols // P)

    @block.gpsimd
    def _(g):
        for j, (src, is_int, cols, dst_fn) in enumerate(jobs):
            if j >= 2:
                if is_int:
                    g.wait_ge(wc, j - 1)
                else:
                    g.wait_ge(wt, chunk_of_job[j - 1])
            buf = (wrow_i if is_int else wrow_b)
            g.dma_start(buf[:, j % 2, :cols], src).then_inc(wl[j % 2], 16)

    @block.vector
    def _(v):
        v = EA(v)
        for j, (src, is_int, cols, dst_fn) in enumerate(jobs):
            v.wait_ge(wl[j % 2], 16 * (j // 2 + 1))
            if is_int:
                if j >= 2:
                    v.wait_ge(wt, chunk_of_job[j - 1])
                v.tensor_copy(wrow_b[:, j % 2, :cols],
                              wrow_i[:, j % 2, :cols]).then_inc(wc, 1)
            else:
                v.engine_nop().then_inc(wc, 1)

    @block.tensor
    def _(te):
        tcount = 0
        for j, (src, is_int, cols, dst_fn) in enumerate(jobs):
            te.wait_ge(wc, j + 1)
            for c in range(cols // P):
                if tcount >= 2:
                    te.wait_ge(we, tcount - 1)
                te.transpose(tp_ps[:, (tcount % 2) * P:(tcount % 2 + 1) * P],
                             wrow_b[:, j % 2, c * P:(c + 1) * P],
                             ident_bf[:, :]).then_inc(wt, 1)
                tcount += 1

    @block.scalar
    def _(act):
        act = EA(act)
        tcount = 0
        for j, (src, is_int, cols, dst_fn) in enumerate(jobs):
            for c in range(cols // P):
                act.wait_ge(wt, tcount + 1)
                act.activation(dst_fn(j, c), tp_ps[:, (tcount % 2) * P:(tcount % 2 + 1) * P],
                               AF.Copy).then_inc(we, 1)
                tcount += 1

    return we, chunk_of_job[-1]



def emit_l1_ffn(p: Prog, x_in, x_out, w_gate, w_up, w_down):
    nc, T = p.nc, p.T
    HID = 3072
    NH = HID // P
    C = 256
    NCHK = T // C
    s = 1.0 / 127.0

    with ExitStack() as st:
        E = st.enter_context
        wgT = E(nc.sbuf_tensor("l1f_wgT", [P, KD, HID], BF16))
        wuT = E(nc.sbuf_tensor("l1f_wuT", [P, KD, HID], BF16))
        wdT = E(nc.sbuf_tensor("l1f_wdT", [P, NH, D], BF16))

        jobs = []
        for gi in range(NH):
            jobs.append((w_gate[gi * P:(gi + 1) * P, :], True, D,
                         ("g", gi)))
        for gi in range(NH):
            jobs.append((w_up[gi * P:(gi + 1) * P, :], True, D, ("u", gi)))
        for gi in range(KD):
            jobs.append((w_down[gi * P:(gi + 1) * P, :], True, HID, ("d", gi)))

        def dst_fn(j, c):
            kind, gi = jobs[j][3]
            if kind == "g":
                return wgT[:, c, gi * P:(gi + 1) * P]
            if kind == "u":
                return wuT[:, c, gi * P:(gi + 1) * P]
            return wdT[:, c, gi * P:(gi + 1) * P]

        jobs2 = [(a, b, c, dst_fn) for (a, b, c, _) in jobs]
        with ExitStack() as st_prep:
            with nc.Block("l1f_prep") as blk:
                emit_weight_prep(p, st_prep, "l1f", jobs2, blk)

        xc = E(nc.sbuf_tensor("l1f_xc", [P, KD, C], F32))
        lnb = E(nc.sbuf_tensor("l1f_lnb", [P, KD, C], BF16))
        sg = E(nc.sbuf_tensor("l1f_sg", [P, C], BF16))
        h_fm = E(nc.sbuf_tensor("l1f_h", [P, NH, C], BF16))
        obuf = E(nc.sbuf_tensor("l1f_o", [P, KD, C], F32))
        ln = LnFm(p, st, "l1fln", C)
        ps_g = E(nc.psum_tensor("l1f_ps_g", [P, C], F32))
        ps_u = E(nc.psum_tensor("l1f_ps_u", [P, C], F32))
        ps_d = [E(nc.psum_tensor("l1f_ps_d0", [P, C], F32)),
                E(nc.psum_tensor("l1f_ps_d1", [P, C], F32))]

        s_x = nc.alloc_semaphore("l1f_sx")
        s_mm = nc.alloc_semaphore("l1f_smm")
        s_sil = nc.alloc_semaphore("l1f_ssil")
        s_h = nc.alloc_semaphore("l1f_sh")
        s_dn = nc.alloc_semaphore("l1f_sdn")
        s_o = nc.alloc_semaphore("l1f_so")
        s_st = nc.alloc_semaphore("l1f_sst")

        x_in_v = fmv(x_in)
        x_out_v = fmv(x_out)
        ND = KD

        with nc.Block("l1f_main") as blk:
            @blk.sync
            def _(sy):
                sy.dma_start(xc[:, :, :], x_in_v[:, :, 0:C]).then_inc(s_x, 16)
                for i in range(NCHK):
                    if i + 1 < NCHK:
                        sy.wait_ge(s_o, (i + 1) * ND)
                        sy.dma_start(xc[:, :, :],
                                     x_in_v[:, :, (i + 1) * C:(i + 2) * C]
                                     ).then_inc(s_x, 16)
                    for d in range(ND):
                        sy.wait_ge(s_o, i * ND + d + 1)
                        sy.dma_start(x_out_v[:, d, i * C:(i + 1) * C],
                                     obuf[:, d, :]).then_inc(s_st, 16)
                sy.wait_ge(s_st, 16 * NCHK * ND)

            @blk.scalar
            def _(act):
                act = EA(act)
                for i in range(NCHK):
                    act.wait_ge(s_x, 16 * (i + 1))
                    ln.act_emit(act, i, xc[:, :, :])
                    for m in range(NH):
                        act.wait_ge(s_mm, (i * NH + m) * 2 + 1)
                        if i * NH + m > 0:
                            act.wait_ge(s_h, i * NH + m)
                        act.activation(sg[:, :], ps_g[:, :], AF.Silu,
                                       scale=s).then_inc(s_sil, 1)

            @blk.tensor
            def _(te):
                for i in range(NCHK):
                    ln.pe_emit(te, i, p.sb["ones_col_bf"][:, :],
                               p.sb["ones_row_f32"][:, :])
                    te.wait_ge(ln.sv, 2 * i + 2)
                    for m in range(NH):
                        if i * NH + m > 0:
                            te.wait_ge(s_sil, i * NH + m)
                        last = None
                        for k in range(KD):
                            last = te.matmul(ps_g[:, :], wgT[:, k, m * P:(m + 1) * P],
                                             lnb[:, k, :], start=(k == 0),
                                             stop=(k == KD - 1))
                        last.then_inc(s_mm, 1)
                        if i * NH + m > 0:
                            te.wait_ge(s_h, i * NH + m)
                        last = None
                        for k in range(KD):
                            last = te.matmul(ps_u[:, :], wuT[:, k, m * P:(m + 1) * P],
                                             lnb[:, k, :], start=(k == 0),
                                             stop=(k == KD - 1))
                        last.then_inc(s_mm, 1)
                    te.wait_ge(s_h, (i + 1) * NH)
                    for d in range(ND):
                        if i * ND + d >= 2:
                            te.wait_ge(s_o, i * ND + d - 1)
                        last = None
                        for k in range(NH):
                            last = te.matmul(ps_d[d % 2][:, :],
                                             wdT[:, k, d * P:(d + 1) * P],
                                             h_fm[:, k, :], start=(k == 0),
                                             stop=(k == NH - 1))
                        last.then_inc(s_dn, 1)

            @blk.vector
            def _(v):
                v = EA(v)
                for i in range(NCHK):
                    if i > 0:
                        v.wait_ge(s_mm, 2 * NH * i)
                        v.wait_ge(s_dn, i * ND)
                    ln.v_emit(v, i, xc[:, :, :], lnb)
                    for m in range(NH):
                        v.wait_ge(s_sil, i * NH + m + 1)
                        v.wait_ge(s_mm, (i * NH + m) * 2 + 2)
                        v.scalar_tensor_tensor(h_fm[:, m, :], ps_u[:, :], s,
                                               sg[:, :], ALU.mult,
                                               ALU.mult).then_inc(s_h, 1)
                    for d in range(ND):
                        v.wait_ge(s_dn, i * ND + d + 1)
                        if i > 0:
                            v.wait_ge(s_st, 16 * i * ND)
                        v.scalar_tensor_tensor(obuf[:, d, :], ps_d[d % 2][:, :],
                                               s, xc[:, d, :], ALU.mult,
                                               ALU.add).then_inc(s_o, 1)



_L0A_SKIP = set()


def emit_l0_mamba(p: Prog, x_in, x_out, w_x, w_dt, b_dt, log_a, d_param):
    nc, T = p.nc, p.T
    C = 256
    NCHK = T // C
    NST = 16

    with ExitStack() as st:
        E = st.enter_context
        wxT = E(nc.sbuf_tensor("l0m_wxT", [P, KD, 96], BF16))
        wdtT = E(nc.sbuf_tensor("l0m_wdtT", [64, D], BF16))
        bdt_c = E(nc.sbuf_tensor("l0m_bdt", [P, KD], F32))
        dpar_c = E(nc.sbuf_tensor("l0m_dpar", [P, KD], F32))
        a_neg = E(nc.sbuf_tensor("l0m_aneg", [P, KD, NST], F32))
        state = E(nc.sbuf_tensor("l0m_state", [P, KD, NST], F32))

        with nc.sbuf_tensor("l0m_wxn", [96, D], BF16) as wxn, \
             nc.sbuf_tensor("l0m_wdtn", [P, KD, 64], BF16) as wdtn, \
             ExitStack() as _tpst:
            tp_ps = BankedPsum(_tpst, nc, "l0m_tp", 2, P, P, BF16)
            cs = nc.alloc_semaphore("l0m_cs")
            ce = nc.alloc_semaphore("l0m_ce")
            cd = nc.alloc_semaphore("l0m_cd")
            ch = nc.alloc_semaphore("l0m_ch")
            with nc.Block("l0m_prep1") as blk:
                @blk.gpsimd
                def _(g):
                    g.dma_start(wxn[:, :], w_x[:, :]).then_inc(cd, 16)
                    g.dma_start(wdtn[:, :, :],
                                w_dt[:, :].rearrange("(k p) r -> p k r", p=P)
                                ).then_inc(cd, 16)
                    g.wait_ge(cd, 32)
                @blk.sync
                def _(sy):
                    for k in range(KD):
                        sy.dma_start(bdt_c[:, k:k + 1],
                                     b_dt[k * P:(k + 1) * P]).then_inc(ch, 16)
                        sy.dma_start(dpar_c[:, k:k + 1],
                                     d_param[k * P:(k + 1) * P]).then_inc(ch, 16)
                    sy.dma_start(a_neg[:, :, :],
                                 log_a[:, :].rearrange("(k p) n -> p k n", p=P)
                                 ).then_inc(ch, 16)
                    sy.wait_ge(ch, 16 * (2 * KD + 1))
            with nc.Block("l0m_prep2") as blk:
                @blk.tensor
                def _(te):
                    cnt = 0
                    for k in range(KD):
                        if cnt >= 2:
                            te.wait_ge(ce, cnt - 1)
                        te.transpose(tp_ps[:, (cnt % 2) * P:(cnt % 2) * P + 96],
                                     wxn[:, k * P:(k + 1) * P],
                                     p.sb["ident_bf"][0:96, 0:96]).then_inc(cs, 1)
                        cnt += 1
                    for k in range(KD):
                        te.wait_ge(ce, cnt - 1)
                        te.transpose(tp_ps[0:64, (cnt % 2) * P:(cnt % 2) * P + P],
                                     wdtn[:, k, :], p.sb["ident_bf"][:, :]
                                     ).then_inc(cs, 1)
                        cnt += 1
                @blk.scalar
                def _(act):
                    act = EA(act)
                    cnt = 0
                    for k in range(KD):
                        act.wait_ge(cs, cnt + 1)
                        act.activation(wxT[:, k, :],
                                       tp_ps[:, (cnt % 2) * P:(cnt % 2) * P + 96],
                                       AF.Copy).then_inc(ce, 1)
                        cnt += 1
                    for k in range(KD):
                        act.wait_ge(cs, cnt + 1)
                        act.activation(wdtT[:, k * P:(k + 1) * P],
                                       tp_ps[0:64, (cnt % 2) * P:(cnt % 2) * P + P],
                                       AF.Copy).then_inc(ce, 1)
                        cnt += 1
                    act.activation(a_neg[:, :, :], a_neg[:, :, :], AF.Exp)
                @blk.vector
                def _(v):
                    v = EA(v)
                    v.memset(state[:, :, :], 0.0)

            with nc.Block("l0m_prep3") as blk:
                @blk.vector
                def _(v):
                    v.tensor_scalar(a_neg[:, :, :], a_neg[:, :, :], -1.0, None,
                                    ALU.mult)

        xc = E(nc.sbuf_tensor("l0m_xc", [P, KD, C], F32))
        lnb = E(nc.sbuf_tensor("l0m_lnb", [P, KD, C], BF16))
        u32 = E(nc.sbuf_tensor("l0m_u32", [P, KD, C], F32))
        ddr = E(nc.sbuf_tensor("l0m_ddr", [64, C], BF16))
        b_sb = E(nc.sbuf_tensor("l0m_bsb", [16, C], F32))
        c_sb = E(nc.sbuf_tensor("l0m_csb", [16, C], F32))
        b_bf = E(nc.sbuf_tensor("l0m_bbf", [16, C], BF16))
        c_bf = E(nc.sbuf_tensor("l0m_cbf", [16, C], BF16))
        drw = E(nc.sbuf_tensor("l0m_drw", [P, KD, C], F32))
        dsb = E(nc.sbuf_tensor("l0m_dsb", [P, KD, C], F32))
        l1b = E(nc.sbuf_tensor("l0m_l1b", [P, KD, C], F32))
        wfm = E(nc.sbuf_tensor("l0m_wfm", [P, KD, C], F32))
        yac = E(nc.sbuf_tensor("l0m_yac", [P, KD, C], F32))
        abar = E(nc.sbuf_tensor("l0m_abar", [P, 2, C], F32))
        belb = E(nc.sbuf_tensor("l0m_bel", [P, C], F32))
        hsb = E(nc.sbuf_tensor("l0m_h", [P, C], F32))
        t2b = E(nc.sbuf_tensor("l0m_t2", [P, C], F32))
        obuf = E(nc.sbuf_tensor("l0m_o", [P, KD, C], F32))
        ln = LnFm(p, st, "l0mln", C)
        psA = E(nc.psum_tensor("l0m_psA", [64, C], F32))
        psBC = E(nc.psum_tensor("l0m_psBC", [16, 1024], F32))
        psD = E(nc.psum_tensor("l0m_psD", [P, C], F32))
        psBb = E(nc.psum_tensor("l0m_psBb", [P, C], F32))
        psCb = E(nc.psum_tensor("l0m_psCb", [P, C], F32))

        s_x = nc.alloc_semaphore("l0m_sx")
        s_pe1 = nc.alloc_semaphore("l0m_spe1")
        s_v1 = nc.alloc_semaphore("l0m_sv1")
        s_pd = nc.alloc_semaphore("l0m_spd")
        s_vd = nc.alloc_semaphore("l0m_svd")
        s_sp = nc.alloc_semaphore("l0m_ssp")
        s_bc = nc.alloc_semaphore("l0m_sbc")
        s_ab = nc.alloc_semaphore("l0m_sab")
        s_sc = nc.alloc_semaphore("l0m_ssc")
        s_st = nc.alloc_semaphore("l0m_sst")
        s_ep = nc.alloc_semaphore("l0m_sep")
        x_in_v = fmv(x_in)
        x_out_v = fmv(x_out)

        for i in range(NCHK):
            with nc.Block(f"l0m_ld{i}") as blk:
                @blk.sync
                def _(sy, i=i):
                    sy.dma_start(xc[:, :, :],
                                 x_in_v[:, :, i * C:(i + 1) * C]).then_inc(s_x, 16)
                    sy.wait_ge(s_x, 16 * (i + 1))

            with nc.Block(f"l0m_m2_{i}") as blk:
                @blk.scalar
                def _(act, i=i):
                    act = EA(act)
                    ln.act_emit(act, i, xc[:, :, :])
                    for k in range(KD):
                        act.wait_ge(s_vd, i * KD + k + 1)
                        act.activation(abar[:, 0, :], drw[:, k, :], AF.Abs,
                                       bias=bdt_c[:, k:k + 1])
                        act.activation(abar[:, 0, :], abar[:, 0, :], AF.Exp,
                                       scale=-1.0)
                        act.activation(l1b[:, k, :], abar[:, 0, :], AF.Ln,
                                       bias=1.0).then_inc(s_sp, 1)

                @blk.tensor
                def _(te, i=i):
                    te2 = te
                    ln.pe_emit(te2, i, p.sb["ones_col_bf"][:, :],
                               p.sb["ones_row_f32"][:, :])
                    te.wait_ge(ln.sv, 2 * i + 2)
                    last = None
                    for k in range(KD):
                        last = te.matmul(psA[:, :], wxT[:, k, 0:64], lnb[:, k, :],
                                         start=(k == 0), stop=(k == KD - 1))
                    for k in range(KD):
                        last = te.matmul(psBC[:, 0:C], wxT[:, k, 64:80],
                                         lnb[:, k, :], start=(k == 0),
                                         stop=(k == KD - 1))
                    for k in range(KD):
                        last = te.matmul(psBC[:, 512:512 + C], wxT[:, k, 80:96],
                                         lnb[:, k, :], start=(k == 0),
                                         stop=(k == KD - 1))
                    last.then_inc(s_pe1, 1)
                    te.wait_ge(s_v1, i + 1)
                    for k in range(KD):
                        if i * KD + k > 0:
                            te.wait_ge(s_vd, i * KD + k)
                        te.matmul(psD[:, :], wdtT[:, k * P:(k + 1) * P],
                                  ddr[:, :], start=True,
                                  stop=True).then_inc(s_pd, 1)

                @blk.vector
                def _(v, i=i):
                    v = EA(v)
                    ln.v_emit(v, i, xc[:, :, :], lnb, out2_ap=u32)
                    v.wait_ge(s_pe1, i + 1)
                    v.tensor_copy(ddr[:, :], psA[:, :])
                    v.tensor_copy(b_sb[:, :], psBC[:, 0:C])
                    v.tensor_copy(c_sb[:, :], psBC[:, 512:512 + C])
                    v.drain()
                    v.tensor_copy(b_bf[:, :], b_sb[:, :])
                    v.tensor_copy(c_bf[:, :], c_sb[:, :]).then_inc(s_v1, 1)
                    for k in range(KD):
                        v.wait_ge(s_pd, i * KD + k + 1)
                        v.tensor_copy(drw[:, k, :], psD[:, :])
                        v.tensor_scalar(dsb[:, k, :], drw[:, k, :],
                                        bdt_c[:, k:k + 1], 0.0, ALU.add,
                                        ALU.max).then_inc(s_vd, 1)
                    for k in range(KD):
                        v.wait_ge(s_sp, i * KD + k + 1)
                        v.tensor_tensor(dsb[:, k, :], dsb[:, k, :], l1b[:, k, :],
                                        ALU.add)
                        v.tensor_tensor(wfm[:, k, :], dsb[:, k, :], u32[:, k, :],
                                        ALU.mult)

            with nc.Block(f"l0m_sc{i}") as blk:
                @blk.tensor
                def _(te, i=i):
                    for n in range(NST):
                        jn = i * NST + n
                        if n > 0:
                            te.wait_ge(s_sc, (jn) * KD)
                        te.matmul(psBb[:, :], p.sb["sel16"][:, n, :],
                                  b_bf[:, :], start=True, stop=True)
                        te.matmul(psCb[:, :], p.sb["sel16"][:, n, :],
                                  c_bf[:, :], start=True,
                                  stop=True).then_inc(s_bc, 1)

                @blk.scalar
                def _(act, i=i):
                    act = EA(act)
                    for n in range(NST):
                        for k in range(KD):
                            j = (i * NST + n) * KD + k
                            if j >= 2 + i * NST * KD:
                                act.wait_ge(s_sc, j - 1 - i * NST * KD
                                            + i * NST * KD)
                            act.activation(abar[:, j % 2, :], dsb[:, k, :],
                                           AF.Exp,
                                           scale=a_neg[:, k, n:n + 1]
                                           ).then_inc(s_ab, 1)

                @blk.vector
                def _(v, i=i):
                    v = EA(v)
                    for n in range(NST):
                        jn = i * NST + n
                        v.wait_ge(s_bc, jn + 1)
                        for k in range(KD):
                            j = jn * KD + k
                            v.wait_ge(s_ab, j + 1)
                            v.tensor_tensor(belb[:, :], wfm[:, k, :], psBb[:, :],
                                            ALU.mult)
                            v.tensor_tensor_scan(hsb[:, :], abar[:, j % 2, :],
                                                 belb[:, :],
                                                 state[:, k, n:n + 1],
                                                 ALU.mult, ALU.add)
                            v.tensor_copy(state[:, k, n:n + 1], hsb[:, C - 1:C])
                            if n == 0:
                                v.tensor_tensor(yac[:, k, :], hsb[:, :],
                                                psCb[:, :],
                                                ALU.mult).then_inc(s_sc, 1)
                            else:
                                v.tensor_tensor(t2b[:, :], hsb[:, :], psCb[:, :],
                                                ALU.mult)
                                v.tensor_tensor(yac[:, k, :], yac[:, k, :],
                                                t2b[:, :],
                                                ALU.add).then_inc(s_sc, 1)

            with nc.Block(f"l0m_ep{i}") as blk:
                @blk.vector
                def _(v, i=i):
                    v = EA(v)
                    last = None
                    for k in range(KD):
                        v.scalar_tensor_tensor(obuf[:, k, :], u32[:, k, :],
                                               dpar_c[:, k:k + 1], yac[:, k, :],
                                               ALU.mult, ALU.add)
                        last = v.tensor_tensor(obuf[:, k, :], obuf[:, k, :],
                                               xc[:, k, :], ALU.add)
                    last.then_inc(s_ep, 1)
                @blk.sync
                def _(sy, i=i):
                    sy.wait_ge(s_ep, i + 1)
                    sy.dma_start(x_out_v[:, :, i * C:(i + 1) * C],
                                 obuf[:, :, :]).then_inc(s_st, 16)
                    sy.wait_ge(s_st, 16 * (i + 1))



def emit_l0_moe(p: Prog, x_in, x_out, router_w, fc_w, proj_w, xe_hbm, ye_hbm):
    nc, T = p.nc, p.T
    C = 512
    NCHK = T // C
    NTT = C // P
    CAPE = 768
    NSLOT = 8 * CAPE
    HD = 4096
    NHT = HD // P
    NES = CAPE // P
    NMB = HD // 512

    with ExitStack() as st:
        E = st.enter_context
        rwT = E(nc.sbuf_tensor("l0f_rwT", [P, KD, 8], F32))
        dest_all = E(nc.sbuf_tensor("l0f_dest", [P, T // P], I32))
        iota_e8 = E(nc.sbuf_tensor("l0f_iotae", [P, 8], F32))
        ps_tb = BankedPsum(st, nc, "l0f_pstb", 2, P, P, BF16)

        with ExitStack() as stp, nc.sbuf_tensor("l0f_rwn", [8, 1024], F32) as rwn, \
             nc.sbuf_tensor("l0f_ioti", [P, 8], I32) as ioti, \
             nc.sbuf_tensor("l0f_zrow", [P, 1024], BF16) as zrow, \
             nc.sbuf_tensor("l0f_zrowf", [1, 1024], F32) as zrowf, \
             ExitStack() as _tpst2:
            tpf = BankedPsum(_tpst2, nc, "l0f_tpf", 2, P, 8, F32)
            cs = nc.alloc_semaphore("l0f_cs")
            ce = nc.alloc_semaphore("l0f_ce")
            ch = nc.alloc_semaphore("l0f_ch")
            with nc.Block("l0f_prep1") as blk:
                @blk.sync
                def _(sy):
                    sy.dma_start(rwn[:, :], router_w[:, :]).then_inc(ch, 16)
                    sy.wait_ge(ch, 16)
                @blk.gpsimd
                def _(g):
                    g.iota(ioti[:, :], [[1, 8]], channel_multiplier=0)
                @blk.vector
                def _(v):
                    v = EA(v)
                    v.memset(zrow[:, :], 0.0)
                    v.memset(zrowf[:, :], 0.0)
            with nc.Block("l0f_prep2") as blk:
                @blk.tensor
                def _(te):
                    for k in range(KD):
                        if k >= 2:
                            te.wait_ge(ce, k - 1)
                        te.transpose(tpf[:, (k % 2) * 8:(k % 2) * 8 + 8],
                                     rwn[:, k * P:(k + 1) * P],
                                     p.sb["ident_f32"][0:8, 0:8]).then_inc(cs, 1)
                @blk.scalar
                def _(act):
                    act = EA(act)
                    for k in range(KD):
                        act.wait_ge(cs, k + 1)
                        act.activation(rwT[:, k, :],
                                       tpf[:, (k % 2) * 8:(k % 2) * 8 + 8],
                                       AF.Copy).then_inc(ce, 1)
                @blk.vector
                def _(v):
                    v.tensor_copy(iota_e8[:, :], ioti[:, :])
                @blk.sync
                def _(sy):
                    nz = (NSLOT + 1 + P - 1) // P
                    for z in range(nz):
                        r0 = z * P
                        r1 = min(r0 + P, NSLOT + 1)
                        sy.dma_start(xe_hbm[r0:r1, :],
                                     zrow[0:r1 - r0, :]).then_inc(ch, 16)
                    sy.dma_start(ye_hbm[NSLOT:NSLOT + 1, :],
                                 zrowf[0:1, :]).then_inc(ch, 16)
                    sy.wait_ge(ch, 32 + 16 * nz)

        s_x = nc.alloc_semaphore("l0f_sx")
        s_lg = nc.alloc_semaphore("l0f_slg")
        s_ev = nc.alloc_semaphore("l0f_sev")
        s_tp = nc.alloc_semaphore("l0f_stp")
        s_ab = nc.alloc_semaphore("l0f_sab")
        s_sc = nc.alloc_semaphore("l0f_ssc")
        x_in_v = fmv(x_in)
        x_out_v = fmv(x_out)
        with ExitStack() as stR:
            ER = stR.enter_context
            xc = ER(nc.sbuf_tensor("l0f_xc", [P, KD, C], F32))
            lnb = ER(nc.sbuf_tensor("l0f_lnb", [P, KD, C], BF16))
            ln32 = ER(nc.sbuf_tensor("l0f_ln32", [P, KD, C], F32))
            lntm = ER(nc.sbuf_tensor("l0f_lntm", [P, NTT, 1024], BF16))
            lg_sb = ER(nc.sbuf_tensor("l0f_lg", [8, C], F32))
            lgtm = ER(nc.sbuf_tensor("l0f_lgtm", [P, NTT, 8], F32))
            m8 = ER(nc.sbuf_tensor("l0f_m8", [P, 1], F32))
            masktm = ER(nc.sbuf_tensor("l0f_masktm", [P, NTT, 8], F32))
            maskfm = ER(nc.sbuf_tensor("l0f_maskfm", [8, C], F32))
            rankfm = ER(nc.sbuf_tensor("l0f_rankfm", [8, C], F32))
            ranktm = ER(nc.sbuf_tensor("l0f_ranktm", [P, NTT, 8], F32))
            tcol = ER(nc.sbuf_tensor("l0f_tcol", [P, 4], F32))
            ones8 = ER(nc.sbuf_tensor("l0f_ones8", [8, C], F32))
            carry = ER(nc.sbuf_tensor("l0f_carry", [8, 1], F32))
            ln = LnFm(p, stR, "l0fln", C)
            pbR = BankedPsum(stR, nc, "l0f_pbr", 2, P, 512, F32)
            ps_lg = pbR.t[0][0:8, 0:C]
            with nc.Block("l0f_rinit") as blk:
                @blk.vector
                def _(v):
                    v = EA(v)
                    v.memset(ones8[:, :], 1.0)
                    v.memset(carry[:, :], 0.0)

            for i in range(NCHK):
                tp0, ev0, ab0 = i * (NTT * KD + 3 * NTT), i * (3 * NTT + 2), \
                    i * (NTT * KD)
                with nc.Block(f"l0f_r{i}") as blk:
                    @blk.sync
                    def _(sy, i=i):
                        sy.dma_start(xc[:, :, :],
                                     x_in_v[:, :, i * C:(i + 1) * C]
                                     ).then_inc(s_x, 16)
                        sy.wait_ge(s_x, 16 * (i + 1))
                with nc.Block(f"l0f_r2_{i}") as blk:
                    @blk.scalar
                    def _(act, i=i, tp0=tp0, ab0=ab0):
                        act = EA(act)
                        ln.act_emit(act, i, xc[:, :, :])
                        for j in range(NTT * KD):
                            tt, k = divmod(j, KD)
                            act.wait_ge(s_tp, tp0 + j + 1)
                            act.activation(lntm[:, tt, k * P:(k + 1) * P],
                                           ps_tb[:, (j % 2) * P:(j % 2 + 1) * P],
                                           AF.Copy).then_inc(s_ab, 1)

                    @blk.tensor
                    def _(te, i=i, tp0=tp0, ev0=ev0, ab0=ab0):
                        ln.pe_emit(te, i, p.sb["ones_col_bf"][:, :],
                                   p.sb["ones_row_f32"][:, :])
                        te.wait_ge(ln.sv, 2 * i + 2)
                        last = None
                        for k in range(KD):
                            last = te.matmul(ps_lg[:, :], rwT[:, k, :],
                                             ln32[:, k, :], start=(k == 0),
                                             stop=(k == KD - 1))
                        last.then_inc(s_lg, 1)
                        for j in range(NTT * KD):
                            tt, k = divmod(j, KD)
                            if j >= 2:
                                te.wait_ge(s_ab, ab0 + j - 1)
                            te.transpose(ps_tb[:, (j % 2) * P:(j % 2 + 1) * P],
                                         lnb[:, k, tt * P:(tt + 1) * P],
                                         p.sb["ident_bf"][:, :]).then_inc(s_tp, 1)
                        te.wait_ge(s_ev, ev0 + 1)
                        for tt in range(NTT):
                            if tt >= 2:
                                te.wait_ge(s_ev, ev0 + tt)
                            te.transpose(pbR.t[tt % 2][0:P, 0:8],
                                         lg_sb[:, tt * P:(tt + 1) * P],
                                         p.sb["ident_f32"][0:8, 0:8]
                                         ).then_inc(s_tp, 1)
                        te.wait_ge(s_ev, ev0 + 1 + NTT)
                        for tt in range(NTT):
                            if tt >= 2:
                                te.wait_ge(s_ev, ev0 + NTT + tt)
                            te.transpose(pbR.t[tt % 2][0:8, 0:P],
                                         masktm[:, tt, :],
                                         p.sb["ident_f32"][:, :]).then_inc(s_tp, 1)
                        te.wait_ge(s_ev, ev0 + 2 * NTT + 2)
                        for tt in range(NTT):
                            if tt >= 2:
                                te.wait_ge(s_ev, ev0 + 2 * NTT + tt + 1)
                            te.transpose(pbR.t[tt % 2][0:P, 0:8],
                                         rankfm[:, tt * P:(tt + 1) * P],
                                         p.sb["ident_f32"][0:8, 0:8]
                                         ).then_inc(s_tp, 1)

                    @blk.vector
                    def _(v, i=i, tp0=tp0, ev0=ev0):
                        v = EA(v)
                        ln.v_emit(v, i, xc[:, :, :], lnb, out2_ap=ln32)
                        v.wait_ge(s_lg, i + 1)
                        v.tensor_copy(lg_sb[:, :], ps_lg[:, :]).then_inc(s_ev, 1)
                        for tt in range(NTT):
                            v.wait_ge(s_tp, tp0 + NTT * KD + tt + 1)
                            v.tensor_copy(lgtm[:, tt, :],
                                          pbR.t[tt % 2][0:P, 0:8])
                            v.reduce_max(m8[:, :], lgtm[:, tt, :], axis=AX.X)
                            v.tensor_scalar(masktm[:, tt, :], lgtm[:, tt, :],
                                            m8[:, :], None,
                                            ALU.is_ge).then_inc(s_ev, 1)
                        for tt in range(NTT):
                            v.wait_ge(s_tp, tp0 + NTT * KD + NTT + tt + 1)
                            v.tensor_copy(maskfm[:, tt * P:(tt + 1) * P],
                                          pbR.t[tt % 2][0:8, 0:P]
                                          ).then_inc(s_ev, 1)
                        v.tensor_tensor_scan(rankfm[:, :], ones8[:, :],
                                             maskfm[:, :], carry[:, :],
                                             ALU.mult, ALU.add)
                        v.tensor_copy(carry[:, :], rankfm[:, C - 1:C])
                        v.tensor_tensor(rankfm[:, :], rankfm[:, :], maskfm[:, :],
                                        ALU.subtract).then_inc(s_ev, 1)
                        for tt in range(NTT):
                            v.wait_ge(s_tp, tp0 + NTT * KD + 2 * NTT + tt + 1)
                            v.tensor_copy(ranktm[:, tt, :],
                                          pbR.t[tt % 2][0:P, 0:8])
                            v.tensor_tensor(ranktm[:, tt, :], ranktm[:, tt, :],
                                            masktm[:, tt, :], ALU.mult)
                            v.reduce_sum(tcol[:, 0:1], ranktm[:, tt, :], axis=AX.X)
                            v.tensor_tensor(lgtm[:, tt, :], masktm[:, tt, :],
                                            iota_e8[:, :], ALU.mult)
                            v.reduce_sum(tcol[:, 1:2], lgtm[:, tt, :], axis=AX.X)
                            v.scalar_tensor_tensor(tcol[:, 2:3], tcol[:, 1:2],
                                                   float(CAPE), tcol[:, 0:1],
                                                   ALU.mult, ALU.add)
                            v.tensor_scalar(tcol[:, 3:4], tcol[:, 2:3],
                                            float(NSLOT), None, ALU.min)
                            v.tensor_copy(dest_all[:, i * NTT + tt:
                                                   i * NTT + tt + 1],
                                          tcol[:, 3:4]).then_inc(s_ev, 1)

                    @blk.gpsimd
                    def _(g, i=i, ev0=ev0, ab0=ab0):
                        for tt in range(NTT):
                            g.wait_ge(s_ev, ev0 + 2 * NTT + 2 + tt + 1)
                            g.wait_ge(s_ab, ab0 + (tt + 1) * KD)
                            g.indirect_dma_start(
                                xe_hbm[:, :],
                                bass.IndirectOffsetOnAxis(
                                    dest_all[:, i * NTT + tt:i * NTT + tt + 1],
                                    0),
                                lntm[:, tt, :], None).then_inc(s_sc, 16)
                        g.wait_ge(s_sc, 16 * NTT * (i + 1))

        s_wp = nc.alloc_semaphore("l0f_swp")
        s_wx = nc.alloc_semaphore("l0f_swx")
        s_wf = [nc.alloc_semaphore("l0f_swf0"), nc.alloc_semaphore("l0f_swf1")]
        s_t2e = nc.alloc_semaphore("l0f_st2")
        s_e2 = nc.alloc_semaphore("l0f_se2")
        s_hm = nc.alloc_semaphore("l0f_shm")
        s_hv = nc.alloc_semaphore("l0f_shv")
        s_ym = nc.alloc_semaphore("l0f_sym")
        s_ye = nc.alloc_semaphore("l0f_sye")
        s_ys = nc.alloc_semaphore("l0f_sys")
        N_TPE = KD * NHT + NES * KD
        with ExitStack() as stE:
            EE = stE.enter_context
            xet = EE(nc.sbuf_tensor("l0f_xet", [P, 1024], BF16))
            xefm = EE(nc.sbuf_tensor("l0f_xefm", [P, KD, CAPE], BF16))
            fcb = EE(nc.sbuf_tensor("l0f_fcb", [P, 2, KD, 512], BF16))
            prn = EE(nc.sbuf_tensor("l0f_prn", [P, HD], BF16))
            prT = EE(nc.sbuf_tensor("l0f_prT", [P, NHT, 1024], BF16))
            relb = EE(nc.sbuf_tensor("l0f_relb", [P, 512], BF16))
            hfm = EE(nc.sbuf_tensor("l0f_hfm", [P, NHT, CAPE], BF16))
            yet = EE(nc.sbuf_tensor("l0f_yet", [P, 1024], F32))
            ps_h = [EE(nc.psum_tensor("l0f_psh0", [P, 512], F32)),
                    EE(nc.psum_tensor("l0f_psh1", [P, 512], F32))]
            ps_y = [EE(nc.psum_tensor("l0f_psy0", [P, 512], F32)),
                    EE(nc.psum_tensor("l0f_psy1", [P, 512], F32))]
            N_E2 = N_TPE + 2 * NHT
            for e in range(8):
                t2_0 = e * N_TPE
                e2_0 = e * N_E2
                hm_0 = e * (NMB * 8)
                hv_0 = e * (NHT * 2)
                ym_0 = e * (NES * 2)
                ye_0 = e * (NES * 2)
                ys_0 = e * (NES * 16)
                wp_0, wx_0 = e * 16 * KD, e * 16 * NES
                wf_0 = [e * 16 * (NMB - NMB // 2), e * 16 * (NMB // 2)]
                with nc.Block(f"l0f_e{e}") as blk:
                    @blk.gpsimd
                    def _(g, e=e, t2_0=t2_0, hm_0=hm_0, wp_0=wp_0, wx_0=wx_0,
                          wf_0=wf_0):
                        for rg in range(KD):
                            g.dma_start(prn[:, :],
                                        proj_w[e, rg * P:(rg + 1) * P, :]
                                        ).then_inc(s_wp, 16)
                            g.wait_ge(s_t2e, t2_0 + (rg + 1) * NHT)
                        for stt in range(NES):
                            g.dma_start(xet[:, :],
                                        xe_hbm[e * CAPE + stt * P:
                                               e * CAPE + (stt + 1) * P, :]
                                        ).then_inc(s_wx, 16)
                            g.wait_ge(s_t2e, t2_0 + KD * NHT + (stt + 1) * KD)
                        for mb in range(NMB):
                            if mb >= 2:
                                g.wait_ge(s_hm, hm_0 + (mb - 1) * 8)
                            g.dma_start(fcb[:, mb % 2, :, :],
                                        fc_w[e, :, mb * 512:(mb + 1) * 512]
                                        .rearrange("(k p) h -> p k h", p=P)
                                        ).then_inc(s_wf[mb % 2], 16)
                        g.wait_ge(s_wf[0], wf_0[0] + 16 * (NMB - NMB // 2))
                        g.wait_ge(s_wf[1], wf_0[1] + 16 * (NMB // 2))

                    @blk.tensor
                    def _(te, e=e, t2_0=t2_0, e2_0=e2_0, hm_0=hm_0, hv_0=hv_0,
                          ym_0=ym_0, ye_0=ye_0, wp_0=wp_0, wx_0=wx_0,
                          wf_0=wf_0):
                        tpn = t2_0
                        for rg in range(KD):
                            te.wait_ge(s_wp, wp_0 + 16 * (rg + 1))
                            for c in range(NHT):
                                if tpn - t2_0 >= 2:
                                    te.wait_ge(s_e2, e2_0 + (tpn - t2_0) - 1)
                                te.transpose(ps_tb[:, (tpn % 2) * P:
                                                   (tpn % 2 + 1) * P],
                                             prn[:, c * P:(c + 1) * P],
                                             p.sb["ident_bf"][:, :]
                                             ).then_inc(s_t2e, 1)
                                tpn += 1
                        for stt in range(NES):
                            te.wait_ge(s_wx, wx_0 + 16 * (stt + 1))
                            for k in range(KD):
                                if tpn - t2_0 >= 2:
                                    te.wait_ge(s_e2, e2_0 + (tpn - t2_0) - 1)
                                te.transpose(ps_tb[:, (tpn % 2) * P:
                                                   (tpn % 2 + 1) * P],
                                             xet[:, k * P:(k + 1) * P],
                                             p.sb["ident_bf"][:, :]
                                             ).then_inc(s_t2e, 1)
                                tpn += 1
                        te.wait_ge(s_e2, e2_0 + N_TPE)
                        for mb in range(NMB):
                            te.wait_ge(s_wf[mb % 2], wf_0[mb % 2]
                                       + 16 * (mb // 2 + 1))
                            for mm in range(4):
                                m = mb * 4 + mm
                                for nch, (n0, n1) in enumerate(((0, 512),
                                                               (512, CAPE))):
                                    j = m * 2 + nch
                                    if j >= 2:
                                        te.wait_ge(s_hv, hv_0 + j - 1)
                                    last = None
                                    for k in range(KD):
                                        last = te.matmul(
                                            ps_h[j % 2][:, 0:n1 - n0],
                                            fcb[:, mb % 2, k,
                                                mm * P:(mm + 1) * P],
                                            xefm[:, k, n0:n1], start=(k == 0),
                                            stop=(k == KD - 1))
                                    last.then_inc(s_hm, 1)
                        te.wait_ge(s_hv, hv_0 + NHT * 2)
                        for stt in range(NES):
                            for nch in range(2):
                                j = stt * 2 + nch
                                if j >= 2:
                                    te.wait_ge(s_ye, ye_0 + j - 1)
                                last = None
                                for k in range(NHT):
                                    last = te.matmul(
                                        ps_y[j % 2][:, :],
                                        hfm[:, k, stt * P:(stt + 1) * P],
                                        prT[:, k, nch * 512:(nch + 1) * 512],
                                        start=(k == 0), stop=(k == NHT - 1))
                                last.then_inc(s_ym, 1)

                    @blk.scalar
                    def _(act, e=e, t2_0=t2_0, e2_0=e2_0, hm_0=hm_0, hv_0=hv_0):
                        act = EA(act)
                        for j in range(KD * NHT):
                            rg, c = divmod(j, NHT)
                            act.wait_ge(s_t2e, t2_0 + j + 1)
                            act.activation(prT[:, c, rg * P:(rg + 1) * P],
                                           ps_tb[:, (j % 2) * P:(j % 2 + 1) * P],
                                           AF.Copy).then_inc(s_e2, 1)
                        for j2 in range(NES * KD):
                            stt, k = divmod(j2, KD)
                            j = KD * NHT + j2
                            act.wait_ge(s_t2e, t2_0 + j + 1)
                            act.activation(xefm[:, k, stt * P:(stt + 1) * P],
                                           ps_tb[:, (j % 2) * P:(j % 2 + 1) * P],
                                           AF.Copy).then_inc(s_e2, 1)
                        for j in range(NHT * 2):
                            act.wait_ge(s_hm, hm_0 + j + 1)
                            if j >= 1:
                                act.wait_ge(s_hv, hv_0 + j)
                            n = 512 if j % 2 == 0 else CAPE - 512
                            act.activation(relb[:, 0:n], ps_h[j % 2][:, 0:n],
                                           AF.Relu).then_inc(s_e2, 1)

                    @blk.vector
                    def _(v, e=e, t2_0=t2_0, e2_0=e2_0, hv_0=hv_0, ym_0=ym_0,
                          ye_0=ye_0, ys_0=ys_0):
                        v = EA(v)
                        base_e2 = e2_0 + N_TPE
                        for j in range(NHT * 2):
                            m, nch = divmod(j, 2)
                            v.wait_ge(s_e2, base_e2 + j + 1)
                            n0 = 0 if nch == 0 else 512
                            n1 = 512 if nch == 0 else CAPE
                            v.tensor_tensor(hfm[:, m, n0:n1],
                                            relb[:, 0:n1 - n0],
                                            relb[:, 0:n1 - n0],
                                            ALU.mult).then_inc(s_hv, 1)
                        for stt in range(NES):
                            for nch in range(2):
                                j = stt * 2 + nch
                                v.wait_ge(s_ym, ym_0 + j + 1)
                                if stt > 0 and nch == 0:
                                    v.wait_ge(s_ys, ys_0 + 16 * stt)
                                v.tensor_copy(yet[:, nch * 512:(nch + 1) * 512],
                                              ps_y[j % 2][:, :]).then_inc(s_ye, 1)

                    @blk.sync
                    def _(sy, e=e, ye_0=ye_0, ys_0=ys_0):
                        for stt in range(NES):
                            sy.wait_ge(s_ye, ye_0 + stt * 2 + 2)
                            sy.dma_start(ye_hbm[e * CAPE + stt * P:
                                                e * CAPE + (stt + 1) * P, :],
                                         yet[:, :]).then_inc(s_ys, 16)
                        sy.wait_ge(s_ys, ys_0 + 16 * NES)

        s_g = nc.alloc_semaphore("l0f_sg")
        s_gt = nc.alloc_semaphore("l0f_sgt")
        s_ge = nc.alloc_semaphore("l0f_sge")
        s_go = nc.alloc_semaphore("l0f_sgo")
        s_gs = nc.alloc_semaphore("l0f_sgs")
        with ExitStack() as stG:
            EG = stG.enter_context
            xcg = EG(nc.sbuf_tensor("l0f_xcg", [P, KD, C], F32))
            dti = EG(nc.sbuf_tensor("l0f_dti", [P, NTT, 1024], F32))
            dfm = EG(nc.sbuf_tensor("l0f_dfm", [P, KD, C], F32))
            obuf = EG(nc.sbuf_tensor("l0f_og", [P, KD, C], F32))
            ps_gf = BankedPsum(stG, nc, "l0f_psgf", 2, P, P, F32)
            for i in range(NCHK):
                with nc.Block(f"l0f_g{i}") as blk:
                    @blk.sync
                    def _(sy, i=i):
                        sy.dma_start(xcg[:, :, :],
                                     x_in_v[:, :, i * C:(i + 1) * C]
                                     ).then_inc(s_x, 16)
                        sy.wait_ge(s_x, 16 * (NCHK + i + 1))
                    @blk.gpsimd
                    def _(g, i=i):
                        for tt in range(NTT):
                            g.indirect_dma_start(
                                dti[:, tt, :], None, ye_hbm[:, :],
                                bass.IndirectOffsetOnAxis(
                                    dest_all[:, i * NTT + tt:i * NTT + tt + 1],
                                    0)).then_inc(s_g, 16)
                        g.wait_ge(s_g, 16 * NTT * (i + 1))
                with nc.Block(f"l0f_g2_{i}") as blk:
                    @blk.tensor
                    def _(te, i=i):
                        te.wait_ge(s_g, 16 * NTT * (i + 1))
                        for j in range(NTT * KD):
                            tt, k = divmod(j, KD)
                            jj = i * NTT * KD + j
                            if j >= 2:
                                te.wait_ge(s_ge, jj - 1)
                            te.transpose(ps_gf[:, (jj % 2) * P:(jj % 2 + 1) * P],
                                         dti[:, tt, k * P:(k + 1) * P],
                                         p.sb["ident_f32"][:, :]).then_inc(s_gt, 1)
                    @blk.scalar
                    def _(act, i=i):
                        act = EA(act)
                        for j in range(NTT * KD):
                            tt, k = divmod(j, KD)
                            jj = i * NTT * KD + j
                            act.wait_ge(s_gt, jj + 1)
                            act.activation(dfm[:, k, tt * P:(tt + 1) * P],
                                           ps_gf[:, (jj % 2) * P:(jj % 2 + 1) * P],
                                           AF.Copy).then_inc(s_ge, 1)
                    @blk.vector
                    def _(v, i=i):
                        v = EA(v)
                        v.wait_ge(s_ge, (i + 1) * NTT * KD)
                        last = None
                        for k in range(KD):
                            last = v.tensor_tensor(obuf[:, k, :], xcg[:, k, :],
                                                   dfm[:, k, :], ALU.add)
                        last.then_inc(s_go, 1)
                    @blk.sync
                    def _(sy, i=i):
                        sy.wait_ge(s_go, i + 1)
                        sy.dma_start(x_out_v[:, :, i * C:(i + 1) * C],
                                     obuf[:, :, :]).then_inc(s_gs, 16)
                        sy.wait_ge(s_gs, 16 * (i + 1))



def emit_l3_moe(p: Prog, x_in, y_out_tm, wr_w, w1_w, w2_w, xe_hbm, ye_hbm,
                lnstage, cdram, acdram, ltmask, cap, ncores,
                stop_phase=None):
    nc, T = p.nc, p.T

    def _dbg_finish():
        s_f = nc.alloc_semaphore("l3f_dbgf")
        with nc.Block("l3f_dbg") as blk:
            @blk.sync
            def _(sy):
                for z in range(8):
                    r0 = z * (T // 8)
                    r1 = r0 + T // 8
                    sy.dma_start(
                        y_out_tm[r0:r1, :].rearrange("t d -> (t d)"),
                        x_in[:, :].rearrange("d t -> (d t)")[r0 * D:r1 * D]
                    ).then_inc(s_f, 16)
                sy.wait_ge(s_f, 16 * 8)
    C = 512
    NCHK = T // C
    NTT = C // P
    NTTT = T // P
    CAPE = 768
    NSLOT = 8 * CAPE
    HD = 4096
    NHT = HD // P
    NES = CAPE // P
    NMB = HD // 512

    with ExitStack() as st:
        E = st.enter_context
        wr_sb = E(nc.sbuf_tensor("l3f_wr", [P, KD, 8], F32))
        iota_e8 = E(nc.sbuf_tensor("l3f_iotae", [P, 8], F32))
        dest_all = E(nc.sbuf_tensor("l3f_dest", [P, NTTT], I32))
        eid_all = E(nc.sbuf_tensor("l3f_eid", [P, NTTT], F32))
        rnk_all = E(nc.sbuf_tensor("l3f_rnk", [P, NTTT], F32))
        p_all = E(nc.sbuf_tensor("l3f_pall", [P, NTTT], F32))
        base8 = E(nc.sbuf_tensor("l3f_base8", [P, 8], F32))
        counts = E(nc.sbuf_tensor("l3f_counts", [8, 1], F32))
        ps_tb = BankedPsum(st, nc, "l3f_pstb", 2, P, P, BF16)

        with nc.sbuf_tensor("l3f_ioti", [P, 8], I32) as ioti, \
             nc.sbuf_tensor("l3f_zrow", [P, 1024], BF16) as zrow, \
             nc.sbuf_tensor("l3f_zrowf", [1, 1024], F32) as zrowf:
            ch = nc.alloc_semaphore("l3f_ch")
            with nc.Block("l3f_prep1") as blk:
                @blk.sync
                def _(sy):
                    sy.dma_start(wr_sb[:, :, :],
                                 wr_w[:, :].rearrange("(k p) e -> p k e", p=P)
                                 ).then_inc(ch, 16)
                    sy.wait_ge(ch, 16)
                @blk.gpsimd
                def _(g):
                    g.iota(ioti[:, :], [[1, 8]], channel_multiplier=0)
                @blk.vector
                def _(v):
                    v = EA(v)
                    v.memset(zrow[:, :], 0.0)
                    v.memset(zrowf[:, :], 0.0)
            with nc.Block("l3f_prep2") as blk:
                @blk.vector
                def _(v):
                    v.tensor_copy(iota_e8[:, :], ioti[:, :])
                @blk.sync
                def _(sy):
                    nz = (NSLOT + 1 + P - 1) // P
                    for z in range(nz):
                        r0 = z * P
                        r1 = min(r0 + P, NSLOT + 1)
                        sy.dma_start(xe_hbm[r0:r1, :],
                                     zrow[0:r1 - r0, :]).then_inc(ch, 16)
                    sy.dma_start(ye_hbm[NSLOT:NSLOT + 1, :],
                                 zrowf[0:1, :]).then_inc(ch, 16)
                    sy.wait_ge(ch, 32 + 16 * nz)

        if stop_phase == "prep":
            _dbg_finish()
            return

        s_x = nc.alloc_semaphore("l3f_sx")
        s_lg = nc.alloc_semaphore("l3f_slg")
        s_ev = nc.alloc_semaphore("l3f_sev")
        s_tp = nc.alloc_semaphore("l3f_stp")
        s_ab = nc.alloc_semaphore("l3f_sab")
        s_ls = nc.alloc_semaphore("l3f_sls")
        x_in_v = fmv(x_in)
        with ExitStack() as stR:
            ER = stR.enter_context
            xc = ER(nc.sbuf_tensor("l3f_xc", [P, KD, C], F32))
            lnb = ER(nc.sbuf_tensor("l3f_lnb", [P, KD, C], BF16))
            ln32 = ER(nc.sbuf_tensor("l3f_ln32", [P, KD, C], F32))
            lntm = ER(nc.sbuf_tensor("l3f_lntm", [P, NTT, 1024], BF16))
            lg_sb = ER(nc.sbuf_tensor("l3f_lg", [8, C], F32))
            lgtm = ER(nc.sbuf_tensor("l3f_lgtm", [P, NTT, 8], F32))
            etb = ER(nc.sbuf_tensor("l3f_etb", [P, 8], F32))
            m8 = ER(nc.sbuf_tensor("l3f_m8", [P, NTT], F32))
            msum = ER(nc.sbuf_tensor("l3f_msum", [P, NTT], F32))
            masktm = ER(nc.sbuf_tensor("l3f_masktm", [P, NTT, 8], F32))
            maskfm = ER(nc.sbuf_tensor("l3f_maskfm", [8, C], F32))
            rankfm = ER(nc.sbuf_tensor("l3f_rankfm", [8, C], F32))
            ranktm = ER(nc.sbuf_tensor("l3f_ranktm", [P, NTT, 8], F32))
            tcol = ER(nc.sbuf_tensor("l3f_tcol", [P, 4], F32))
            ones8 = ER(nc.sbuf_tensor("l3f_ones8", [8, C], F32))
            carry = ER(nc.sbuf_tensor("l3f_carry", [8, 1], F32))
            ln = LnFm(p, stR, "l3fln", C)
            pbR = BankedPsum(stR, nc, "l3f_pbr", 2, P, 512, F32)
            ps_lg = pbR.t[0][0:8, 0:C]
            with nc.Block("l3f_rinit") as blk:
                @blk.vector
                def _(v):
                    v = EA(v)
                    v.memset(ones8[:, :], 1.0)
                    v.memset(carry[:, :], 0.0)

            for i in range(NCHK):
                ev0, ab0 = i * (3 * NTT + 2), i * (NTT * (KD + 1))
                tp0 = i * (NTT * KD + 3 * NTT)
                with nc.Block(f"l3f_r{i}") as blk:
                    @blk.sync
                    def _(sy, i=i):
                        sy.dma_start(xc[:, :, :],
                                     x_in_v[:, :, i * C:(i + 1) * C]
                                     ).then_inc(s_x, 16)
                        sy.wait_ge(s_x, 16 * (i + 1))
                with nc.Block(f"l3f_r2_{i}") as blk:
                    @blk.scalar
                    def _(act, i=i, tp0=tp0, ab0=ab0, ev0=ev0):
                        act = EA(act)
                        ln.act_emit(act, i, xc[:, :, :])
                        for j in range(NTT * KD):
                            tt, k = divmod(j, KD)
                            act.wait_ge(s_tp, tp0 + j + 1)
                            act.activation(lntm[:, tt, k * P:(k + 1) * P],
                                           ps_tb[:, (j % 2) * P:(j % 2 + 1) * P],
                                           AF.Copy).then_inc(s_ab, 1)
                        for tt in range(NTT):
                            act.wait_ge(s_ev, ev0 + 1 + tt + 1)
                            act.activation(etb[:, :], lgtm[:, tt, :], AF.Exp,
                                           bias=m8[:, tt:tt + 1],
                                           accum_out=msum[:, tt:tt + 1]
                                           ).then_inc(s_ab, 1)

                    @blk.tensor
                    def _(te, i=i, tp0=tp0, ev0=ev0, ab0=ab0):
                        ln.pe_emit(te, i, p.sb["ones_col_bf"][:, :],
                                   p.sb["ones_row_f32"][:, :])
                        te.wait_ge(ln.sv, 2 * i + 2)
                        last = None
                        for k in range(KD):
                            last = te.matmul(ps_lg[:, :], wr_sb[:, k, :],
                                             ln32[:, k, :], start=(k == 0),
                                             stop=(k == KD - 1))
                        last.then_inc(s_lg, 1)
                        for j in range(NTT * KD):
                            tt, k = divmod(j, KD)
                            if j >= 2:
                                te.wait_ge(s_ab, ab0 + j - 1)
                            te.transpose(ps_tb[:, (j % 2) * P:(j % 2 + 1) * P],
                                         lnb[:, k, tt * P:(tt + 1) * P],
                                         p.sb["ident_bf"][:, :]).then_inc(s_tp, 1)
                        te.wait_ge(s_ev, ev0 + 1)
                        for tt in range(NTT):
                            if tt >= 2:
                                te.wait_ge(s_ev, ev0 + tt)
                            te.transpose(pbR.t[tt % 2][0:P, 0:8],
                                         lg_sb[:, tt * P:(tt + 1) * P],
                                         p.sb["ident_f32"][0:8, 0:8]
                                         ).then_inc(s_tp, 1)
                        te.wait_ge(s_ev, ev0 + 1 + NTT)
                        for tt in range(NTT):
                            if tt >= 2:
                                te.wait_ge(s_ev, ev0 + NTT + tt)
                            te.transpose(pbR.t[tt % 2][0:8, 0:P],
                                         masktm[:, tt, :],
                                         p.sb["ident_f32"][:, :]).then_inc(s_tp, 1)
                        te.wait_ge(s_ev, ev0 + 2 * NTT + 2)
                        for tt in range(NTT):
                            if tt >= 2:
                                te.wait_ge(s_ev, ev0 + 2 * NTT + tt + 1)
                            te.transpose(pbR.t[tt % 2][0:P, 0:8],
                                         rankfm[:, tt * P:(tt + 1) * P],
                                         p.sb["ident_f32"][0:8, 0:8]
                                         ).then_inc(s_tp, 1)

                    @blk.vector
                    def _(v, i=i, tp0=tp0, ev0=ev0, ab0=ab0):
                        v = EA(v)
                        ln.v_emit(v, i, xc[:, :, :], lnb, out2_ap=ln32)
                        v.wait_ge(s_lg, i + 1)
                        v.tensor_copy(lg_sb[:, :], ps_lg[:, :]).then_inc(s_ev, 1)
                        for tt in range(NTT):
                            v.wait_ge(s_tp, tp0 + NTT * KD + tt + 1)
                            v.tensor_copy(lgtm[:, tt, :],
                                          pbR.t[tt % 2][0:P, 0:8])
                            v.reduce_max(m8[:, tt:tt + 1], lgtm[:, tt, :],
                                         axis=AX.X, negate=True)
                            v.tensor_scalar(masktm[:, tt, :], lgtm[:, tt, :],
                                            m8[:, tt:tt + 1], 0.0,
                                            ALU.add, ALU.is_ge).then_inc(s_ev, 1)
                        for tt in range(NTT):
                            v.wait_ge(s_tp, tp0 + NTT * KD + NTT + tt + 1)
                            v.tensor_copy(maskfm[:, tt * P:(tt + 1) * P],
                                          pbR.t[tt % 2][0:8, 0:P]
                                          ).then_inc(s_ev, 1)
                        v.tensor_tensor_scan(rankfm[:, :], ones8[:, :],
                                             maskfm[:, :], carry[:, :],
                                             ALU.mult, ALU.add)
                        v.tensor_copy(carry[:, :], rankfm[:, C - 1:C])
                        v.tensor_tensor(rankfm[:, :], rankfm[:, :], maskfm[:, :],
                                        ALU.subtract).then_inc(s_ev, 1)
                        for tt in range(NTT):
                            g_tt = i * NTT + tt
                            v.wait_ge(s_tp, tp0 + NTT * KD + 2 * NTT + tt + 1)
                            v.wait_ge(s_ab, ab0 + NTT * KD + tt + 1)
                            v.reciprocal(p_all[:, g_tt:g_tt + 1],
                                         msum[:, tt:tt + 1])
                            v.tensor_copy(ranktm[:, tt, :],
                                          pbR.t[tt % 2][0:P, 0:8])
                            v.tensor_tensor(ranktm[:, tt, :], ranktm[:, tt, :],
                                            masktm[:, tt, :], ALU.mult)
                            v.reduce_sum(rnk_all[:, g_tt:g_tt + 1],
                                         ranktm[:, tt, :], axis=AX.X)
                            v.tensor_tensor(lgtm[:, tt, :], masktm[:, tt, :],
                                            iota_e8[:, :], ALU.mult)
                            v.reduce_sum(eid_all[:, g_tt:g_tt + 1],
                                         lgtm[:, tt, :], axis=AX.X
                                         ).then_inc(s_ev, 1)

                    @blk.sync
                    def _(sy, i=i, ab0=ab0):
                        sy.wait_ge(s_ab, ab0 + NTT * KD)
                        sy.dma_start(
                            lnstage[:, :].rearrange(
                                "(n p) d -> p n d", p=P)[:, i * NTT:(i + 1) * NTT,
                                                         :],
                            lntm[:, :, :]).then_inc(s_ls, 16)
                        sy.wait_ge(s_ls, 16 * (i + 1))

            with nc.Block("l3f_cnt") as blk:
                @blk.vector
                def _(v):
                    v.tensor_copy(counts[:, :], carry[:, :])

        if stop_phase == "R":
            _dbg_finish()
            return

        s_cc = nc.alloc_semaphore("l3f_scc")
        s_cd = nc.alloc_semaphore("l3f_scd")
        s_cg = nc.alloc_semaphore("l3f_scg")
        with nc.sbuf_tensor("l3f_ac", [8, 8], F32) as acsb, \
             nc.sbuf_tensor("l3f_ltm", [8, 1], F32) as ltmsb, \
             nc.sbuf_tensor("l3f_b8r", [1, 8], F32) as b8r, \
             nc.psum_tensor("l3f_psb", [1, 8], F32) as ps_b, \
             nc.psum_tensor("l3f_psb2", [P, 8], F32) as ps_b2:
            with nc.Block("l3f_cc1") as blk:
                @blk.sync
                def _(sy):
                    sy.dma_start(cdram[:, :], counts[:, :]).then_inc(s_cd, 16)
                    sy.dma_start(ltmsb[:, :], ltmask[:, :]).then_inc(s_cd, 16)
                    sy.wait_ge(s_cd, 32)
                @blk.vector
                def _(v):
                    v.memset(acsb[:, :], 0.0)
            with nc.Block("l3f_cc2") as blk:
                @blk.gpsimd
                def _(g):
                    if ncores > 1:
                        g.collective_compute(
                            "AllGather", ALU.bypass,
                            replica_groups=[list(range(ncores))],
                            ins=[cdram.ap().opt()],
                            outs=[acdram.ap().opt()]).then_inc(s_cc, 1)
                        g.wait_ge(s_cc, 1)
                    else:
                        g.dma_start(acdram[0:1, :], cdram[:, :]).then_inc(s_cg, 16)
                        g.wait_ge(s_cg, 16)
            with nc.Block("l3f_cc3") as blk:
                @blk.sync
                def _(sy):
                    sy.dma_start(acsb[0:ncores, :],
                                 acdram[0:ncores, :]).then_inc(s_cd, 16)
                    sy.wait_ge(s_cd, 48)
            with nc.Block("l3f_cc4") as blk:
                @blk.tensor
                def _(te):
                    te.matmul(ps_b[:, :], ltmsb[0:8, :], acsb[:, :],
                              start=True, stop=True).then_inc(s_cc, 1)
                @blk.vector
                def _(v):
                    v.wait_ge(s_cc, 2 if ncores > 1 else 1)
                    v.tensor_copy(b8r[:, :], ps_b[:, :]).then_inc(s_cc, 1)
            with nc.Block("l3f_cc5") as blk:
                @blk.tensor
                def _(te):
                    te.matmul(ps_b2[:, :], p.sb["ones_row_f32"][:, :],
                              b8r[:, :], start=True, stop=True).then_inc(s_cc, 1)
                @blk.vector
                def _(v):
                    v.wait_ge(s_cc, 4 if ncores > 1 else 3)
                    v.tensor_copy(base8[:, :], ps_b2[:, :])

        if stop_phase == "CC":
            _dbg_finish()
            return

        s_r2 = nc.alloc_semaphore("l3f_sr2")
        s_sc = [nc.alloc_semaphore("l3f_ssc0"), nc.alloc_semaphore("l3f_ssc1")]
        s_lld = [nc.alloc_semaphore("l3f_slld0"), nc.alloc_semaphore("l3f_slld1")]
        with nc.sbuf_tensor("l3f_lnt2", [P, 2, 1024], BF16) as lnt2, \
             nc.sbuf_tensor("l3f_t4", [P, 6], F32) as t4, \
             nc.sbuf_tensor("l3f_eq8", [P, 8], F32) as eq8:
            lnsv = lnstage[:, :].rearrange("(n p) d -> p n d", p=P)
            with nc.Block("l3f_rr2") as blk:
                @blk.sync
                def _(sy):
                    for tt in range(NTTT):
                        if tt >= 2:
                            sy.wait_ge(s_sc[tt % 2], 16 * (tt // 2))
                        sy.dma_start(lnt2[:, tt % 2, :],
                                     lnsv[:, tt, :]).then_inc(s_lld[tt % 2], 16)

                @blk.vector
                def _(v):
                    v = EA(v)
                    for tt in range(NTTT):
                        v.tensor_scalar(eq8[:, :], iota_e8[:, :],
                                        eid_all[:, tt:tt + 1], None,
                                        ALU.is_equal)
                        v.tensor_tensor(eq8[:, :], eq8[:, :], base8[:, :],
                                        ALU.mult)
                        v.reduce_sum(t4[:, 0:1], eq8[:, :], axis=AX.X)
                        v.tensor_tensor(t4[:, 1:2], rnk_all[:, tt:tt + 1],
                                        t4[:, 0:1], ALU.add)
                        v.tensor_scalar(t4[:, 2:3], t4[:, 1:2], float(cap),
                                        None, ALU.is_ge)
                        v.scalar_tensor_tensor(t4[:, 3:4], eid_all[:, tt:tt + 1],
                                               float(CAPE),
                                               rnk_all[:, tt:tt + 1],
                                               ALU.mult, ALU.add)
                        v.scalar_tensor_tensor(t4[:, 4:5], t4[:, 2:3], 1e9,
                                               t4[:, 3:4], ALU.mult, ALU.add)
                        v.tensor_scalar(t4[:, 5:6], t4[:, 4:5], float(NSLOT),
                                        None, ALU.min)
                        v.tensor_copy(dest_all[:, tt:tt + 1],
                                      t4[:, 5:6]).then_inc(s_r2, 1)

                @blk.gpsimd
                def _(g):
                    for tt in range(NTTT):
                        g.wait_ge(s_r2, tt + 1)
                        g.wait_ge(s_lld[tt % 2], 16 * (tt // 2 + 1))
                        g.indirect_dma_start(
                            xe_hbm[:, :],
                            bass.IndirectOffsetOnAxis(
                                dest_all[:, tt:tt + 1], 0),
                            lnt2[:, tt % 2, :], None,
                            bounds_check=NSLOT - 1,
                            oob_is_err=False).then_inc(s_sc[tt % 2], 16)
                    g.wait_ge(s_sc[0], 16 * ((NTTT + 1) // 2))
                    g.wait_ge(s_sc[1], 16 * (NTTT // 2))

        if stop_phase == "R2":
            _dbg_finish()
            return

        s_wx = nc.alloc_semaphore("l3f_swx")
        s_w2 = nc.alloc_semaphore("l3f_sw2")
        s_wf = [nc.alloc_semaphore("l3f_swf0"), nc.alloc_semaphore("l3f_swf1")]
        s_t2e = nc.alloc_semaphore("l3f_st2")
        s_e2 = nc.alloc_semaphore("l3f_se2")
        s_hm = nc.alloc_semaphore("l3f_shm")
        s_ym = nc.alloc_semaphore("l3f_sym")
        s_ye = nc.alloc_semaphore("l3f_sye")
        s_ys = nc.alloc_semaphore("l3f_sys")
        N_TPE = NES * KD
        N_E2 = N_TPE + 2 * NHT
        with ExitStack() as stE:
            EE = stE.enter_context
            xet = EE(nc.sbuf_tensor("l3f_xet", [P, 1024], BF16))
            xefm = EE(nc.sbuf_tensor("l3f_xefm", [P, KD, CAPE], BF16))
            fcb = EE(nc.sbuf_tensor("l3f_fcb", [P, 2, KD, 512], BF16))
            w2sb = EE(nc.sbuf_tensor("l3f_w2", [P, NHT, 1024], BF16))
            hfm = EE(nc.sbuf_tensor("l3f_hfm", [P, NHT, CAPE], BF16))
            yet = EE(nc.sbuf_tensor("l3f_yet", [P, 1024], F32))
            ps_h = [EE(nc.psum_tensor("l3f_psh0", [P, 512], F32)),
                    EE(nc.psum_tensor("l3f_psh1", [P, 512], F32))]
            ps_y = [EE(nc.psum_tensor("l3f_psy0", [P, 512], F32)),
                    EE(nc.psum_tensor("l3f_psy1", [P, 512], F32))]
            for e in range(8):
                t2_0 = e * N_TPE
                e2_0 = e * N_E2
                hm_0 = e * (NMB * 8)
                ym_0 = e * (NES * 2)
                ye_0 = e * (NES * 2)
                ys_0 = e * (NES * 16)
                wx_0 = e * 16 * NES
                w2_0 = e * 16
                wf_0 = [e * 16 * (NMB - NMB // 2), e * 16 * (NMB // 2)]
                with nc.Block(f"l3f_e{e}") as blk:
                    @blk.gpsimd
                    def _(g, e=e, t2_0=t2_0, hm_0=hm_0, wx_0=wx_0, w2_0=w2_0,
                          wf_0=wf_0):
                        for stt in range(NES):
                            g.dma_start(xet[:, :],
                                        xe_hbm[e * CAPE + stt * P:
                                               e * CAPE + (stt + 1) * P, :]
                                        ).then_inc(s_wx, 16)
                            g.wait_ge(s_t2e, t2_0 + (stt + 1) * KD)
                        g.dma_start(w2sb[:, :, :],
                                    w2_w[e, :, :].rearrange(
                                        "(k p) d -> p k d", p=P)
                                    ).then_inc(s_w2, 16)
                        for mb in range(NMB):
                            if mb >= 2:
                                g.wait_ge(s_hm, hm_0 + (mb - 1) * 8)
                            g.dma_start(fcb[:, mb % 2, :, :],
                                        w1_w[e, :, mb * 512:(mb + 1) * 512]
                                        .rearrange("(k p) h -> p k h", p=P)
                                        ).then_inc(s_wf[mb % 2], 16)
                        g.wait_ge(s_wf[0], wf_0[0] + 16 * (NMB - NMB // 2))
                        g.wait_ge(s_wf[1], wf_0[1] + 16 * (NMB // 2))
                        g.wait_ge(s_w2, w2_0 + 16)

                    @blk.tensor
                    def _(te, e=e, t2_0=t2_0, e2_0=e2_0, hm_0=hm_0, ym_0=ym_0,
                          ye_0=ye_0, wx_0=wx_0, w2_0=w2_0, wf_0=wf_0):
                        tpn = 0
                        for stt in range(NES):
                            te.wait_ge(s_wx, wx_0 + 16 * (stt + 1))
                            for k in range(KD):
                                if tpn >= 2:
                                    te.wait_ge(s_e2, e2_0 + tpn - 1)
                                te.transpose(ps_tb[:, (tpn % 2) * P:
                                                   (tpn % 2 + 1) * P],
                                             xet[:, k * P:(k + 1) * P],
                                             p.sb["ident_bf"][:, :]
                                             ).then_inc(s_t2e, 1)
                                tpn += 1
                        te.wait_ge(s_e2, e2_0 + N_TPE)
                        for mb in range(NMB):
                            te.wait_ge(s_wf[mb % 2], wf_0[mb % 2]
                                       + 16 * (mb // 2 + 1))
                            for mm in range(4):
                                m = mb * 4 + mm
                                for nch, (n0, n1) in enumerate(((0, 512),
                                                               (512, CAPE))):
                                    j = m * 2 + nch
                                    if j >= 2:
                                        te.wait_ge(s_e2, e2_0 + N_TPE + j - 1)
                                    last = None
                                    for k in range(KD):
                                        last = te.matmul(
                                            ps_h[j % 2][:, 0:n1 - n0],
                                            fcb[:, mb % 2, k,
                                                mm * P:(mm + 1) * P],
                                            xefm[:, k, n0:n1], start=(k == 0),
                                            stop=(k == KD - 1))
                                    last.then_inc(s_hm, 1)
                        te.wait_ge(s_e2, e2_0 + N_E2)
                        te.wait_ge(s_w2, w2_0 + 16)
                        for stt in range(NES):
                            for nch in range(2):
                                j = stt * 2 + nch
                                if j >= 2:
                                    te.wait_ge(s_ye, ye_0 + j - 1)
                                last = None
                                for k in range(NHT):
                                    last = te.matmul(
                                        ps_y[j % 2][:, :],
                                        hfm[:, k, stt * P:(stt + 1) * P],
                                        w2sb[:, k, nch * 512:(nch + 1) * 512],
                                        start=(k == 0), stop=(k == NHT - 1))
                                last.then_inc(s_ym, 1)

                    @blk.scalar
                    def _(act, e=e, t2_0=t2_0, e2_0=e2_0, hm_0=hm_0):
                        act = EA(act)
                        for j in range(N_TPE):
                            stt, k = divmod(j, KD)
                            act.wait_ge(s_t2e, t2_0 + j + 1)
                            act.activation(xefm[:, k, stt * P:(stt + 1) * P],
                                           ps_tb[:, (j % 2) * P:(j % 2 + 1) * P],
                                           AF.Copy).then_inc(s_e2, 1)
                        for j in range(NHT * 2):
                            m, nch = divmod(j, 2)
                            act.wait_ge(s_hm, hm_0 + j + 1)
                            n0 = 0 if nch == 0 else 512
                            n1 = 512 if nch == 0 else CAPE
                            act.activation(hfm[:, m, n0:n1],
                                           ps_h[j % 2][:, 0:n1 - n0],
                                           AF.Relu).then_inc(s_e2, 1)

                    @blk.vector
                    def _(v, e=e, ym_0=ym_0, ye_0=ye_0, ys_0=ys_0):
                        v = EA(v)
                        for stt in range(NES):
                            for nch in range(2):
                                j = stt * 2 + nch
                                v.wait_ge(s_ym, ym_0 + j + 1)
                                if stt > 0 and nch == 0:
                                    v.wait_ge(s_ys, ys_0 + 16 * stt)
                                v.tensor_copy(yet[:, nch * 512:(nch + 1) * 512],
                                              ps_y[j % 2][:, :]).then_inc(s_ye, 1)

                    @blk.sync
                    def _(sy, e=e, ye_0=ye_0, ys_0=ys_0):
                        for stt in range(NES):
                            sy.wait_ge(s_ye, ye_0 + stt * 2 + 2)
                            sy.dma_start(ye_hbm[e * CAPE + stt * P:
                                                e * CAPE + (stt + 1) * P, :],
                                         yet[:, :]).then_inc(s_ys, 16)
                        sy.wait_ge(s_ys, ys_0 + 16 * NES)

        if stop_phase == "E":
            _dbg_finish()
            return

        s_g = nc.alloc_semaphore("l3f_sg")
        s_gt = nc.alloc_semaphore("l3f_sgt")
        s_ge = nc.alloc_semaphore("l3f_sge")
        s_go = nc.alloc_semaphore("l3f_sgo")
        s_gs = nc.alloc_semaphore("l3f_sgs")
        s_x2 = nc.alloc_semaphore("l3f_sx2")
        with ExitStack() as stG:
            EG = stG.enter_context
            xcg = EG(nc.sbuf_tensor("l3f_xcg", [P, KD, C], F32))
            dti = EG(nc.sbuf_tensor("l3f_dti", [P, NTT, 1024], F32))
            xtm = EG(nc.sbuf_tensor("l3f_xtm", [P, 1024], F32))
            ytile = EG(nc.sbuf_tensor("l3f_yt", [P, 1024], F32))
            ps_gf = BankedPsum(stG, nc, "l3f_psgf", 2, P, P, F32)
            for i in range(NCHK):
                with nc.Block(f"l3f_g{i}") as blk:
                    @blk.sync
                    def _(sy, i=i):
                        sy.dma_start(xcg[:, :, :],
                                     x_in_v[:, :, i * C:(i + 1) * C]
                                     ).then_inc(s_x2, 16)
                        sy.wait_ge(s_x2, 16 * (i + 1))
                    @blk.gpsimd
                    def _(g, i=i):
                        for tt in range(NTT):
                            g.indirect_dma_start(
                                dti[:, tt, :], None, ye_hbm[:, :],
                                bass.IndirectOffsetOnAxis(
                                    dest_all[:, i * NTT + tt:i * NTT + tt + 1],
                                    0)).then_inc(s_g, 16)
                        g.wait_ge(s_g, 16 * NTT * (i + 1))
                with nc.Block(f"l3f_g2_{i}") as blk:
                    @blk.tensor
                    def _(te, i=i):
                        for j in range(NTT * KD):
                            tt, k = divmod(j, KD)
                            if j >= 2:
                                te.wait_ge(s_ge, i * NTT * KD + j - 1)
                            te.transpose(ps_gf[:, (j % 2) * P:(j % 2 + 1) * P],
                                         xcg[:, k, tt * P:(tt + 1) * P],
                                         p.sb["ident_f32"][:, :]).then_inc(s_gt, 1)
                    @blk.scalar
                    def _(act, i=i):
                        act = EA(act)
                        for j in range(NTT * KD):
                            tt, k = divmod(j, KD)
                            act.wait_ge(s_gt, i * NTT * KD + j + 1)
                            if k == 0 and tt > 0:
                                act.wait_ge(s_go, i * NTT + tt)
                            act.activation(xtm[:, k * P:(k + 1) * P],
                                           ps_gf[:, (j % 2) * P:(j % 2 + 1) * P],
                                           AF.Copy).then_inc(s_ge, 1)
                    @blk.vector
                    def _(v, i=i):
                        v = EA(v)
                        for tt in range(NTT):
                            g_tt = i * NTT + tt
                            v.wait_ge(s_ge, g_tt * KD + KD)
                            if g_tt > 0:
                                v.wait_ge(s_gs, 16 * g_tt)
                            v.scalar_tensor_tensor(ytile[:, :], dti[:, tt, :],
                                                   p_all[:, g_tt:g_tt + 1],
                                                   xtm[:, :], ALU.mult,
                                                   ALU.add).then_inc(s_go, 1)
                    @blk.sync
                    def _(sy, i=i):
                        for tt in range(NTT):
                            g_tt = i * NTT + tt
                            sy.wait_ge(s_go, g_tt + 1)
                            sy.dma_start(y_out_tm[g_tt * P:(g_tt + 1) * P, :],
                                         ytile[:, :]).then_inc(s_gs, 16)
                        sy.wait_ge(s_gs, 16 * NTT * (i + 1))



def _emit_rope_consts(p, st, tag):
    nc = p.nc
    E = st.enter_context
    invf = E(nc.sbuf_tensor(f"{tag}_invf", [P, 1], F32))
    sgn = E(nc.sbuf_tensor(f"{tag}_sgn", [P, 1], F32))
    with nc.sbuf_tensor(f"{tag}_it", [P, 1], I32) as it, \
         nc.sbuf_tensor(f"{tag}_rtmp", [P, 3], F32) as rt:
        with nc.Block(f"{tag}_rc1") as blk:
            @blk.gpsimd
            def _(g):
                g.iota(it[:, :], [[0, 1]], channel_multiplier=1)
        with nc.Block(f"{tag}_rc2") as blk:
            @blk.vector
            def _(v):
                v = EA(v)
                v.tensor_copy(invf[:, :], it[:, :])
                for q in range(3):
                    v.tensor_scalar(rt[:, q:q + 1], invf[:, :],
                                    32.0 * (q + 1), None, ALU.is_ge)
                v.tensor_tensor(sgn[:, :], rt[:, 0:1], rt[:, 1:2],
                                ALU.subtract)
                v.tensor_tensor(sgn[:, :], sgn[:, :], rt[:, 2:3], ALU.add)
                v.tensor_scalar(sgn[:, :], sgn[:, :], 2.0, -1.0, ALU.mult,
                                ALU.add)
                v.tensor_tensor(rt[:, 0:1], rt[:, 0:1], rt[:, 1:2], ALU.add)
                v.tensor_tensor(rt[:, 0:1], rt[:, 0:1], rt[:, 2:3], ALU.add)
                v.tensor_scalar(rt[:, 0:1], rt[:, 0:1], -32.0, None, ALU.mult)
                v.tensor_tensor(invf[:, :], invf[:, :], rt[:, 0:1], ALU.add)
                v.tensor_scalar(invf[:, :], invf[:, :],
                                -math.log(10000.0) / 32.0, None, ALU.mult)
        with nc.Block(f"{tag}_rc3") as blk:
            @blk.scalar
            def _(act):
                act.activation(invf[:, :], invf[:, :], AF.Exp)
    return invf, sgn


def _emit_wT_prep(p, tg, jobs):
    nc = p.nc
    with nc.sbuf_tensor(f"{tg}_twrow", [P, 2, 1024], BF16) as wrow, \
         nc.sbuf_tensor(f"{tg}_twts", [P, KD, P], BF16) as wTs, \
         ExitStack() as _tpst3:
        ptp = BankedPsum(_tpst3, nc, f"{tg}_twps", 2, P, P, BF16)
        s_l = [nc.alloc_semaphore(f"{tg}_twl0"), nc.alloc_semaphore(f"{tg}_twl1")]
        s_t = nc.alloc_semaphore(f"{tg}_twt")
        s_e = nc.alloc_semaphore(f"{tg}_twe")
        s_o = nc.alloc_semaphore(f"{tg}_two")
        ngrp_tot = sum(j[1] for j in jobs)
        with nc.Block(f"{tg}_tw") as blk:
            @blk.gpsimd
            def _(g):
                gi = 0
                for (wd, ngrp, dst) in jobs:
                    for rg in range(ngrp):
                        if gi >= 2:
                            g.wait_ge(s_t, (gi - 1) * KD)
                        g.dma_start(wrow[:, gi % 2, :],
                                    wd[rg * P:(rg + 1) * P, :]
                                    ).then_inc(s_l[gi % 2], 16)
                        gi += 1
                g.wait_ge(s_l[0], 16 * ((gi + 1) // 2))
                g.wait_ge(s_l[1], 16 * (gi // 2))
            @blk.tensor
            def _(te):
                gi = 0
                for (wd, ngrp, dst) in jobs:
                    for rg in range(ngrp):
                        te.wait_ge(s_l[gi % 2], 16 * (gi // 2 + 1))
                        for c in range(KD):
                            j = gi * KD + c
                            if j >= 2:
                                te.wait_ge(s_e, j - 1)
                            te.transpose(ptp[:, (j % 2) * P:(j % 2 + 1) * P],
                                         wrow[:, gi % 2, c * P:(c + 1) * P],
                                         p.sb["ident_bf"][:, :]).then_inc(s_t, 1)
                        gi += 1
            @blk.scalar
            def _(act):
                act = EA(act)
                gi = 0
                for (wd, ngrp, dst) in jobs:
                    for rg in range(ngrp):
                        for c in range(KD):
                            j = gi * KD + c
                            act.wait_ge(s_t, j + 1)
                            if gi >= 1:
                                act.wait_ge(s_o, 16 * gi)
                            act.activation(wTs[:, c, :],
                                           ptp[:, (j % 2) * P:(j % 2 + 1) * P],
                                           AF.Copy).then_inc(s_e, 1)
                        gi += 1
            @blk.sync
            def _(sy):
                gi = 0
                for (wd, ngrp, dst) in jobs:
                    dv = dst[:, :].rearrange("(k p) c -> p k c", p=P)
                    for rg in range(ngrp):
                        sy.wait_ge(s_e, (gi + 1) * KD)
                        sy.dma_start(dv[:, :, rg * P:(rg + 1) * P],
                                     wTs[:, :, :]).then_inc(s_o, 16)
                        gi += 1
                sy.wait_ge(s_o, 16 * gi)


def emit_attn(p: Prog, layer, x_in, x_out, w, scratch):
    nc, T = p.nc, p.T
    C1 = 128
    NTC = C1 // P
    NC1 = T // C1
    SB = 256
    NSB = T // SB
    NKT = T // P
    NH = 16
    x_in_v = fmv(x_in)
    x_out_v = fmv(x_out)
    is2 = layer == "l2"
    tg = layer + "a"
    cnt = {}

    def sem(name):
        if name not in cnt:
            cnt[name] = [nc.alloc_semaphore(f"{tg}_{name}"), 0]
        return cnt[name][0]

    def val(name):
        sem(name)
        return cnt[name][1]

    def bump(name, n):
        sem(name)
        cnt[name][1] += n
        return cnt[name][1]

    with ExitStack() as st:
        E = st.enter_context
        k_fm = E(nc.sbuf_tensor(f"{tg}_kfm", [P, KD, T], BF16))
        v_aug = E(nc.sbuf_tensor(f"{tg}_vaug", [P, NKT, NH * 65], BF16))
        k2m = E(nc.sbuf_tensor(f"{tg}_k2m", [2, KD], F32))
        esel = E(nc.sbuf_tensor(f"{tg}_esel", [2, 2, P], F32))
        bcol = E(nc.sbuf_tensor(f"{tg}_bcol", [P, KD, 4], F32))
        sqk = E(nc.sbuf_tensor(f"{tg}_sqk", [2, KD], F32))
        if is2:
            invf, sgn = _emit_rope_consts(p, st, tg)
            wnorm = None
        else:
            wnorm = E(nc.sbuf_tensor(f"{tg}_wn", [P, 2], F32))
            invf = sgn = None
        pb = [E(nc.psum_tensor(f"{tg}_pb{z}", [P, 512], F32)) for z in range(4)]

        with nc.sbuf_tensor(f"{tg}_ii", [2, 2, P], I32) as ii, \
             nc.sbuf_tensor(f"{tg}_ij", [2, 2, P], I32) as ij:
            with nc.Block(f"{tg}_pc1") as blk:
                @blk.gpsimd
                def _(g):
                    g.iota(ii[:, :, :], [[1, 2], [0, P]], channel_multiplier=0)
                    g.iota(ij[:, :, :], [[0, 2], [0, P]], channel_multiplier=1)
                @blk.vector
                def _(v):
                    v = EA(v)
                    v.memset(v_aug[:, :, :], 0.0)
                    v.memset(k2m[:, :], 0.0)
                @blk.sync
                def _(sy):
                    spr = sem("pr")
                    if not is2:
                        for half in range(2):
                            sy.dma_start(wnorm[half * 64:(half + 1) * 64, 0:1],
                                         w["q_norm_w"][:]).then_inc(spr, 16)
                            sy.dma_start(wnorm[half * 64:(half + 1) * 64, 1:2],
                                         w["k_norm_w"][:]).then_inc(spr, 16)
                        bump("pr", 64)
                        for m in range(KD):
                            sy.dma_start(bcol[:, m, 2:3],
                                         w["proj_b"][m * P:(m + 1) * P]
                                         ).then_inc(spr, 16)
                        bump("pr", 16 * KD)
                    else:
                        for m in range(KD):
                            for ci, nm in ((0, "bk"), (1, "bv"), (2, "bo"),
                                           (3, "bq")):
                                sy.dma_start(bcol[:, m, ci:ci + 1],
                                             w[nm][m * P:(m + 1) * P]
                                             ).then_inc(spr, 16)
                        bump("pr", 16 * KD * 4)
                    sy.wait_ge(spr, val("pr"))
            with nc.Block(f"{tg}_pc2") as blk:
                @blk.vector
                def _(v):
                    v = EA(v)
                    v.tensor_tensor(esel[:, :, :], ii[:, :, :], ij[:, :, :],
                                    ALU.is_equal)
                    v.memset(v_aug[:, :, :].rearrange(
                        "p n (h c) -> p n h c", c=65)[:, :, :, 64:65], 1.0)

        if not is2:
            _emit_wT_prep(p, tg, [(w["qkv_w"], 24, scratch["qkvT"]),
                                  (w["proj_w"], 8, scratch["pwT"])])

        def trig_build_v(v, trig_t, iot_t, itf_t):
            v.tensor_copy(itf_t[:, :], iot_t[:, :])
            v.tensor_scalar(trig_t[:, 0, :], itf_t[:, :], invf[:, :],
                            None, ALU.mult)
            v.tensor_scalar(trig_t[:, 0, :], trig_t[:, 0, :],
                            1.0 / (2 * math.pi), None, ALU.mult)
            v.tensor_scalar(trig_t[:, 1, :], trig_t[:, 0, :], 0.25,
                            None, ALU.add)
            for z in range(2):
                v.tensor_copy(iot_t[:, :], trig_t[:, z, :])
                v.tensor_copy(itf_t[:, :], iot_t[:, :])
                v.tensor_tensor(trig_t[:, z, :], trig_t[:, z, :],
                                itf_t[:, :], ALU.subtract)
                v.drain()
                v.tensor_scalar(trig_t[:, z, :], trig_t[:, z, :], 0.5, None,
                                ALU.subtract)

        def trig_build_act(act, trig_t, cos_t, sin_t):
            act.activation(sin_t[:, :], trig_t[:, 0, :], AF.Sin,
                           scale=2 * math.pi)
            act.activation(cos_t[:, :], trig_t[:, 1, :], AF.Sin,
                           scale=2 * math.pi)

        def emit_proj_norm(CW, NCW, get_iot, get_trig, get_cos, get_sin,
                           wten, woff, bias_ci, wn_ci, out_fm, out_range_of,
                           nmax, lnb_t, raw_t, sq_t, rr_t, rot_t, n2t,
                           blkname):
            with nc.Block(blkname + "p") as blk:
                @blk.tensor
                def _(te):
                    for m in range(KD):
                        if m >= 2:
                            te.wait_ge(sem("ke"), val("ke") + m - 1)
                        last = None
                        for kk in range(KD):
                            last = te.matmul(
                                pb[m % 2][:, 0:CW],
                                wten[:, kk, woff + m * P:woff + (m + 1) * P],
                                lnb_t[:, kk, 0:CW], start=(kk == 0),
                                stop=(kk == KD - 1))
                        last.then_inc(sem("km"), 1)
                @blk.scalar
                def _(act):
                    act = EA(act)
                    for m in range(KD):
                        act.wait_ge(sem("km"), val("km") + m + 1)
                        kw = {}
                        fn = AF.Copy
                        if is2 and bias_ci is not None:
                            kw["bias"] = bcol[:, m, bias_ci:bias_ci + 1]
                            fn = AF.Identity
                        act.activation(raw_t[:, m, 0:CW],
                                       pb[m % 2][:, 0:CW], fn,
                                       **kw).then_inc(sem("ke"), 1)
                bump("km", KD)
                bump("ke", KD)

            with nc.Block(blkname + "r") as blk:
                @blk.scalar
                def _(act):
                    act = EA(act)
                    for m in range(KD):
                        if not is2:
                            if m > 0:
                                act.wait_ge(sem("nm"), val("nm") + 3 * m)
                            act.activation(sq_t[:, 0:CW], raw_t[:, m, 0:CW],
                                           AF.Square).then_inc(sem("na"), 1)
                            act.wait_ge(sem("nv"), val("nv") + 3 * m + 1)
                            if m > 0:
                                act.wait_ge(sem("nm"), val("nm") + 3 * m - 1)
                            act.activation(rr_t[:, 0:CW], rr_t[:, CW:2 * CW],
                                           AF.Sqrt).then_inc(sem("na"), 1)
                        else:
                            if m > 0:
                                act.wait_ge(sem("nm"), val("nm") + 2 * m)
                        act.wait_ge(sem("nv"),
                                    val("nv") + (2 if is2 else 3) * m
                                    + (1 if is2 else 2))
                        act.activation(sq_t[:, 0:CW], out_range_of(m),
                                       AF.Square).then_inc(sem("na"), 1)
                @blk.tensor
                def _(te):
                    for m in range(KD):
                        if is2:
                            if m > 0:
                                te.wait_ge(sem("nv"), val("nv") + 2 * m - 1)
                            te.matmul(pb[2][:, 0:CW], p.sb["swap_bf"][:, :],
                                      raw_t[:, m, 0:CW], start=True,
                                      stop=True).then_inc(sem("nm"), 1)
                            te.wait_ge(sem("na"), val("na") + m + 1)
                            if m > 0:
                                te.wait_ge(sem("nv"), val("nv") + 2 * m)
                            te.matmul(pb[3][0:2, 0:CW], p.sb["bones_bf"][:, :],
                                      sq_t[:, 0:CW], start=True,
                                      stop=True).then_inc(sem("nm"), 1)
                        else:
                            te.wait_ge(sem("na"), val("na") + 3 * m + 1)
                            if m > 0:
                                te.wait_ge(sem("nv"), val("nv") + 3 * m)
                            te.matmul(pb[3][0:2, 0:CW], p.sb["bones_bf"][:, :],
                                      sq_t[:, 0:CW], start=True,
                                      stop=True).then_inc(sem("nm"), 1)
                            te.wait_ge(sem("na"), val("na") + 3 * m + 2)
                            if m > 0:
                                te.wait_ge(sem("nv"), val("nv") + 3 * m + 1)
                            te.matmul(pb[2][:, 0:CW], p.sb["bones2_f32"][:, :],
                                      rr_t[0:2, 0:CW], start=True,
                                      stop=True).then_inc(sem("nm"), 1)
                            te.wait_ge(sem("na"), val("na") + 3 * m + 3)
                            te.wait_ge(sem("nv"), val("nv") + 3 * m + 2)
                            te.matmul(pb[3][0:2, 0:CW], p.sb["bones_bf"][:, :],
                                      sq_t[:, 0:CW], start=True,
                                      stop=True).then_inc(sem("nm"), 1)
                @blk.vector
                def _(v):
                    v = EA(v)
                    for m in range(KD):
                        if is2:
                            v.wait_ge(sem("nm"), val("nm") + 2 * m + 1)
                            v.tensor_tensor(rot_t[:, 0:CW], pb[2][:, 0:CW],
                                            get_sin()[:, 0:CW], ALU.mult)
                            v.tensor_scalar(rot_t[:, 0:CW], rot_t[:, 0:CW],
                                            sgn[:, :], None, ALU.mult)
                            v.tensor_tensor(out_range_of(m), raw_t[:, m, 0:CW],
                                            get_cos()[:, 0:CW], ALU.mult)
                            v.drain()
                            v.scalar_tensor_tensor(
                                out_range_of(m), out_range_of(m), -1.0,
                                rot_t[:, 0:CW], ALU.mult,
                                ALU.subtract).then_inc(sem("nv"), 1)
                            v.wait_ge(sem("nm"), val("nm") + 2 * m + 2)
                            v.reduce_max(n2t[:, 0:1], pb[3][0:2, 0:CW],
                                         axis=AX.X)
                            v.tensor_tensor(nmax[:, m:m + 1], nmax[:, m:m + 1],
                                            n2t[:, 0:1],
                                            ALU.max).then_inc(sem("nv"), 1)
                        else:
                            v.wait_ge(sem("nm"), val("nm") + 3 * m + 1)
                            v.tensor_scalar(rr_t[:, CW:2 * CW],
                                            pb[3][0:2, 0:CW], 1.0 / 64.0,
                                            1e-6, ALU.mult, ALU.add)
                            v.drain()
                            v.reciprocal(rr_t[:, CW:2 * CW],
                                         rr_t[:, CW:2 * CW]
                                         ).then_inc(sem("nv"), 1)
                            v.wait_ge(sem("nm"), val("nm") + 3 * m + 2)
                            v.scalar_tensor_tensor(
                                out_range_of(m), raw_t[:, m, 0:CW],
                                wnorm[:, wn_ci:wn_ci + 1], pb[2][:, 0:CW],
                                ALU.mult, ALU.mult).then_inc(sem("nv"), 1)
                            v.wait_ge(sem("nm"), val("nm") + 3 * m + 3)
                            v.reduce_max(n2t[:, 0:1], pb[3][0:2, 0:CW],
                                         axis=AX.X)
                            v.tensor_tensor(nmax[:, m:m + 1], nmax[:, m:m + 1],
                                            n2t[:, 0:1],
                                            ALU.max).then_inc(sem("nv"), 1)
                bump("nm", (2 if is2 else 3) * KD)
                bump("na", (1 if is2 else 3) * KD)
                bump("nv", (2 if is2 else 3) * KD)

        with ExitStack() as st1:
            E1 = st1.enter_context
            kvw = E1(nc.sbuf_tensor(f"{tg}_kvw", [P, KD, 2048], BF16))
            xc = E1(nc.sbuf_tensor(f"{tg}_xc1", [P, KD, C1], F32))
            lnb = E1(nc.sbuf_tensor(f"{tg}_lnb1", [P, KD, C1], BF16))
            kraw = E1(nc.sbuf_tensor(f"{tg}_kraw", [P, KD, C1], BF16))
            sqb = E1(nc.sbuf_tensor(f"{tg}_sqb", [P, C1], BF16))
            rrb = E1(nc.sbuf_tensor(f"{tg}_rrb", [2, 2 * C1], F32))
            vtmp = E1(nc.sbuf_tensor(f"{tg}_vtmp", [P, C1], BF16))
            k2t = E1(nc.sbuf_tensor(f"{tg}_k2t", [2, 1], F32))
            rot1 = E1(nc.sbuf_tensor(f"{tg}_rot1", [P, C1], F32))
            if is2:
                cosb = E1(nc.sbuf_tensor(f"{tg}_cosb", [P, C1], F32))
                sinb = E1(nc.sbuf_tensor(f"{tg}_sinb", [P, C1], F32))
                trig = E1(nc.sbuf_tensor(f"{tg}_trig", [P, 2, C1], F32))
                iot = E1(nc.sbuf_tensor(f"{tg}_iot", [P, C1], I32))
                itf = E1(nc.sbuf_tensor(f"{tg}_itf", [P, C1], F32))
            else:
                cosb = sinb = trig = iot = itf = None
            ln = LnFm(p, st1, f"{tg}ln1", C1)
            ps_tp = BankedPsum(st1, nc, f"{tg}_pstp", 2, P, P, BF16)

            with nc.Block(f"{tg}_p1w") as blk:
                @blk.gpsimd
                def _(g):
                    if is2:
                        g.dma_start(kvw[:, :, 0:1024],
                                    w["Wk"][:, :].rearrange(
                                        "(k p) c -> p k c", p=P)
                                    ).then_inc(sem("wg"), 16)
                        g.dma_start(kvw[:, :, 1024:2048],
                                    w["Wv"][:, :].rearrange(
                                        "(k p) c -> p k c", p=P)
                                    ).then_inc(sem("wg"), 16)
                        g.wait_ge(sem("wg"), bump("wg", 32))
                @blk.sync
                def _(sy):
                    if not is2:
                        sy.dma_start(kvw[:, :, :],
                                     scratch["qkvT"][:, 1024:3072].rearrange(
                                         "(k p) c -> p k c", p=P)
                                     ).then_inc(sem("wl"), 16)
                        sy.wait_ge(sem("wl"), bump("wl", 16))

            for i in range(NC1):
                with nc.Block(f"{tg}_p1l{i}") as blk:
                    @blk.sync
                    def _(sy, i=i):
                        sy.dma_start(xc[:, :, :],
                                     x_in_v[:, :, i * C1:(i + 1) * C1]
                                     ).then_inc(sem("x1"), 16)
                        sy.wait_ge(sem("x1"), bump("x1", 16))
                    if is2:
                        @blk.gpsimd
                        def _(g, i=i):
                            g.iota(iot[:, :], [[1, C1]], base=i * C1,
                                   channel_multiplier=0)
                with nc.Block(f"{tg}_p1n{i}") as blk:
                    @blk.vector
                    def _(v, i=i):
                        v = EA(v)
                        if is2:
                            trig_build_v(v, trig, iot, itf)
                            v.engine_nop().then_inc(sem("tv"), 1)
                        ln.v_emit(v, i, xc[:, :, :], lnb)
                    @blk.scalar
                    def _(act, i=i):
                        act = EA(act)
                        if is2:
                            act.wait_ge(sem("tv"), val("tv") + 1)
                            trig_build_act(act, trig, cosb, sinb)
                        ln.act_emit(act, i, xc[:, :, :])
                    @blk.tensor
                    def _(te, i=i):
                        ln.pe_emit(te, i, p.sb["ones_col_bf"][:, :],
                                   p.sb["ones_row_f32"][:, :])
                if is2:
                    bump("tv", 1)

                emit_proj_norm(
                    C1, NC1, None, None, (lambda: cosb), (lambda: sinb),
                    kvw, 0, 0, 1, k_fm,
                    (lambda m, i=i: k_fm[:, m, i * C1:(i + 1) * C1]),
                    k2m, lnb, kraw, sqb, rrb, rot1, k2t, f"{tg}_p1k{i}")

                with nc.Block(f"{tg}_p1v{i}") as blk:
                    @blk.tensor
                    def _(te, i=i):
                        tpj = 0
                        for m in range(KD):
                            if m >= 1:
                                te.wait_ge(sem("ve"),
                                           val("ve") + (1 + NTC) * m - NTC)
                            last = None
                            for kk in range(KD):
                                last = te.matmul(
                                    pb[m % 2][:, 0:C1],
                                    kvw[:, kk, 1024 + m * P:1024 + (m + 1) * P],
                                    lnb[:, kk, :], start=(kk == 0),
                                    stop=(kk == KD - 1))
                            last.then_inc(sem("vm"), 1)
                            for tc in range(NTC):
                                j = NTC * m + tc
                                te.wait_ge(sem("ve"),
                                           val("ve") + (1 + NTC) * m + 1)
                                if j >= 2:
                                    te.wait_ge(sem("ve"),
                                               val("ve")
                                               + (1 + NTC) * ((j - 2) // NTC)
                                               + 2 + ((j - 2) % NTC))
                                te.transpose(
                                    ps_tp[:, (j % 2) * P:(j % 2 + 1) * P],
                                    vtmp[:, tc * P:(tc + 1) * P],
                                    p.sb["ident_bf"][:, :]).then_inc(sem("vt"), 1)
                    @blk.scalar
                    def _(act, i=i):
                        act = EA(act)
                        for m in range(KD):
                            act.wait_ge(sem("vm"), val("vm") + m + 1)
                            kw = {}
                            fn = AF.Copy
                            if is2:
                                kw["bias"] = bcol[:, m, 1:2]
                                fn = AF.Identity
                            act.activation(vtmp[:, :], pb[m % 2][:, 0:C1],
                                           fn, **kw).then_inc(sem("ve"), 1)
                            for tc in range(NTC):
                                j = NTC * m + tc
                                act.wait_ge(sem("vt"), val("vt") + j + 1)
                                sl = ps_tp[:, (j % 2) * P:(j % 2 + 1) * P]
                                act.activation(
                                    v_aug[:, i * NTC + tc,
                                          (2 * m) * 65:(2 * m) * 65 + 64],
                                    sl[:, 0:64], AF.Copy)
                                act.activation(
                                    v_aug[:, i * NTC + tc,
                                          (2 * m + 1) * 65:(2 * m + 1) * 65 + 64],
                                    sl[:, 64:128], AF.Copy).then_inc(sem("ve"), 1)
                    bump("vm", KD)
                    bump("vt", NTC * KD)
                    bump("ve", (1 + NTC) * KD)

        with nc.Block(f"{tg}_kb") as blk:
            @blk.scalar
            def _(act):
                act.activation(sqk[:, :], k2m[:, :], AF.Sqrt)

        with ExitStack() as st2:
            E2 = st2.enter_context
            qw = E2(nc.sbuf_tensor(f"{tg}_qw", [P, KD, 1024], BF16))
            ow = qw
            xc2 = E2(nc.sbuf_tensor(f"{tg}_xc2", [P, KD, SB], F32))
            lnb2 = E2(nc.sbuf_tensor(f"{tg}_lnb2", [P, KD, SB], BF16))
            qraw = E2(nc.sbuf_tensor(f"{tg}_qraw", [P, KD, SB], BF16))
            q_fm = qraw
            sqb2 = E2(nc.sbuf_tensor(f"{tg}_sqb2", [P, SB], BF16))
            rrb2 = (None if is2 else
                    E2(nc.sbuf_tensor(f"{tg}_rrb2", [2, 2 * SB], F32)))
            rot2 = E2(nc.sbuf_tensor(f"{tg}_rot2", [P, SB], F32))
            q2t = E2(nc.sbuf_tensor(f"{tg}_q2t", [2, 1], F32))
            q2m = E2(nc.sbuf_tensor(f"{tg}_q2m", [2, KD], F32))
            nbv = E2(nc.sbuf_tensor(f"{tg}_nbv", [2, KD], F32))
            nbc = E2(nc.sbuf_tensor(f"{tg}_nbc", [P, 1], F32))
            probsT = E2(nc.sbuf_tensor(f"{tg}_pT", [P, max(NKT // 4, 1), SB], BF16))
            ctx_fm = E2(nc.sbuf_tensor(f"{tg}_ctx", [P, KD, SB], BF16))
            rrow = E2(nc.sbuf_tensor(f"{tg}_rrow", [1, SB], F32))
            obuf = xc2
            if is2:
                cosq = E2(nc.sbuf_tensor(f"{tg}_cosq", [P, SB], F32))
                sinq = E2(nc.sbuf_tensor(f"{tg}_sinq", [P, SB], F32))
                trigq = E2(nc.sbuf_tensor(f"{tg}_trigq", [P, 2, SB], F32))
                iotq = E2(nc.sbuf_tensor(f"{tg}_iotq", [P, SB], I32))
                itfq = E2(nc.sbuf_tensor(f"{tg}_itfq", [P, SB], F32))
            else:
                cosq = sinq = trigq = iotq = itfq = None
            pb.append(E2(nc.psum_tensor(f"{tg}_pb4", [P, 512], F32)))
            ln2 = LnFm(p, st2, f"{tg}ln2", C1)
            NC2 = SB // C1

            for sbi in range(NSB):
                with nc.Block(f"{tg}_p2l{sbi}") as blk:
                    @blk.sync
                    def _(sy, sbi=sbi):
                        sy.dma_start(xc2[:, :, :],
                                     x_in_v[:, :, sbi * SB:(sbi + 1) * SB]
                                     ).then_inc(sem("x2"), 16)
                        sy.wait_ge(sem("x2"), bump("x2", 16))
                        if not is2:
                            sy.dma_start(qw[:, :, :],
                                         scratch["qkvT"][:, 0:1024].rearrange(
                                             "(k p) c -> p k c", p=P)
                                         ).then_inc(sem("wl"), 16)
                            sy.wait_ge(sem("wl"), bump("wl", 16))
                    @blk.gpsimd
                    def _(g, sbi=sbi):
                        if is2:
                            g.dma_start(qw[:, :, :],
                                        w["Wq"][:, :].rearrange(
                                            "(k p) c -> p k c", p=P)
                                        ).then_inc(sem("wg"), 16)
                            g.wait_ge(sem("wg"), bump("wg", 16))
                            g.iota(iotq[:, :], [[1, SB]], base=sbi * SB,
                                   channel_multiplier=0)
                    @blk.vector
                    def _(v, sbi=sbi):
                        v.memset(q2m[:, :], 0.0)

                with nc.Block(f"{tg}_p2n{sbi}") as blk:
                    @blk.vector
                    def _(v, sbi=sbi):
                        v = EA(v)
                        if is2:
                            trig_build_v(v, trigq, iotq, itfq)
                            v.engine_nop().then_inc(sem("tv"), 1)
                        for cc in range(NC2):
                            ln2.v_emit(v, sbi * NC2 + cc,
                                       xc2[:, :, cc * C1:(cc + 1) * C1],
                                       lnb2[:, :, cc * C1:(cc + 1) * C1])
                    @blk.scalar
                    def _(act, sbi=sbi):
                        act = EA(act)
                        if is2:
                            act.wait_ge(sem("tv"), val("tv") + 1)
                            trig_build_act(act, trigq, cosq, sinq)
                        for cc in range(NC2):
                            ln2.act_emit(act, sbi * NC2 + cc,
                                         xc2[:, :, cc * C1:(cc + 1) * C1])
                    @blk.tensor
                    def _(te, sbi=sbi):
                        for cc in range(NC2):
                            ln2.pe_emit(te, sbi * NC2 + cc,
                                        p.sb["ones_col_bf"][:, :],
                                        p.sb["ones_row_f32"][:, :])
                if is2:
                    bump("tv", 1)

                emit_proj_norm(
                    SB, NSB, None, None, (lambda: cosq), (lambda: sinq),
                    qw, 0, 3, 0, q_fm,
                    (lambda m: q_fm[:, m, :]),
                    q2m, lnb2, qraw, sqb2, rrb2, rot2, q2t, f"{tg}_p2q{sbi}")

                with nc.Block(f"{tg}_p2b{sbi}") as blk:
                    @blk.scalar
                    def _(act, sbi=sbi):
                        act.activation(nbv[:, :], q2m[:, :], AF.Sqrt)
                with nc.Block(f"{tg}_p2b2{sbi}") as blk:
                    @blk.vector
                    def _(v, sbi=sbi):
                        v = EA(v)
                        v.tensor_tensor(nbv[:, :], nbv[:, :], sqk[:, :],
                                        ALU.mult)
                        v.drain()
                        v.tensor_scalar(nbv[:, :], nbv[:, :], -0.125, None,
                                        ALU.mult)

                NPASS = 4 if NKT >= 4 else 2
                NKH = NKT // NPASS
                with nc.Block(f"{tg}_p2f{sbi}") as blk:
                    @blk.tensor
                    def _(te, sbi=sbi):
                        for h in range(NH):
                            hp, hm = h % 2, h // 2
                            if h > 0:
                                te.wait_ge(sem("fa"), val("fa") + 2 * h)
                            te.matmul(pb[2][:, 0:1], esel[:, hp, :],
                                      nbv[:, hm:hm + 1], start=True,
                                      stop=True).then_inc(sem("fm"), 1)
                            for half in range(NPASS):
                                for g2 in range(NKH):
                                    kt = half * NKH + g2
                                    if kt >= 2:
                                        te.wait_ge(sem("fe"),
                                                   val("fe") + h * NKT
                                                   + kt - 1)
                                    if half > 0 and g2 == 0:
                                        te.wait_ge(sem("fc"),
                                                   val("fc") + h * NKT
                                                   + half * NKH)
                                    te.matmul(
                                        pb[g2 % 2][:, 0:SB],
                                        k_fm[hp * 64:hp * 64 + 64, hm,
                                             kt * P:(kt + 1) * P],
                                        q_fm[hp * 64:hp * 64 + 64, hm, :],
                                        start=True,
                                        stop=True).then_inc(sem("fm"), 1)
                                te.wait_ge(sem("fe"),
                                           val("fe") + h * NKT
                                           + half * NKH + NKH)
                                if half == 0 and h > 0:
                                    te.wait_ge(sem("fv"), val("fv") + 2 * h - 1)
                                for kt2 in range(NKH):
                                    kt = half * NKH + kt2
                                    te.matmul(
                                        pb[4][0:65, 0:SB],
                                        v_aug[:, kt, h * 65:(h + 1) * 65],
                                        probsT[:, kt2, :],
                                        start=(half == 0 and kt2 == 0),
                                        stop=(half == NPASS - 1
                                              and kt2 == NKH - 1)
                                        ).then_inc(sem("fc"), 1)
                            te.wait_ge(sem("fv"), val("fv") + 2 * h + 1)
                            te.matmul(pb[3][0:64, 0:SB],
                                      p.sb["ones_row_f32"][0:1, 0:64],
                                      rrow[:, :], start=True,
                                      stop=True).then_inc(sem("fm"), 1)
                    @blk.scalar
                    def _(act, sbi=sbi):
                        act = EA(act)
                        for h in range(NH):
                            act.wait_ge(sem("fm"), val("fm") + h * (NKT + 2) + 1)
                            act.activation(nbc[:, :], pb[2][:, 0:1],
                                           AF.Copy).then_inc(sem("fa"), 1)
                            for half in range(NPASS):
                                for g2 in range(NKH):
                                    kt = half * NKH + g2
                                    act.wait_ge(sem("fm"),
                                                val("fm") + h * (NKT + 2) + 1
                                                + kt + 1)
                                    act.activation(
                                        probsT[:, g2, :], pb[g2 % 2][:, 0:SB],
                                        AF.Exp, bias=nbc[:, :],
                                        scale=0.125).then_inc(sem("fe"), 1)
                            act.wait_ge(sem("fc"), val("fc") + (h + 1) * NKT)
                            act.activation(
                                ctx_fm[(h % 2) * 64:(h % 2) * 64 + 64,
                                       h // 2, :],
                                pb[4][0:64, 0:SB], AF.Copy).then_inc(sem("fa"), 1)
                    @blk.vector
                    def _(v, sbi=sbi):
                        v = EA(v)
                        for h in range(NH):
                            v.wait_ge(sem("fc"), val("fc") + (h + 1) * NKT)
                            v.tensor_scalar(rrow[:, :], pb[4][64:65, 0:SB],
                                            1e-30, None, ALU.add)
                            v.drain()
                            v.reciprocal(rrow[:, :],
                                         rrow[:, :]).then_inc(sem("fv"), 1)
                            v.wait_ge(sem("fa"), val("fa") + 2 * h + 2)
                            v.wait_ge(sem("fm"), val("fm") + (h + 1) * (NKT + 2))
                            v.tensor_tensor(
                                ctx_fm[(h % 2) * 64:(h % 2) * 64 + 64,
                                       h // 2, :],
                                ctx_fm[(h % 2) * 64:(h % 2) * 64 + 64,
                                       h // 2, :],
                                pb[3][0:64, 0:SB], ALU.mult).then_inc(sem("fv"), 1)
                    bump("fm", NH * (NKT + 2))
                    bump("fe", NH * NKT)
                    bump("fc", NH * NKT)
                    bump("fa", NH * 2)
                    bump("fv", NH * 2)

                with nc.Block(f"{tg}_p2ow{sbi}") as blk:
                    @blk.gpsimd
                    def _(g, sbi=sbi):
                        if is2:
                            g.dma_start(ow[:, :, :],
                                        w["Wo"][:, :].rearrange(
                                            "(k p) c -> p k c", p=P)
                                        ).then_inc(sem("wg"), 16)
                            g.wait_ge(sem("wg"), bump("wg", 16))
                    @blk.sync
                    def _(sy, sbi=sbi):
                        if not is2:
                            sy.dma_start(ow[:, :, :],
                                         scratch["pwT"][:, :].rearrange(
                                             "(k p) c -> p k c", p=P)
                                         ).then_inc(sem("wl"), 16)
                            sy.wait_ge(sem("wl"), bump("wl", 16))
                with nc.Block(f"{tg}_p2o{sbi}") as blk:
                    @blk.tensor
                    def _(te, sbi=sbi):
                        for m in range(KD):
                            if m >= 2:
                                te.wait_ge(sem("oe"), val("oe") + m - 1)
                            last = None
                            for kk in range(KD):
                                last = te.matmul(
                                    pb[m % 2][:, 0:SB],
                                    ow[:, kk, m * P:(m + 1) * P],
                                    ctx_fm[:, kk, :], start=(kk == 0),
                                    stop=(kk == KD - 1))
                            last.then_inc(sem("om"), 1)
                    @blk.vector
                    def _(v, sbi=sbi):
                        v = EA(v)
                        for m in range(KD):
                            v.wait_ge(sem("om"), val("om") + m + 1)
                            v.scalar_tensor_tensor(
                                obuf[:, m, :], pb[m % 2][:, 0:SB],
                                bcol[:, m, 2:3], xc2[:, m, :], ALU.add,
                                ALU.add).then_inc(sem("oe"), 1)
                    @blk.sync
                    def _(sy, sbi=sbi):
                        sy.wait_ge(sem("oe"), val("oe") + KD)
                        sy.dma_start(x_out_v[:, :, sbi * SB:(sbi + 1) * SB],
                                     obuf[:, :, :]).then_inc(sem("os"), 16)
                        sy.wait_ge(sem("os"), bump("os", 16))
                    bump("om", KD)
                    bump("oe", KD)





def emit_l1_attn(p: Prog, x_in, x_out, w_qkv, w_out, b_out):
    nc, T = p.nc, p.T
    CW = 512
    NCH = T // CW
    tg = "l1a"
    x_in_v = fmv(x_in)
    x_out_v = fmv(x_out)
    cnt = {}

    def sem(name):
        if name not in cnt:
            cnt[name] = [nc.alloc_semaphore(f"{tg}_{name}"), 0]
        return cnt[name][0]

    def val(name):
        sem(name)
        return cnt[name][1]

    def bump(name, n):
        sem(name)
        cnt[name][1] += n
        return cnt[name][1]

    PAIRS = [(i, j) for i in range(4) for j in range(4)]

    with ExitStack() as st:
        E = st.enter_context
        wqkv = E(nc.sbuf_tensor(f"{tg}_wqkv", [P, KD, 1024], BF16))
        wo = E(nc.sbuf_tensor(f"{tg}_wo", [P, KD, 1024], BF16))
        bor = E(nc.sbuf_tensor(f"{tg}_bor", [1, 1024], F32))
        cntr = E(nc.sbuf_tensor(f"{tg}_cntr", [1, 512], F32))
        bsel = E(nc.sbuf_tensor(f"{tg}_bsel", [P, KD, 16], BF16))
        bselT = E(nc.sbuf_tensor(f"{tg}_bselT", [16, KD, P], BF16))
        carry = E(nc.sbuf_tensor(f"{tg}_carry", [P, KD, 2], F32))

        with nc.sbuf_tensor(f"{tg}_ia", [P, KD, 16], I32) as ia, \
             nc.sbuf_tensor(f"{tg}_ib", [P, KD, 16], I32) as ib, \
             nc.sbuf_tensor(f"{tg}_ic", [16, KD, P], I32) as ic, \
             nc.sbuf_tensor(f"{tg}_id", [16, KD, P], I32) as idd, \
             nc.sbuf_tensor(f"{tg}_ih", [16, KD, P], I32) as ih, \
             nc.sbuf_tensor(f"{tg}_ip", [P, 1], I32) as ip, \
             nc.sbuf_tensor(f"{tg}_fa", [P, KD, 16], F32) as fa, \
             nc.sbuf_tensor(f"{tg}_fb", [P, KD, 16], F32) as fb, \
             nc.sbuf_tensor(f"{tg}_fp", [P, 1], F32) as fp, \
             nc.sbuf_tensor(f"{tg}_fc", [16, KD, P], F32) as fc, \
             nc.sbuf_tensor(f"{tg}_fd", [16, KD, P], F32) as fd, \
             nc.sbuf_tensor(f"{tg}_fh", [16, KD, P], F32) as fh:
            with nc.Block(f"{tg}_pp1") as blk:
                @blk.gpsimd
                def _(g):
                    g.iota(ia[:, :, :], [[2, KD], [0, 16]], channel_multiplier=0)
                    g.iota(ib[:, :, :], [[0, KD], [1, 16]], channel_multiplier=0)
                    g.iota(ic[:, :, :], [[2, KD], [0, P]], channel_multiplier=0)
                    g.iota(idd[:, :, :], [[0, KD], [1, P]], channel_multiplier=0)
                    g.iota(ih[:, :, :], [[0, KD], [0, P]], channel_multiplier=1)
                    g.iota(ip[:, :], [[0, 1]], channel_multiplier=1)
                    g.dma_start(wo[:, :, :],
                                w_out[:, :].rearrange("(k p) c -> p k c", p=P)
                                ).then_inc(sem("wg"), 16)
                    g.wait_ge(sem("wg"), bump("wg", 16))
                @blk.sync
                def _(sy):
                    sy.dma_start(bor[:, :], b_out[:]).then_inc(sem("wl"), 16)
                    sy.wait_ge(sem("wl"), bump("wl", 16))
            with nc.Block(f"{tg}_pp2") as blk:
                @blk.vector
                def _(v):
                    v = EA(v)
                    v.tensor_copy(fa[:, :, :], ia[:, :, :])
                    v.tensor_copy(fb[:, :, :], ib[:, :, :])
                    v.tensor_copy(fp[:, :], ip[:, :])
                    v.tensor_scalar(fp[:, :], fp[:, :], 64.0, None, ALU.is_ge)
                    v.tensor_scalar(fa[:, :, :], fa[:, :, :], fp[:, :], None,
                                    ALU.add)
                    v.tensor_tensor(fa[:, :, :], fa[:, :, :], fb[:, :, :],
                                    ALU.is_equal)
                    v.tensor_copy(bsel[:, :, :], fa[:, :, :])
                    v.tensor_copy(fc[:, :, :], ic[:, :, :])
                    v.tensor_copy(fd[:, :, :], idd[:, :, :])
                    v.tensor_copy(fh[:, :, :], ih[:, :, :])
                    v.tensor_scalar(fd[:, :, :], fd[:, :, :], 64.0, None,
                                    ALU.is_ge)
                    v.drain()
                    v.tensor_tensor(fc[:, :, :], fc[:, :, :], fd[:, :, :],
                                    ALU.add)
                    v.drain()
                    v.tensor_tensor(fc[:, :, :], fc[:, :, :], fh[:, :, :],
                                    ALU.is_equal)
                    v.tensor_copy(bselT[:, :, :], fc[:, :, :])
                    v.memset(carry[:, :, :], 0.0)

        with ExitStack() as st1:
            E1 = st1.enter_context
            TOKMAX = CW + 128
            xc = E1(nc.sbuf_tensor(f"{tg}_xc", [P, KD, TOKMAX], F32))
            lnb = E1(nc.sbuf_tensor(f"{tg}_lnb", [P, KD, TOKMAX], BF16))
            qf = E1(nc.sbuf_tensor(f"{tg}_qf", [P, KD, TOKMAX], BF16))
            kf = E1(nc.sbuf_tensor(f"{tg}_kf", [P, KD, TOKMAX], BF16))
            vf = E1(nc.sbuf_tensor(f"{tg}_vf", [P, KD, TOKMAX], BF16))
            prod = E1(nc.sbuf_tensor(f"{tg}_prod", [P, 2, CW // 2], BF16))
            dots = E1(nc.sbuf_tensor(f"{tg}_dots", [16, 16, CW // 2], F32))
            mx = E1(nc.sbuf_tensor(f"{tg}_mx", [16, CW // 2], F32))
            smx = E1(nc.sbuf_tensor(f"{tg}_smx", [16, 2, CW // 2], F32))
            probs = E1(nc.sbuf_tensor(f"{tg}_probs", [16, 16, CW // 2], BF16))
            acc = E1(nc.sbuf_tensor(f"{tg}_acc", [P, KD, CW + 2], F32))
            accb = lnb
            obuf = xc
            tmpv = E1(nc.sbuf_tensor(f"{tg}_tmpv", [P, CW // 2], F32))
            ln = LnFm(p, st1, f"{tg}ln", 128)
            pba = E1(nc.psum_tensor(f"{tg}_pba", [P, 512], F32))
            pbb = E1(nc.psum_tensor(f"{tg}_pbb", [P, 512], F32))
            pdots = [E1(nc.psum_tensor(f"{tg}_pd0", [16, CW // 2], F32)),
                     E1(nc.psum_tensor(f"{tg}_pd1", [16, CW // 2], F32))]

            for ci in range(NCH):
                TOK = TOKMAX if ci < NCH - 1 else CW
                NLN = TOK // 128
                WN = CW // 2 if ci < NCH - 1 else CW // 2 - 1
                t0 = ci * CW

                with nc.Block(f"{tg}_c{ci}l") as blk:
                    @blk.sync
                    def _(sy, ci=ci, TOK=TOK, t0=t0):
                        sy.dma_start(xc[:, :, 0:TOK],
                                     x_in_v[:, :, t0:t0 + TOK]
                                     ).then_inc(sem("x"), 16)
                        sy.wait_ge(sem("x"), bump("x", 16))
                    @blk.vector
                    def _(v, ci=ci):
                        v = EA(v)
                        v.memset(acc[:, :, :], 0.0)
                        v.drain()
                        v.tensor_copy(acc[:, :, 0:2], carry[:, :, :])
                        v.memset(cntr[:, :], 2.0)
                        if ci == 0:
                            v.drain()
                            v.memset(cntr[:, 0:2], 1.0)
                        if ci == NCH - 1:
                            v.drain()
                            v.memset(cntr[:, CW - 2:CW], 1.0)

                with nc.Block(f"{tg}_c{ci}n") as blk:
                    @blk.vector
                    def _(v, ci=ci, NLN=NLN):
                        v = EA(v)
                        for cc in range(NLN):
                            ln.v_emit(v, val("lnc") + cc,
                                      xc[:, :, cc * 128:(cc + 1) * 128],
                                      lnb[:, :, cc * 128:(cc + 1) * 128])
                    @blk.scalar
                    def _(act, ci=ci, NLN=NLN):
                        act = EA(act)
                        for cc in range(NLN):
                            ln.act_emit(act, val("lnc") + cc,
                                        xc[:, :, cc * 128:(cc + 1) * 128])
                    @blk.tensor
                    def _(te, ci=ci, NLN=NLN):
                        for cc in range(NLN):
                            ln.pe_emit(te, val("lnc") + cc,
                                       p.sb["ones_col_bf"][:, :],
                                       p.sb["ones_row_f32"][:, :])
                bump("lnc", NLN)

                NSEG = (TOK + 511) // 512
                with nc.Block(f"{tg}_c{ci}q") as blk:
                    @blk.gpsimd
                    def _(g, ci=ci, TOK=TOK, NSEG=NSEG):
                        for which in range(3):
                            if which > 0:
                                g.wait_ge(sem("qm"),
                                          val("qm") + which * KD * NSEG)
                            g.dma_start(
                                wqkv[:, :, :],
                                w_qkv[:, which * 1024:(which + 1) * 1024]
                                .rearrange("(k p) c -> p k c", p=P)
                                ).then_inc(sem("wq"), 16)
                        g.wait_ge(sem("wq"), bump("wq", 48))
                    @blk.tensor
                    def _(te, ci=ci, TOK=TOK, NSEG=NSEG):
                        nn = 0
                        for which in range(3):
                            te.wait_ge(sem("wq"), val("wq") - 48 + 16 * (which + 1))
                            for m in range(KD):
                                for n0 in range(0, TOK, 512):
                                    n1 = min(n0 + 512, TOK)
                                    if nn >= 2:
                                        te.wait_ge(sem("qe"), val("qe") + nn - 1)
                                    last = None
                                    for kk in range(KD):
                                        last = te.matmul(
                                            (pba if nn % 2 == 0 else pbb)[:, 0:n1 - n0],
                                            wqkv[:, kk, m * P:(m + 1) * P],
                                            lnb[:, kk, n0:n1], start=(kk == 0),
                                            stop=(kk == KD - 1))
                                    last.then_inc(sem("qm"), 1)
                                    nn += 1
                    @blk.scalar
                    def _(act, ci=ci, TOK=TOK):
                        act = EA(act)
                        nn = 0
                        for which in range(3):
                            dst = (qf, kf, vf)[which]
                            for m in range(KD):
                                for n0 in range(0, TOK, 512):
                                    n1 = min(n0 + 512, TOK)
                                    act.wait_ge(sem("qm"), val("qm") + nn + 1)
                                    act.activation(dst[:, m, n0:n1],
                                                   (pba if nn % 2 == 0 else pbb)[:, 0:n1 - n0],
                                                   AF.Copy).then_inc(sem("qe"), 1)
                                    nn += 1
                    nqkv = 3 * KD * ((TOK + 511) // 512)
                    bump("qm", nqkv)
                    bump("qe", nqkv)

                with nc.Block(f"{tg}_c{ci}d") as blk:
                    @blk.vector
                    def _(v, ci=ci, WN=WN):
                        v = EA(v)
                        for pi, (i, j) in enumerate(PAIRS):
                            for m in range(KD):
                                jj = pi * KD + m
                                if jj >= 2:
                                    v.wait_ge(sem("dm"), val("dm") + jj - 1)
                                qs = qf[:, m, :].rearrange(
                                    "p (w two) -> p w two", two=2)[
                                    :, i // 2:i // 2 + WN, i % 2]
                                ks = kf[:, m, :].rearrange(
                                    "p (w two) -> p w two", two=2)[
                                    :, j // 2:j // 2 + WN, j % 2]
                                v.tensor_tensor(prod[:, jj % 2, 0:WN], qs, ks,
                                                ALU.mult).then_inc(sem("dv"), 1)
                    @blk.tensor
                    def _(te, ci=ci, WN=WN):
                        for pi, (i, j) in enumerate(PAIRS):
                            for m in range(KD):
                                jj = pi * KD + m
                                te.wait_ge(sem("dv"), val("dv") + jj + 1)
                                if m == 0 and pi >= 2:
                                    te.wait_ge(sem("de"), val("de") + pi - 1)
                                te.matmul(pdots[pi % 2][:, 0:WN],
                                          bsel[:, m, :],
                                          prod[:, jj % 2, 0:WN],
                                          start=(m == 0),
                                          stop=(m == KD - 1)
                                          ).then_inc(sem("dm"), 1)
                    @blk.scalar
                    def _(act, ci=ci, WN=WN):
                        act = EA(act)
                        for pi in range(16):
                            act.wait_ge(sem("dm"), val("dm") + (pi + 1) * KD)
                            act.activation(dots[:, pi, 0:WN],
                                           pdots[pi % 2][:, 0:WN],
                                           AF.Copy).then_inc(sem("de"), 1)
                    bump("dv", 16 * KD)
                    bump("dm", 16 * KD)
                    bump("de", 16)

                with nc.Block(f"{tg}_c{ci}s") as blk:
                    @blk.vector
                    def _(v, ci=ci, WN=WN):
                        v = EA(v)
                        for i in range(4):
                            v.tensor_tensor(mx[:, 0:WN], dots[:, 4 * i, 0:WN],
                                            dots[:, 4 * i + 1, 0:WN], ALU.max)
                            v.drain()
                            v.tensor_tensor(mx[:, 0:WN], mx[:, 0:WN],
                                            dots[:, 4 * i + 2, 0:WN], ALU.max)
                            v.drain()
                            v.tensor_tensor(mx[:, 0:WN], mx[:, 0:WN],
                                            dots[:, 4 * i + 3, 0:WN], ALU.max)
                            for j in range(4):
                                v.drain()
                                v.tensor_tensor(dots[:, 4 * i + j, 0:WN],
                                                dots[:, 4 * i + j, 0:WN],
                                                mx[:, 0:WN], ALU.subtract)
                            v.engine_nop().then_inc(sem("sv"), 1)
                            v.wait_ge(sem("se"), val("se") + i + 1)
                            v.tensor_tensor(smx[:, 0, 0:WN],
                                            smx[:, 1, 0:WN]
                                            if False else dots[:, 4 * i, 0:WN],
                                            dots[:, 4 * i + 1, 0:WN], ALU.add)
                            v.drain()
                            v.tensor_tensor(smx[:, 0, 0:WN], smx[:, 0, 0:WN],
                                            dots[:, 4 * i + 2, 0:WN], ALU.add)
                            v.drain()
                            v.tensor_tensor(smx[:, 0, 0:WN], smx[:, 0, 0:WN],
                                            dots[:, 4 * i + 3, 0:WN], ALU.add)
                            v.drain()
                            v.reciprocal(smx[:, 1, 0:WN], smx[:, 0, 0:WN])
                            v.drain()
                            for j in range(4):
                                v.tensor_tensor(probs[:, 4 * i + j, 0:WN],
                                                dots[:, 4 * i + j, 0:WN],
                                                smx[:, 1, 0:WN],
                                                ALU.mult)
                            v.engine_nop().then_inc(sem("sv"), 1)
                    @blk.scalar
                    def _(act, ci=ci, WN=WN):
                        act = EA(act)
                        for i in range(4):
                            act.wait_ge(sem("sv"), val("sv") + 2 * i + 1)
                            act.activation(dots[:, 4 * i:4 * i + 4, 0:WN]
                                           if WN == 512 else
                                           dots[:, 4 * i:4 * i + 4, 0:WN],
                                           dots[:, 4 * i:4 * i + 4, 0:WN],
                                           AF.Exp,
                                           scale=1.0 / 32.0).then_inc(sem("se"), 1)
                    bump("sv", 8)
                    bump("se", 4)

                with nc.Block(f"{tg}_c{ci}a") as blk:
                    @blk.tensor
                    def _(te, ci=ci, WN=WN):
                        for pi, (i, j) in enumerate(PAIRS):
                            for m in range(KD):
                                jj = pi * KD + m
                                if jj >= 2:
                                    te.wait_ge(sem("av"), val("av") + 2 * (jj - 1))
                                te.matmul((pba if jj % 2 == 0 else pbb)[:, 0:WN],
                                          bselT[:, m, :], probs[:, pi, 0:WN],
                                          start=True,
                                          stop=True).then_inc(sem("am"), 1)
                    @blk.vector
                    def _(v, ci=ci, WN=WN):
                        v = EA(v)
                        for pi, (i, j) in enumerate(PAIRS):
                            for m in range(KD):
                                jj = pi * KD + m
                                v.wait_ge(sem("am"), val("am") + jj + 1)
                                vs = vf[:, m, :].rearrange(
                                    "p (w two) -> p w two", two=2)[
                                    :, j // 2:j // 2 + WN, j % 2]
                                v.tensor_tensor(tmpv[:, 0:WN],
                                                (pba if jj % 2 == 0 else pbb)[:, 0:WN],
                                                vs, ALU.mult).then_inc(sem("av"), 1)
                                asl = acc[:, m, :].rearrange(
                                    "p (w two) -> p w two", two=2)[
                                    :, i // 2:i // 2 + WN, i % 2]
                                v.drain()
                                v.tensor_tensor(asl, asl, tmpv[:, 0:WN],
                                                ALU.add).then_inc(sem("av"), 1)
                    bump("am", 16 * KD)
                    bump("av", 2 * 16 * KD)

                with nc.Block(f"{tg}_c{ci}o") as blk:
                    @blk.vector
                    def _(v, ci=ci):
                        v = EA(v)
                        v.tensor_copy(carry[:, :, :], acc[:, :, CW:CW + 2])
                        v.tensor_copy(accb[:, :, 0:CW], acc[:, :, 0:CW])
                        v.engine_nop().then_inc(sem("ov"), 1)
                        for m in range(KD):
                            oo = m
                            v.wait_ge(sem("om"), val("om") + oo + 1)
                            v.scalar_tensor_tensor(
                                xc[:, m, 0:CW],
                                (pba if oo % 2 == 0 else pbb)[:, 0:CW],
                                1.0, xc[:, m, 0:CW],
                                ALU.mult, ALU.add).then_inc(sem("oe"), 1)
                    @blk.tensor
                    def _(te, ci=ci, t0=t0):
                        te.wait_ge(sem("ov"), val("ov") + 1)
                        for m in range(KD):
                            oo = m
                            if oo >= 2:
                                te.wait_ge(sem("oe"), val("oe") + oo - 1)
                            ps = (pba if oo % 2 == 0 else pbb)
                            last = None
                            for kk in range(KD):
                                last = te.matmul(
                                    ps[:, 0:CW], wo[:, kk, m * P:(m + 1) * P],
                                    accb[:, kk, 0:CW],
                                    start=(kk == 0), stop=False)
                            te.matmul(ps[:, 0:CW],
                                      bor[:, m * P:(m + 1) * P],
                                      cntr[:, 0:CW],
                                      start=False,
                                      stop=True).then_inc(sem("om"), 1)
                    @blk.sync
                    def _(sy, ci=ci, t0=t0):
                        sy.wait_ge(sem("oe"), val("oe") + KD)
                        sy.dma_start(x_out_v[:, :, t0:t0 + CW],
                                     xc[:, :, 0:CW]).then_inc(sem("os"), 16)
                        sy.wait_ge(sem("os"), bump("os", 16))
                    bump("ov", 1)
                    bump("om", KD)
                    bump("oe", KD)



def emit_l2_ffn(p: Prog, x_in, x_out, g1r, g2r, g1i, g2i, br, bi, wtmp_dram):
    nc, T = p.nc, p.T
    C = 512
    NCHK = T // C

    with ExitStack() as st:
        E = st.enter_context
        wrT = E(nc.sbuf_tensor("l2f_wrT", [P, 4, 512], BF16))
        wiT = E(nc.sbuf_tensor("l2f_wiT", [P, 4, 512], BF16))
        wiTn = E(nc.sbuf_tensor("l2f_wiTn", [P, 4, 512], BF16))
        bias = E(nc.sbuf_tensor("l2f_bias", [P, 8], F32))

        with nc.sbuf_tensor("l2f_g1", [4, 1024], F32) as g1sb, \
             nc.sbuf_tensor("l2f_g2", [4, 256], F32) as g2sb, \
             nc.sbuf_tensor("l2f_m2", [P, 8, 256], F32) as m2sb, \
             nc.psum_tensor("l2f_ps_m1", [P, 256], F32) as ps_m1:
            sW = nc.alloc_semaphore("l2f_sW")
            s_ld = nc.alloc_semaphore("l2f_sld")
            s_gd = nc.alloc_semaphore("l2f_sgd")
            ldv = [0]
            gdv = [0]
            swv = [0]
            for wi, (g1d, g2d) in enumerate(((g1r, g2r), (g1i, g2i))):
                dstT = wrT if wi == 0 else wiT
                with nc.Block(f"l2f_wg{wi}") as blk:
                    @blk.sync
                    def _(sy, g1d=g1d, g2d=g2d):
                        sy.dma_start(g1sb[:, :],
                                     g1d[:, :, :, :].rearrange("a b c d -> (a c) b d")
                                     ).then_inc(s_ld, 16)
                        sy.dma_start(g2sb[:, :],
                                     g2d[:, :, :, :].rearrange("c e a g -> (a c) e g")
                                     ).then_inc(s_ld, 16)
                        ldv[0] += 32
                        sy.wait_ge(s_ld, ldv[0])
                base = swv[0]
                with nc.Block(f"l2f_wm{wi}") as blk:
                    @blk.tensor
                    def _(te, base=base):
                        for m in range(8):
                            if m > 0:
                                te.wait_ge(sW, base + 2 * m)
                            te.matmul(ps_m1[:, :], g1sb[:, m * P:(m + 1) * P],
                                      g2sb[:, :], start=True,
                                      stop=True).then_inc(sW, 1)
                    @blk.vector
                    def _(v, base=base):
                        v = EA(v)
                        for m in range(8):
                            v.wait_ge(sW, base + 2 * m + 1)
                            dst = m2sb[:, m, :].rearrange("p (g e) -> p e g", g=16)
                            v.tensor_copy(dst, ps_m1[:, :]).then_inc(sW, 1)
                swv[0] = base + 16
                with nc.Block(f"l2f_ww{wi}") as blk:
                    @blk.sync
                    def _(sy):
                        sy.dma_start(wtmp_dram[:, :].rearrange(
                            "(m p) n -> p m n", p=P), m2sb[:, :, :]).then_inc(sW, 16)
                        sy.wait_ge(sW, swv[0] + 16)
                swv[0] += 16
                with nc.Block(f"l2f_wr{wi}") as blk:
                    @blk.gpsimd
                    def _(g, dstT=dstT):
                        flat = wtmp_dram[:, :].rearrange("a b -> (a b)")
                        for q in range(4):
                            srcap = bass.AP(flat.tensor, q * 8 * 256,
                                            [[256, 8], [16, 16], [8192, 32], [1, 16]])
                            g.dma_start(dstT[:, q, :], srcap).then_inc(s_gd, 16)
                        gdv[0] += 64
                        g.wait_ge(s_gd, gdv[0])
        with nc.Block("l2f_neg") as blk:
            @blk.sync
            def _(sy):
                for m in range(4):
                    sy.dma_start(bias[:, m:m + 1],
                                 br[m * P:(m + 1) * P]).then_inc(s_ld, 16)
                    sy.dma_start(bias[:, 4 + m:5 + m],
                                 bi[m * P:(m + 1) * P]).then_inc(s_ld, 16)
                ldv[0] += 128
                sy.wait_ge(s_ld, ldv[0])
            @blk.vector
            def _(v):
                v.tensor_scalar(wiTn[:, :, :], wiT[:, :, :], -1.0, None, ALU.mult)

        xc = E(nc.sbuf_tensor("l2f_xc", [P, KD, C], F32))
        lnb = E(nc.sbuf_tensor("l2f_lnb", [P, KD, C], BF16))
        obuf = E(nc.sbuf_tensor("l2f_o", [P, KD, C], F32))
        ln = LnFm(p, st, "l2fln", C)
        ps_o = [E(nc.psum_tensor("l2f_ps_o0", [P, C], F32)),
                E(nc.psum_tensor("l2f_ps_o1", [P, C], F32))]
        s_x = nc.alloc_semaphore("l2f_sx")
        s_mm = nc.alloc_semaphore("l2f_smm")
        s_o = nc.alloc_semaphore("l2f_so")
        s_st = nc.alloc_semaphore("l2f_sst")
        x_in_v = fmv(x_in)
        x_out_v = fmv(x_out)

        def mm_groups(i):
            out = []
            for g in range(8):
                m = g % 4
                if g < 4:
                    ops = [(wrT, 0), (wiTn, 4)]
                else:
                    ops = [(wiT, 0), (wrT, 4)]
                out.append((g, m, ops))
            return out

        with nc.Block("l2f_main") as blk:
            @blk.sync
            def _(sy):
                sy.dma_start(xc[:, :, :], x_in_v[:, :, 0:C]).then_inc(s_x, 16)
                for i in range(NCHK):
                    if i + 1 < NCHK:
                        sy.wait_ge(s_o, (i + 1) * 8)
                        sy.dma_start(xc[:, :, :],
                                     x_in_v[:, :, (i + 1) * C:(i + 2) * C]
                                     ).then_inc(s_x, 16)
                    for g in range(8):
                        sy.wait_ge(s_o, i * 8 + g + 1)
                        sy.dma_start(x_out_v[:, g, i * C:(i + 1) * C],
                                     obuf[:, g, :]).then_inc(s_st, 16)
                sy.wait_ge(s_st, 16 * NCHK * 8)

            @blk.scalar
            def _(act):
                act = EA(act)
                for i in range(NCHK):
                    act.wait_ge(s_x, 16 * (i + 1))
                    ln.act_emit(act, i, xc[:, :, :])

            @blk.tensor
            def _(te):
                for i in range(NCHK):
                    ln.pe_emit(te, i, p.sb["ones_col_bf"][:, :],
                               p.sb["ones_row_f32"][:, :])
                    te.wait_ge(ln.sv, 2 * i + 2)
                    for g, m, ops in mm_groups(i):
                        gi = i * 8 + g
                        if gi >= 2:
                            te.wait_ge(s_o, gi - 1)
                        last = None
                        for (wT, koff) in ops:
                            for q in range(4):
                                first = (wT is ops[0][0]) and q == 0
                                last = te.matmul(ps_o[g % 2][:, :],
                                                 wT[:, q, m * P:(m + 1) * P],
                                                 lnb[:, koff + q, :],
                                                 start=(koff == ops[0][1] and q == 0),
                                                 stop=(koff == ops[1][1] and q == 3))
                        last.then_inc(s_mm, 1)

            @blk.vector
            def _(v):
                v = EA(v)
                for i in range(NCHK):
                    if i > 0:
                        v.wait_ge(s_mm, 8 * i)
                    ln.v_emit(v, i, xc[:, :, :], lnb)
                    for g in range(8):
                        gi = i * 8 + g
                        v.wait_ge(s_mm, gi + 1)
                        if i > 0:
                            v.wait_ge(s_st, 16 * i * 8)
                        v.scalar_tensor_tensor(obuf[:, g, :], ps_o[g % 2][:, :],
                                               bias[:, g:g + 1],
                                               xc[:, g, :], ALU.add,
                                               ALU.add).then_inc(s_o, 1)



TT = 4096
NCORES = 8
SUBLAYERS = ["l0a", "l0f", "l1a", "l1f", "l2a", "l2f", "l3a", "l3f"]

_WSPECS = {
    "l0a": [("L0_attn_W_x", [96, 1024], F32), ("L0_attn_W_dt", [1024, 64], F32),
            ("L0_attn_b_dt", [1024], F32), ("L0_attn_log_A", [1024, 16], F32),
            ("L0_attn_D_param", [1024], F32)],
    "l0f": [("L0_ffn_router_w", [8, 1024], F32),
            ("L0_ffn_expert_fc", [8, 1024, 4096], F32),
            ("L0_ffn_expert_proj", [8, 1024, 4096], F32)],
    "l1a": [("L1_attn_w_qkv", [1024, 3072], F32),
            ("L1_attn_w_out", [1024, 1024], F32),
            ("L1_attn_b_out", [1024], F32)],
    "l1f": [("L1_ffn_w_gate", [3072, 1024], I32),
            ("L1_ffn_w_up", [3072, 1024], I32),
            ("L1_ffn_w_down", [1024, 3072], I32)],
    "l2a": [("L2_attn_Wq", [1024, 1024], F32), ("L2_attn_bq", [1024], F32),
            ("L2_attn_Wk", [1024, 1024], F32), ("L2_attn_bk", [1024], F32),
            ("L2_attn_Wv", [1024, 1024], F32), ("L2_attn_bv", [1024], F32),
            ("L2_attn_Wo", [1024, 1024], F32), ("L2_attn_bo", [1024], F32)],
    "l2f": [("L2_ffn_G1_R", [1, 32, 4, 32], F32), ("L2_ffn_G2_R", [4, 16, 1, 16], F32),
            ("L2_ffn_G1_I", [1, 32, 4, 32], F32), ("L2_ffn_G2_I", [4, 16, 1, 16], F32),
            ("L2_ffn_bias_real", [512], F32), ("L2_ffn_bias_imag", [512], F32)],
    "l3a": [("L3_attn_qkv_w", [3072, 1024], F32),
            ("L3_attn_proj_w", [1024, 1024], F32),
            ("L3_attn_proj_b", [1024], F32),
            ("L3_attn_q_norm_w", [64], F32), ("L3_attn_k_norm_w", [64], F32)],
    "l3f": [("L3_ffn_Wr", [1024, 8], F32), ("L3_ffn_W1", [8, 1024, 4096], F32),
            ("L3_ffn_W2", [8, 4096, 1024], F32), ("core_lt_mask", [8, 1], F32)],
}


def build_program(which, T=TT, ncores=NCORES):
    nc = bass.Bass("TRN2", target_bir_lowering=False)
    p = Prog(nc, T)
    ins = {}
    for s in which:
        for (nm, shp, dt) in _WSPECS[s]:
            ins[nm] = nc.dram_tensor(nm, list(shp), dt, kind="ExternalInput")
    x_in = nc.dram_tensor("xT", [D, T], F32, kind="ExternalInput")
    last = which[-1]
    if last == "l3f":
        y_out = nc.dram_tensor("y", [T, D], F32, kind="ExternalOutput")
    else:
        y_out = nc.dram_tensor("y", [D, T], F32, kind="ExternalOutput")
    xb = [nc.dram_tensor("xb0", [D, T], F32, kind="Internal"),
          nc.dram_tensor("xb1", [D, T], F32, kind="Internal")]

    emit_consts(p)

    def _emit_one(s, cur, nxt):
        if s == "l0a":
            emit_l0_mamba(p, cur, nxt, ins["L0_attn_W_x"], ins["L0_attn_W_dt"],
                          ins["L0_attn_b_dt"], ins["L0_attn_log_A"],
                          ins["L0_attn_D_param"])
        elif s == "l0f":
            xe = nc.dram_tensor("l0f_xe", [8 * 768 + 1, 1024], BF16, kind="Internal")
            ye = nc.dram_tensor("l0f_ye", [8 * 768 + 1, 1024], F32, kind="Internal")
            emit_l0_moe(p, cur, nxt, ins["L0_ffn_router_w"],
                        ins["L0_ffn_expert_fc"], ins["L0_ffn_expert_proj"],
                        xe, ye)
        elif s == "l1a":
            emit_l1_attn(p, cur, nxt, ins["L1_attn_w_qkv"],
                         ins["L1_attn_w_out"], ins["L1_attn_b_out"])
        elif s == "l1f":
            emit_l1_ffn(p, cur, nxt, ins["L1_ffn_w_gate"], ins["L1_ffn_w_up"],
                        ins["L1_ffn_w_down"])
        elif s == "l2a":
            emit_attn(p, "l2", cur, nxt,
                      {k: ins["L2_attn_" + k] for k in
                       ("Wq", "bq", "Wk", "bk", "Wv", "bv", "Wo", "bo")}, {})
        elif s == "l2f":
            wtmp = nc.dram_tensor("l2f_wtmp", [1024, 256], F32, kind="Internal")
            emit_l2_ffn(p, cur, nxt, ins["L2_ffn_G1_R"], ins["L2_ffn_G2_R"],
                        ins["L2_ffn_G1_I"], ins["L2_ffn_G2_I"],
                        ins["L2_ffn_bias_real"], ins["L2_ffn_bias_imag"], wtmp)
        elif s == "l3a":
            scr = {"qkvT": nc.dram_tensor("l3a_qkvT", [1024, 3072], BF16,
                                          kind="Internal"),
                   "pwT": nc.dram_tensor("l3a_pwT", [1024, 1024], BF16,
                                         kind="Internal")}
            emit_attn(p, "l3", cur, nxt,
                      {"qkv_w": ins["L3_attn_qkv_w"],
                       "proj_w": ins["L3_attn_proj_w"],
                       "proj_b": ins["L3_attn_proj_b"],
                       "q_norm_w": ins["L3_attn_q_norm_w"],
                       "k_norm_w": ins["L3_attn_k_norm_w"]}, scr)
        elif s == "l3f":
            xe = nc.dram_tensor("l3f_xe", [8 * 768 + 1, 1024], BF16, kind="Internal")
            ye = nc.dram_tensor("l3f_ye", [8 * 768 + 1, 1024], F32, kind="Internal")
            lns = nc.dram_tensor("l3f_lns", [T, 1024], BF16, kind="Internal")
            cd = nc.dram_tensor("l3f_cd", [1, 8], F32, kind="Internal")
            acd = nc.dram_tensor("l3f_acd", [8, 8], F32, kind="Internal")
            cap = (T * ncores) // 8
            emit_l3_moe(p, cur, nxt, ins["L3_ffn_Wr"], ins["L3_ffn_W1"],
                        ins["L3_ffn_W2"], xe, ye, lns, cd, acd,
                        ins["core_lt_mask"], cap, ncores)

    cur = x_in
    for si, s in enumerate(which):
        is_last = si == len(which) - 1
        nxt = y_out if is_last else xb[si % 2]
        with nc.cleanup_on_exit():
            _emit_one(s, cur, nxt)
        cur = nxt
    nc.finalize()
    return nc


GROUPS = [SUBLAYERS]

_PROG_CACHE = {}


def _get_prog(which):
    key = tuple(which)
    if key not in _PROG_CACHE:
        _PROG_CACHE[key] = build_program(list(which))
    return _PROG_CACHE[key]


def kernel(**inputs):
    x = np.ascontiguousarray(inputs["L0_attn_u"], dtype=np.float32)
    B = x.shape[0]
    cur = [np.ascontiguousarray(x[c].T) for c in range(B)]

    for which in GROUPS:
        t0 = time.time()
        nc = _get_prog(which)
        t1 = time.time()
        in_maps = []
        for c in range(B):
            m = {"xT": cur[c]}
            for s in which:
                for (nm, shp, dt) in _WSPECS[s]:
                    if nm == "core_lt_mask":
                        m[nm] = (np.arange(8) < c).astype(
                            np.float32).reshape(8, 1)
                    else:
                        m[nm] = np.ascontiguousarray(inputs[nm])
            in_maps.append(m)
        t2 = time.time()
        res = run_bass_kernel_spmd(nc, in_maps, core_ids=list(range(B)))
        cur = [np.asarray(res.results[c]["y"]) for c in range(B)]
        print(f"[kernel] {'+'.join(which)}: build {t1 - t0:.1f}s "
              f"inmaps {t2 - t1:.1f}s run {time.time() - t2:.1f}s",
              file=sys.stderr, flush=True)

    return np.stack(cur, axis=0)


# version 07
# speedup vs baseline: 1.0858x; 1.0858x over previous
import math
import sys
import time
from contextlib import ExitStack

import numpy as np
import concourse.bass as bass
import concourse.mybir as mybir
from concourse.bass_utils import run_bass_kernel_spmd

F32 = mybir.dt.float32
BF16 = mybir.dt.bfloat16
I32 = mybir.dt.int32
AF = mybir.ActivationFunctionType
ALU = mybir.AluOpType
AX = mybir.AxisListType

D = 1024
P = 128
KD = D // P


class Prog:
    def __init__(self, nc, T):
        self.nc = nc
        self.T = T
        self.sb = {}

    def sb_t(self, name, shape, dtype):
        t = self.nc.alloc_sbuf_tensor(name, shape, dtype)
        self.sb[name] = t
        return t



class BankedPsum:

    def __init__(self, st, nc, name, slots, rows, slot_cols, dtype):
        self.slot_cols = slot_cols
        self.t = [st.enter_context(
            nc.psum_tensor(f"{name}_bk{j}", [rows, slot_cols], dtype))
            for j in range(slots)]

    def __getitem__(self, key):
        rows, cols = key
        a = cols.start or 0
        b = cols.stop
        s = a // self.slot_cols
        a2 = a - s * self.slot_cols
        b2 = b - s * self.slot_cols
        assert 0 <= a2 < b2 <= self.slot_cols, (a, b, self.slot_cols)
        return self.t[s][rows, a2:b2]


def fmv(dram_t):
    return dram_t[:, :].rearrange("(k p) t -> p k t", p=P)


def _apname(x):
    t = getattr(x, "tensor", None)
    return getattr(t, "name", None)


class EA:

    def __init__(self, eng):
        self._e = eng
        self._w = set()
        self._r = set()

    def __getattr__(self, name):
        return getattr(self._e, name)

    def _pre(self, outs, ins):
        no = {n for n in (_apname(a) for a in outs) if n}
        ni = {n for n in (_apname(a) for a in ins) if n}
        if ((no | ni) & self._w) or (no & self._r):
            self._e.drain()
            self._w.clear()
            self._r.clear()
        self._w |= no
        self._r |= ni

    def tensor_copy(self, out, in_):
        self._pre([out], [in_])
        return self._e.tensor_copy(out, in_)

    def tensor_tensor(self, out, a, b, op):
        self._pre([out], [a, b])
        return self._e.tensor_tensor(out, a, b, op)

    def tensor_scalar(self, out, in0, s1, s2, op0, op1=ALU.bypass, accum_out=None):
        outs = [out] + ([accum_out] if accum_out is not None else [])
        self._pre(outs, [in0, s1, s2])
        if accum_out is not None:
            return self._e.tensor_scalar(out, in0, s1, s2, op0, op1,
                                         accum_out=accum_out)
        return self._e.tensor_scalar(out, in0, s1, s2, op0, op1)

    def scalar_tensor_tensor(self, out, in0, s, in1, op0, op1, accum_out=None):
        outs = [out] + ([accum_out] if accum_out is not None else [])
        self._pre(outs, [in0, s, in1])
        if accum_out is not None:
            return self._e.scalar_tensor_tensor(out, in0, s, in1, op0, op1,
                                                accum_out=accum_out)
        return self._e.scalar_tensor_tensor(out, in0, s, in1, op0, op1)

    def tensor_tensor_scan(self, out, d0, d1, initial, op0, op1):
        self._pre([out], [d0, d1, initial])
        return self._e.tensor_tensor_scan(out, d0, d1, initial, op0, op1)

    def reciprocal(self, out, in_):
        self._pre([out], [in_])
        return self._e.reciprocal(out, in_)

    def reduce_sum(self, out, in_, axis, negate=None):
        self._pre([out], [in_])
        if negate is None:
            return self._e.reduce_sum(out, in_, axis=axis)
        return self._e.reduce_sum(out, in_, axis=axis, negate=negate)

    def reduce_max(self, out, in_, axis, negate=None):
        self._pre([out], [in_])
        if negate is None:
            return self._e.reduce_max(out, in_, axis=axis)
        return self._e.reduce_max(out, in_, axis=axis, negate=negate)

    def memset(self, ap, c):
        self._pre([ap], [])
        return self._e.memset(ap, c)

    def select(self, out, mask, a, b):
        self._pre([out], [mask, a, b])
        return self._e.select(out, mask, a, b)

    def max_with_indices(self, out_max, out_idx, in_):
        self._pre([out_max, out_idx], [in_])
        return self._e.max_with_indices(out_max, out_idx, in_)

    def iota(self, out, pattern, base=None, channel_multiplier=None):
        self._pre([out], [])
        kw = {}
        if base is not None:
            kw["base"] = base
        if channel_multiplier is not None:
            kw["channel_multiplier"] = channel_multiplier
        return self._e.iota(out, pattern, **kw)

    def activation(self, out, in_, func, bias=None, scale=None, accum_out=None):
        outs = [out] + ([accum_out] if accum_out is not None else [])
        self._pre(outs, [in_, bias, scale])
        kw = {}
        if bias is not None:
            kw["bias"] = bias
        if scale is not None:
            kw["scale"] = scale
        if accum_out is not None:
            kw["accum_out"] = accum_out
        return self._e.activation(out, in_, func, **kw)

    def wait_ge(self, sem, val):
        return self._e.wait_ge(sem, val)

    def engine_nop(self):
        return self._e.engine_nop()

    def drain(self):
        self._w.clear()
        self._r.clear()
        return self._e.drain()

    def dma_start(self, out, in_, **kw):
        return self._e.dma_start(out, in_, **kw)


def emit_consts(p: Prog):
    nc = p.nc
    p.sb_t("ident_bf", [P, P], BF16)
    p.sb_t("ident_f32", [P, P], F32)
    p.sb_t("ones_col_bf", [P, 1], BF16)
    p.sb_t("ones_col_f32", [P, 1], F32)
    p.sb_t("ones_row_bf", [1, P], BF16)
    p.sb_t("ones_row_f32", [1, P], F32)
    p.sb_t("sel16", [16, 16, P], BF16)
    p.sb_t("swap_bf", [P, P], BF16)
    p.sb_t("bones_bf", [P, 2], BF16)
    p.sb_t("bones2_f32", [2, P], F32)
    with nc.sbuf_tensor("iota_a", [P, P], I32) as ia, \
         nc.sbuf_tensor("iota_b", [P, P], I32) as ib, \
         nc.sbuf_tensor("iota_c", [16, 16, P], I32) as ic, \
         nc.sbuf_tensor("iota_d", [16, 16, P], I32) as idd, \
         nc.sbuf_tensor("iota_e", [2, P], I32) as ie, \
         nc.sbuf_tensor("iota_f", [2, P], I32) as iff:
        with nc.Block("consts1") as blk:
            @blk.gpsimd
            def _(g):
                g.iota(ia[:, :], [[1, P]], channel_multiplier=0)
                g.iota(ib[:, :], [[0, P]], channel_multiplier=1)
                g.iota(ic[:, :, :], [[1, 16], [0, P]], channel_multiplier=0)
                g.iota(idd[:, :, :], [[0, 16], [0, P]], channel_multiplier=1)
                g.iota(ie[:, :], [[1, P]], channel_multiplier=0)
                g.iota(iff[:, :], [[0, P]], channel_multiplier=1)
                g.memset(p.sb["ones_col_bf"][:, :], 1.0)
                g.memset(p.sb["ones_col_f32"][:, :], 1.0)
                g.memset(p.sb["ones_row_bf"][:, :], 1.0)
                g.memset(p.sb["ones_row_f32"][:, :], 1.0)
        with nc.Block("consts2") as blk:
            @blk.vector
            def _(v):
                v.tensor_tensor(p.sb["ident_bf"][:, :], ia[:, :], ib[:, :], ALU.is_equal)
                v.tensor_tensor(p.sb["ident_f32"][:, :], ia[:, :], ib[:, :], ALU.is_equal)
                v.tensor_tensor(p.sb["sel16"][:, :, :], ic[:, :, :], idd[:, :, :],
                                ALU.is_equal)
                v.drain()
                v.tensor_scalar(ia[:, :], ia[:, :], 32, None, ALU.bitwise_xor)
                v.drain()
                v.tensor_tensor(p.sb["swap_bf"][:, :], ia[:, :], ib[:, :],
                                ALU.is_equal)
                v.tensor_scalar(p.sb["bones_bf"][:, 1:2], ib[:, 0:1], 64, None,
                                ALU.is_ge)
                v.tensor_scalar(p.sb["bones_bf"][:, 0:1], ib[:, 0:1], 64, None,
                                ALU.is_lt)
                v.tensor_scalar(ie[:, :], ie[:, :], 64, None, ALU.is_ge)
                v.drain()
                v.tensor_tensor(p.sb["bones2_f32"][:, :], ie[:, :], iff[:, :],
                                ALU.is_equal)



class LnFm:
    def __init__(self, p: Prog, st: ExitStack, tag, C):
        nc = p.nc
        self.p, self.C = p, C
        E = st.enter_context
        self.xb = E(nc.sbuf_tensor(f"{tag}_xb", [P, KD, C], BF16))
        self.x2 = E(nc.sbuf_tensor(f"{tag}_x2", [P, KD, C], BF16))
        self.xc = E(nc.sbuf_tensor(f"{tag}_xc", [P, KD, C], F32))
        self.rows = E(nc.sbuf_tensor(f"{tag}_rows", [1, 2 * C], F32))
        self.tmpr = E(nc.sbuf_tensor(f"{tag}_tmpr", [1, 2 * C], F32))
        self.ps_stat = E(nc.psum_tensor(f"{tag}_ps_stat", [1, 2 * C], F32))
        self.ps_bc = E(nc.psum_tensor(f"{tag}_ps_bc", [P, 2 * C], F32))
        self.sa = nc.alloc_semaphore(f"{tag}_sa")
        self.sp = nc.alloc_semaphore(f"{tag}_sp")
        self.sv = nc.alloc_semaphore(f"{tag}_sv")
        self.sq = nc.alloc_semaphore(f"{tag}_sq")

    def act_emit(self, act, i, xc_sb):
        if i > 0:
            act.wait_ge(self.sv, 2 * i)
        act.activation(self.xb[:, :, :], xc_sb, AF.Copy)
        act.activation(self.x2[:, :, :], xc_sb, AF.Square).then_inc(self.sa, 1)
        act.wait_ge(self.sv, 2 * i + 1)
        act.activation(self.rows[0:1, self.C:2 * self.C],
                       self.tmpr[0:1, 0:self.C], AF.Sqrt).then_inc(self.sq, 1)

    def pe_emit(self, te, i, ones_col_bf, ones_row_f32):
        C = self.C
        te.wait_ge(self.sa, i + 1)
        if i > 0:
            te.wait_ge(self.sv, 2 * i - 1)
        for k in range(KD):
            te.matmul(self.ps_stat[0:1, 0:C], ones_col_bf, self.xb[:, k, :],
                      start=(k == 0), stop=(k == KD - 1))
        last = None
        for k in range(KD):
            last = te.matmul(self.ps_stat[0:1, C:2 * C], ones_col_bf,
                             self.x2[:, k, :], start=(k == 0), stop=(k == KD - 1))
        last.then_inc(self.sp, 1)
        te.wait_ge(self.sq, i + 1)
        if i > 0:
            te.wait_ge(self.sv, 2 * i)
        te.matmul(self.ps_bc[:, 0:C], ones_row_f32, self.rows[0:1, 0:C],
                  start=True, stop=True)
        te.matmul(self.ps_bc[:, C:2 * C], ones_row_f32,
                  self.rows[0:1, C:2 * C], start=True, stop=True).then_inc(self.sp, 1)

    def v_emit(self, v, i, xc_sb, out_ap, out2_ap=None, eps=1e-5):
        C = self.C
        v.wait_ge(self.sp, 2 * i + 1)
        v.tensor_scalar(self.rows[0:1, 0:C], self.ps_stat[0:1, 0:C],
                        1.0 / D, None, ALU.mult)
        v.tensor_scalar(self.tmpr[0:1, C:2 * C], self.ps_stat[0:1, C:2 * C],
                        1.0 / D, None, ALU.mult)
        v.tensor_tensor(self.tmpr[0:1, 0:C], self.rows[0:1, 0:C],
                        self.rows[0:1, 0:C], ALU.mult)
        v.drain()
        v.scalar_tensor_tensor(self.tmpr[0:1, C:2 * C], self.tmpr[0:1, C:2 * C],
                               eps, self.tmpr[0:1, 0:C], ALU.add, ALU.subtract)
        v.drain()
        v.reciprocal(self.tmpr[0:1, 0:C],
                     self.tmpr[0:1, C:2 * C]).then_inc(self.sv, 1)
        v.wait_ge(self.sp, 2 * i + 2)
        for k in range(KD):
            v.tensor_tensor(self.xc[:, k, :], xc_sb[:, k, :],
                            self.ps_bc[:, 0:C], ALU.subtract)
        last = None
        for k in range(KD):
            last = v.scalar_tensor_tensor(out_ap[:, k, :], self.xc[:, k, :], 1.0,
                                          self.ps_bc[:, C:2 * C], ALU.mult, ALU.mult)
        if out2_ap is not None:
            for k in range(KD):
                last = v.scalar_tensor_tensor(out2_ap[:, k, :], self.xc[:, k, :],
                                              1.0, self.ps_bc[:, C:2 * C],
                                              ALU.mult, ALU.mult)
        last.then_inc(self.sv, 1)



def emit_weight_prep(p: Prog, st: ExitStack, tag, jobs, block):
    nc = p.nc
    E = st.enter_context
    maxcols = max(j[2] for j in jobs)
    any_int = any(j[1] for j in jobs)
    wrow_b = E(nc.sbuf_tensor(f"{tag}_wrow_b", [P, 2, maxcols], BF16))
    wrow_i = E(nc.sbuf_tensor(f"{tag}_wrow_i", [P, 2, maxcols], I32)) if any_int else None
    tp_ps = BankedPsum(st, nc, f"{tag}_tp_ps", 2, P, P, BF16)
    wl = [nc.alloc_semaphore(f"{tag}_wl0"), nc.alloc_semaphore(f"{tag}_wl1")]
    wc = nc.alloc_semaphore(f"{tag}_wc")
    wt = nc.alloc_semaphore(f"{tag}_wt")
    we = nc.alloc_semaphore(f"{tag}_we")
    ident_bf = p.sb["ident_bf"]

    chunk_of_job = [0]
    for (_, _, cols, _) in jobs:
        chunk_of_job.append(chunk_of_job[-1] + cols // P)

    @block.gpsimd
    def _(g):
        for j, (src, is_int, cols, dst_fn) in enumerate(jobs):
            if j >= 2:
                if is_int:
                    g.wait_ge(wc, j - 1)
                else:
                    g.wait_ge(wt, chunk_of_job[j - 1])
            buf = (wrow_i if is_int else wrow_b)
            g.dma_start(buf[:, j % 2, :cols], src).then_inc(wl[j % 2], 16)

    @block.vector
    def _(v):
        v = EA(v)
        for j, (src, is_int, cols, dst_fn) in enumerate(jobs):
            v.wait_ge(wl[j % 2], 16 * (j // 2 + 1))
            if is_int:
                if j >= 2:
                    v.wait_ge(wt, chunk_of_job[j - 1])
                v.tensor_copy(wrow_b[:, j % 2, :cols],
                              wrow_i[:, j % 2, :cols]).then_inc(wc, 1)
            else:
                v.engine_nop().then_inc(wc, 1)

    @block.tensor
    def _(te):
        tcount = 0
        for j, (src, is_int, cols, dst_fn) in enumerate(jobs):
            te.wait_ge(wc, j + 1)
            for c in range(cols // P):
                if tcount >= 2:
                    te.wait_ge(we, tcount - 1)
                te.transpose(tp_ps[:, (tcount % 2) * P:(tcount % 2 + 1) * P],
                             wrow_b[:, j % 2, c * P:(c + 1) * P],
                             ident_bf[:, :]).then_inc(wt, 1)
                tcount += 1

    @block.scalar
    def _(act):
        act = EA(act)
        tcount = 0
        for j, (src, is_int, cols, dst_fn) in enumerate(jobs):
            for c in range(cols // P):
                act.wait_ge(wt, tcount + 1)
                act.activation(dst_fn(j, c), tp_ps[:, (tcount % 2) * P:(tcount % 2 + 1) * P],
                               AF.Copy).then_inc(we, 1)
                tcount += 1

    return we, chunk_of_job[-1]



def emit_l1_ffn(p: Prog, x_in, x_out, w_gate, w_up, w_down):
    nc, T = p.nc, p.T
    HID = 3072
    NH = HID // P
    C = 256
    NCHK = T // C
    s = 1.0 / 127.0

    with ExitStack() as st:
        E = st.enter_context
        wgT = E(nc.sbuf_tensor("l1f_wgT", [P, KD, HID], BF16))
        wuT = E(nc.sbuf_tensor("l1f_wuT", [P, KD, HID], BF16))
        wdT = E(nc.sbuf_tensor("l1f_wdT", [P, NH, D], BF16))

        jobs = []
        for gi in range(NH):
            jobs.append((w_gate[gi * P:(gi + 1) * P, :], True, D,
                         ("g", gi)))
        for gi in range(NH):
            jobs.append((w_up[gi * P:(gi + 1) * P, :], True, D, ("u", gi)))
        for gi in range(KD):
            jobs.append((w_down[gi * P:(gi + 1) * P, :], True, HID, ("d", gi)))

        def dst_fn(j, c):
            kind, gi = jobs[j][3]
            if kind == "g":
                return wgT[:, c, gi * P:(gi + 1) * P]
            if kind == "u":
                return wuT[:, c, gi * P:(gi + 1) * P]
            return wdT[:, c, gi * P:(gi + 1) * P]

        jobs2 = [(a, b, c, dst_fn) for (a, b, c, _) in jobs]
        with ExitStack() as st_prep:
            with nc.Block("l1f_prep") as blk:
                emit_weight_prep(p, st_prep, "l1f", jobs2, blk)

        xc = E(nc.sbuf_tensor("l1f_xc", [P, KD, C], F32))
        lnb = E(nc.sbuf_tensor("l1f_lnb", [P, KD, C], BF16))
        sg = E(nc.sbuf_tensor("l1f_sg", [P, C], BF16))
        h_fm = E(nc.sbuf_tensor("l1f_h", [P, NH, C], BF16))
        obuf = E(nc.sbuf_tensor("l1f_o", [P, KD, C], F32))
        ln = LnFm(p, st, "l1fln", C)
        ps_g = E(nc.psum_tensor("l1f_ps_g", [P, C], F32))
        ps_u = E(nc.psum_tensor("l1f_ps_u", [P, C], F32))
        ps_d = [E(nc.psum_tensor("l1f_ps_d0", [P, C], F32)),
                E(nc.psum_tensor("l1f_ps_d1", [P, C], F32))]

        s_x = nc.alloc_semaphore("l1f_sx")
        s_mm = nc.alloc_semaphore("l1f_smm")
        s_sil = nc.alloc_semaphore("l1f_ssil")
        s_h = nc.alloc_semaphore("l1f_sh")
        s_dn = nc.alloc_semaphore("l1f_sdn")
        s_o = nc.alloc_semaphore("l1f_so")
        s_st = nc.alloc_semaphore("l1f_sst")

        x_in_v = fmv(x_in)
        x_out_v = fmv(x_out)
        ND = KD

        with nc.Block("l1f_main") as blk:
            @blk.sync
            def _(sy):
                sy.dma_start(xc[:, :, :], x_in_v[:, :, 0:C]).then_inc(s_x, 16)
                for i in range(NCHK):
                    if i + 1 < NCHK:
                        sy.wait_ge(s_o, (i + 1) * ND)
                        sy.dma_start(xc[:, :, :],
                                     x_in_v[:, :, (i + 1) * C:(i + 2) * C]
                                     ).then_inc(s_x, 16)
                    for d in range(ND):
                        sy.wait_ge(s_o, i * ND + d + 1)
                        sy.dma_start(x_out_v[:, d, i * C:(i + 1) * C],
                                     obuf[:, d, :]).then_inc(s_st, 16)
                sy.wait_ge(s_st, 16 * NCHK * ND)

            @blk.scalar
            def _(act):
                act = EA(act)
                for i in range(NCHK):
                    act.wait_ge(s_x, 16 * (i + 1))
                    ln.act_emit(act, i, xc[:, :, :])
                    for m in range(NH):
                        act.wait_ge(s_mm, (i * NH + m) * 2 + 1)
                        if i * NH + m > 0:
                            act.wait_ge(s_h, i * NH + m)
                        act.activation(sg[:, :], ps_g[:, :], AF.Silu,
                                       scale=s).then_inc(s_sil, 1)

            @blk.tensor
            def _(te):
                for i in range(NCHK):
                    ln.pe_emit(te, i, p.sb["ones_col_bf"][:, :],
                               p.sb["ones_row_f32"][:, :])
                    te.wait_ge(ln.sv, 2 * i + 2)
                    for m in range(NH):
                        if i * NH + m > 0:
                            te.wait_ge(s_sil, i * NH + m)
                        last = None
                        for k in range(KD):
                            last = te.matmul(ps_g[:, :], wgT[:, k, m * P:(m + 1) * P],
                                             lnb[:, k, :], start=(k == 0),
                                             stop=(k == KD - 1))
                        last.then_inc(s_mm, 1)
                        if i * NH + m > 0:
                            te.wait_ge(s_h, i * NH + m)
                        last = None
                        for k in range(KD):
                            last = te.matmul(ps_u[:, :], wuT[:, k, m * P:(m + 1) * P],
                                             lnb[:, k, :], start=(k == 0),
                                             stop=(k == KD - 1))
                        last.then_inc(s_mm, 1)
                    te.wait_ge(s_h, (i + 1) * NH)
                    for d in range(ND):
                        if i * ND + d >= 2:
                            te.wait_ge(s_o, i * ND + d - 1)
                        last = None
                        for k in range(NH):
                            last = te.matmul(ps_d[d % 2][:, :],
                                             wdT[:, k, d * P:(d + 1) * P],
                                             h_fm[:, k, :], start=(k == 0),
                                             stop=(k == NH - 1))
                        last.then_inc(s_dn, 1)

            @blk.vector
            def _(v):
                v = EA(v)
                for i in range(NCHK):
                    if i > 0:
                        v.wait_ge(s_mm, 2 * NH * i)
                        v.wait_ge(s_dn, i * ND)
                    ln.v_emit(v, i, xc[:, :, :], lnb)
                    for m in range(NH):
                        v.wait_ge(s_sil, i * NH + m + 1)
                        v.wait_ge(s_mm, (i * NH + m) * 2 + 2)
                        v.scalar_tensor_tensor(h_fm[:, m, :], ps_u[:, :], s,
                                               sg[:, :], ALU.mult,
                                               ALU.mult).then_inc(s_h, 1)
                    for d in range(ND):
                        v.wait_ge(s_dn, i * ND + d + 1)
                        if i > 0:
                            v.wait_ge(s_st, 16 * i * ND)
                        v.scalar_tensor_tensor(obuf[:, d, :], ps_d[d % 2][:, :],
                                               s, xc[:, d, :], ALU.mult,
                                               ALU.add).then_inc(s_o, 1)



_L0A_SKIP = set()


def emit_l0_mamba(p: Prog, x_in, x_out, w_x, w_dt, b_dt, log_a, d_param):
    nc, T = p.nc, p.T
    C = 256
    NCHK = T // C
    NST = 16

    with ExitStack() as st:
        E = st.enter_context
        wxT = E(nc.sbuf_tensor("l0m_wxT", [P, KD, 96], BF16))
        wdtT = E(nc.sbuf_tensor("l0m_wdtT", [64, D], BF16))
        bdt_c = E(nc.sbuf_tensor("l0m_bdt", [P, KD], F32))
        dpar_c = E(nc.sbuf_tensor("l0m_dpar", [P, KD], F32))
        a_neg = E(nc.sbuf_tensor("l0m_aneg", [P, KD, NST], F32))
        state = E(nc.sbuf_tensor("l0m_state", [P, KD, NST], F32))

        with nc.sbuf_tensor("l0m_wxn", [96, D], BF16) as wxn, \
             nc.sbuf_tensor("l0m_wdtn", [P, KD, 64], BF16) as wdtn, \
             ExitStack() as _tpst:
            tp_ps = BankedPsum(_tpst, nc, "l0m_tp", 2, P, P, BF16)
            cs = nc.alloc_semaphore("l0m_cs")
            ce = nc.alloc_semaphore("l0m_ce")
            cd = nc.alloc_semaphore("l0m_cd")
            ch = nc.alloc_semaphore("l0m_ch")
            with nc.Block("l0m_prep1") as blk:
                @blk.gpsimd
                def _(g):
                    g.dma_start(wxn[:, :], w_x[:, :]).then_inc(cd, 16)
                    g.dma_start(wdtn[:, :, :],
                                w_dt[:, :].rearrange("(k p) r -> p k r", p=P)
                                ).then_inc(cd, 16)
                    g.wait_ge(cd, 32)
                @blk.sync
                def _(sy):
                    for k in range(KD):
                        sy.dma_start(bdt_c[:, k:k + 1],
                                     b_dt[k * P:(k + 1) * P]).then_inc(ch, 16)
                        sy.dma_start(dpar_c[:, k:k + 1],
                                     d_param[k * P:(k + 1) * P]).then_inc(ch, 16)
                    sy.dma_start(a_neg[:, :, :],
                                 log_a[:, :].rearrange("(k p) n -> p k n", p=P)
                                 ).then_inc(ch, 16)
                    sy.wait_ge(ch, 16 * (2 * KD + 1))
            with nc.Block("l0m_prep2") as blk:
                @blk.tensor
                def _(te):
                    cnt = 0
                    for k in range(KD):
                        if cnt >= 2:
                            te.wait_ge(ce, cnt - 1)
                        te.transpose(tp_ps[:, (cnt % 2) * P:(cnt % 2) * P + 96],
                                     wxn[:, k * P:(k + 1) * P],
                                     p.sb["ident_bf"][0:96, 0:96]).then_inc(cs, 1)
                        cnt += 1
                    for k in range(KD):
                        te.wait_ge(ce, cnt - 1)
                        te.transpose(tp_ps[0:64, (cnt % 2) * P:(cnt % 2) * P + P],
                                     wdtn[:, k, :], p.sb["ident_bf"][:, :]
                                     ).then_inc(cs, 1)
                        cnt += 1
                @blk.scalar
                def _(act):
                    act = EA(act)
                    cnt = 0
                    for k in range(KD):
                        act.wait_ge(cs, cnt + 1)
                        act.activation(wxT[:, k, :],
                                       tp_ps[:, (cnt % 2) * P:(cnt % 2) * P + 96],
                                       AF.Copy).then_inc(ce, 1)
                        cnt += 1
                    for k in range(KD):
                        act.wait_ge(cs, cnt + 1)
                        act.activation(wdtT[:, k * P:(k + 1) * P],
                                       tp_ps[0:64, (cnt % 2) * P:(cnt % 2) * P + P],
                                       AF.Copy).then_inc(ce, 1)
                        cnt += 1
                    act.activation(a_neg[:, :, :], a_neg[:, :, :], AF.Exp)
                @blk.vector
                def _(v):
                    v = EA(v)
                    v.memset(state[:, :, :], 0.0)

            with nc.Block("l0m_prep3") as blk:
                @blk.vector
                def _(v):
                    v.tensor_scalar(a_neg[:, :, :], a_neg[:, :, :], -1.0, None,
                                    ALU.mult)

        xc = E(nc.sbuf_tensor("l0m_xc", [P, KD, C], F32))
        lnb = E(nc.sbuf_tensor("l0m_lnb", [P, KD, C], BF16))
        u32 = E(nc.sbuf_tensor("l0m_u32", [P, KD, C], F32))
        ddr = E(nc.sbuf_tensor("l0m_ddr", [64, C], BF16))
        b_sb = E(nc.sbuf_tensor("l0m_bsb", [16, C], F32))
        c_sb = E(nc.sbuf_tensor("l0m_csb", [16, C], F32))
        b_bf = E(nc.sbuf_tensor("l0m_bbf", [16, C], BF16))
        c_bf = E(nc.sbuf_tensor("l0m_cbf", [16, C], BF16))
        drw = E(nc.sbuf_tensor("l0m_drw", [P, KD, C], F32))
        dsb = E(nc.sbuf_tensor("l0m_dsb", [P, KD, C], F32))
        l1b = E(nc.sbuf_tensor("l0m_l1b", [P, KD, C], F32))
        wfm = E(nc.sbuf_tensor("l0m_wfm", [P, KD, C], F32))
        yac = E(nc.sbuf_tensor("l0m_yac", [P, KD, C], F32))
        abar = E(nc.sbuf_tensor("l0m_abar", [P, 2, C], F32))
        belb = E(nc.sbuf_tensor("l0m_bel", [P, C], F32))
        hsb = E(nc.sbuf_tensor("l0m_h", [P, C], F32))
        t2b = E(nc.sbuf_tensor("l0m_t2", [P, C], F32))
        obuf = E(nc.sbuf_tensor("l0m_o", [P, KD, C], F32))
        ln = LnFm(p, st, "l0mln", C)
        psA = E(nc.psum_tensor("l0m_psA", [64, C], F32))
        psBC = E(nc.psum_tensor("l0m_psBC", [16, 1024], F32))
        psD = E(nc.psum_tensor("l0m_psD", [P, C], F32))
        psBb = E(nc.psum_tensor("l0m_psBb", [P, C], F32))
        psCb = E(nc.psum_tensor("l0m_psCb", [P, C], F32))

        s_x = nc.alloc_semaphore("l0m_sx")
        s_pe1 = nc.alloc_semaphore("l0m_spe1")
        s_v1 = nc.alloc_semaphore("l0m_sv1")
        s_pd = nc.alloc_semaphore("l0m_spd")
        s_vd = nc.alloc_semaphore("l0m_svd")
        s_sp = nc.alloc_semaphore("l0m_ssp")
        s_bc = nc.alloc_semaphore("l0m_sbc")
        s_ab = nc.alloc_semaphore("l0m_sab")
        s_sc = nc.alloc_semaphore("l0m_ssc")
        s_st = nc.alloc_semaphore("l0m_sst")
        s_ep = nc.alloc_semaphore("l0m_sep")
        x_in_v = fmv(x_in)
        x_out_v = fmv(x_out)

        for i in range(NCHK):
            with nc.Block(f"l0m_ld{i}") as blk:
                @blk.sync
                def _(sy, i=i):
                    sy.dma_start(xc[:, :, :],
                                 x_in_v[:, :, i * C:(i + 1) * C]).then_inc(s_x, 16)
                    sy.wait_ge(s_x, 16 * (i + 1))

            with nc.Block(f"l0m_m2_{i}") as blk:
                @blk.scalar
                def _(act, i=i):
                    act = EA(act)
                    ln.act_emit(act, i, xc[:, :, :])
                    for k in range(KD):
                        act.wait_ge(s_vd, i * KD + k + 1)
                        act.activation(abar[:, 0, :], drw[:, k, :], AF.Abs,
                                       bias=bdt_c[:, k:k + 1])
                        act.activation(abar[:, 0, :], abar[:, 0, :], AF.Exp,
                                       scale=-1.0)
                        act.activation(l1b[:, k, :], abar[:, 0, :], AF.Ln,
                                       bias=1.0).then_inc(s_sp, 1)

                @blk.tensor
                def _(te, i=i):
                    te2 = te
                    ln.pe_emit(te2, i, p.sb["ones_col_bf"][:, :],
                               p.sb["ones_row_f32"][:, :])
                    te.wait_ge(ln.sv, 2 * i + 2)
                    last = None
                    for k in range(KD):
                        last = te.matmul(psA[:, :], wxT[:, k, 0:64], lnb[:, k, :],
                                         start=(k == 0), stop=(k == KD - 1))
                    for k in range(KD):
                        last = te.matmul(psBC[:, 0:C], wxT[:, k, 64:80],
                                         lnb[:, k, :], start=(k == 0),
                                         stop=(k == KD - 1))
                    for k in range(KD):
                        last = te.matmul(psBC[:, 512:512 + C], wxT[:, k, 80:96],
                                         lnb[:, k, :], start=(k == 0),
                                         stop=(k == KD - 1))
                    last.then_inc(s_pe1, 1)
                    te.wait_ge(s_v1, i + 1)
                    for k in range(KD):
                        if i * KD + k > 0:
                            te.wait_ge(s_vd, i * KD + k)
                        te.matmul(psD[:, :], wdtT[:, k * P:(k + 1) * P],
                                  ddr[:, :], start=True,
                                  stop=True).then_inc(s_pd, 1)

                @blk.vector
                def _(v, i=i):
                    v = EA(v)
                    ln.v_emit(v, i, xc[:, :, :], lnb, out2_ap=u32)
                    v.wait_ge(s_pe1, i + 1)
                    v.tensor_copy(ddr[:, :], psA[:, :])
                    v.tensor_copy(b_sb[:, :], psBC[:, 0:C])
                    v.tensor_copy(c_sb[:, :], psBC[:, 512:512 + C])
                    v.drain()
                    v.tensor_copy(b_bf[:, :], b_sb[:, :])
                    v.tensor_copy(c_bf[:, :], c_sb[:, :]).then_inc(s_v1, 1)
                    for k in range(KD):
                        v.wait_ge(s_pd, i * KD + k + 1)
                        v.tensor_copy(drw[:, k, :], psD[:, :])
                        v.tensor_scalar(dsb[:, k, :], drw[:, k, :],
                                        bdt_c[:, k:k + 1], 0.0, ALU.add,
                                        ALU.max).then_inc(s_vd, 1)
                    for k in range(KD):
                        v.wait_ge(s_sp, i * KD + k + 1)
                        v.tensor_tensor(dsb[:, k, :], dsb[:, k, :], l1b[:, k, :],
                                        ALU.add)
                        v.tensor_tensor(wfm[:, k, :], dsb[:, k, :], u32[:, k, :],
                                        ALU.mult)

            with nc.Block(f"l0m_sc{i}") as blk:
                @blk.tensor
                def _(te, i=i):
                    for n in range(NST):
                        jn = i * NST + n
                        if n > 0:
                            te.wait_ge(s_sc, (jn) * KD)
                        te.matmul(psBb[:, :], p.sb["sel16"][:, n, :],
                                  b_bf[:, :], start=True, stop=True)
                        te.matmul(psCb[:, :], p.sb["sel16"][:, n, :],
                                  c_bf[:, :], start=True,
                                  stop=True).then_inc(s_bc, 1)

                @blk.scalar
                def _(act, i=i):
                    act = EA(act)
                    for n in range(NST):
                        for k in range(KD):
                            j = (i * NST + n) * KD + k
                            if j >= 2 + i * NST * KD:
                                act.wait_ge(s_sc, j - 1 - i * NST * KD
                                            + i * NST * KD)
                            act.activation(abar[:, j % 2, :], dsb[:, k, :],
                                           AF.Exp,
                                           scale=a_neg[:, k, n:n + 1]
                                           ).then_inc(s_ab, 1)

                @blk.vector
                def _(v, i=i):
                    v = EA(v)
                    for n in range(NST):
                        jn = i * NST + n
                        v.wait_ge(s_bc, jn + 1)
                        for k in range(KD):
                            j = jn * KD + k
                            v.wait_ge(s_ab, j + 1)
                            v.tensor_tensor(belb[:, :], wfm[:, k, :], psBb[:, :],
                                            ALU.mult)
                            v.tensor_tensor_scan(hsb[:, :], abar[:, j % 2, :],
                                                 belb[:, :],
                                                 state[:, k, n:n + 1],
                                                 ALU.mult, ALU.add)
                            v.tensor_copy(state[:, k, n:n + 1], hsb[:, C - 1:C])
                            if n == 0:
                                v.tensor_tensor(yac[:, k, :], hsb[:, :],
                                                psCb[:, :],
                                                ALU.mult).then_inc(s_sc, 1)
                            else:
                                v.tensor_tensor(t2b[:, :], hsb[:, :], psCb[:, :],
                                                ALU.mult)
                                v.tensor_tensor(yac[:, k, :], yac[:, k, :],
                                                t2b[:, :],
                                                ALU.add).then_inc(s_sc, 1)

            with nc.Block(f"l0m_ep{i}") as blk:
                @blk.vector
                def _(v, i=i):
                    v = EA(v)
                    last = None
                    for k in range(KD):
                        v.scalar_tensor_tensor(obuf[:, k, :], u32[:, k, :],
                                               dpar_c[:, k:k + 1], yac[:, k, :],
                                               ALU.mult, ALU.add)
                        last = v.tensor_tensor(obuf[:, k, :], obuf[:, k, :],
                                               xc[:, k, :], ALU.add)
                    last.then_inc(s_ep, 1)
                @blk.sync
                def _(sy, i=i):
                    sy.wait_ge(s_ep, i + 1)
                    sy.dma_start(x_out_v[:, :, i * C:(i + 1) * C],
                                 obuf[:, :, :]).then_inc(s_st, 16)
                    sy.wait_ge(s_st, 16 * (i + 1))



def emit_l0_moe(p: Prog, x_in, x_out, router_w, fc_w, proj_w, xe_hbm, ye_hbm):
    nc, T = p.nc, p.T
    C = 512
    NCHK = T // C
    NTT = C // P
    CAPE = 768
    NSLOT = 8 * CAPE
    HD = 4096
    NHT = HD // P
    NES = CAPE // P
    NMB = HD // 512

    with ExitStack() as st:
        E = st.enter_context
        rwT = E(nc.sbuf_tensor("l0f_rwT", [P, KD, 8], F32))
        dest_all = E(nc.sbuf_tensor("l0f_dest", [P, T // P], I32))
        iota_e8 = E(nc.sbuf_tensor("l0f_iotae", [P, 8], F32))
        ps_tb = BankedPsum(st, nc, "l0f_pstb", 2, P, P, BF16)

        with ExitStack() as stp, nc.sbuf_tensor("l0f_rwn", [8, 1024], F32) as rwn, \
             nc.sbuf_tensor("l0f_ioti", [P, 8], I32) as ioti, \
             nc.sbuf_tensor("l0f_zrow", [P, 1024], BF16) as zrow, \
             nc.sbuf_tensor("l0f_zrowf", [1, 1024], F32) as zrowf, \
             ExitStack() as _tpst2:
            tpf = BankedPsum(_tpst2, nc, "l0f_tpf", 2, P, 8, F32)
            cs = nc.alloc_semaphore("l0f_cs")
            ce = nc.alloc_semaphore("l0f_ce")
            ch = nc.alloc_semaphore("l0f_ch")
            with nc.Block("l0f_prep1") as blk:
                @blk.sync
                def _(sy):
                    sy.dma_start(rwn[:, :], router_w[:, :]).then_inc(ch, 16)
                    sy.wait_ge(ch, 16)
                @blk.gpsimd
                def _(g):
                    g.iota(ioti[:, :], [[1, 8]], channel_multiplier=0)
                @blk.vector
                def _(v):
                    v = EA(v)
                    v.memset(zrow[:, :], 0.0)
                    v.memset(zrowf[:, :], 0.0)
            with nc.Block("l0f_prep2") as blk:
                @blk.tensor
                def _(te):
                    for k in range(KD):
                        if k >= 2:
                            te.wait_ge(ce, k - 1)
                        te.transpose(tpf[:, (k % 2) * 8:(k % 2) * 8 + 8],
                                     rwn[:, k * P:(k + 1) * P],
                                     p.sb["ident_f32"][0:8, 0:8]).then_inc(cs, 1)
                @blk.scalar
                def _(act):
                    act = EA(act)
                    for k in range(KD):
                        act.wait_ge(cs, k + 1)
                        act.activation(rwT[:, k, :],
                                       tpf[:, (k % 2) * 8:(k % 2) * 8 + 8],
                                       AF.Copy).then_inc(ce, 1)
                @blk.vector
                def _(v):
                    v.tensor_copy(iota_e8[:, :], ioti[:, :])
                @blk.sync
                def _(sy):
                    nz = (NSLOT + 1 + P - 1) // P
                    for z in range(nz):
                        r0 = z * P
                        r1 = min(r0 + P, NSLOT + 1)
                        sy.dma_start(xe_hbm[r0:r1, :],
                                     zrow[0:r1 - r0, :]).then_inc(ch, 16)
                    sy.dma_start(ye_hbm[NSLOT:NSLOT + 1, :],
                                 zrowf[0:1, :]).then_inc(ch, 16)
                    sy.wait_ge(ch, 32 + 16 * nz)

        s_x = nc.alloc_semaphore("l0f_sx")
        s_lg = nc.alloc_semaphore("l0f_slg")
        s_ev = nc.alloc_semaphore("l0f_sev")
        s_tp = nc.alloc_semaphore("l0f_stp")
        s_ab = nc.alloc_semaphore("l0f_sab")
        s_sc = nc.alloc_semaphore("l0f_ssc")
        x_in_v = fmv(x_in)
        x_out_v = fmv(x_out)
        with ExitStack() as stR:
            ER = stR.enter_context
            xc = ER(nc.sbuf_tensor("l0f_xc", [P, KD, C], F32))
            lnb = ER(nc.sbuf_tensor("l0f_lnb", [P, KD, C], BF16))
            ln32 = ER(nc.sbuf_tensor("l0f_ln32", [P, KD, C], F32))
            lntm = ER(nc.sbuf_tensor("l0f_lntm", [P, NTT, 1024], BF16))
            lg_sb = ER(nc.sbuf_tensor("l0f_lg", [8, C], F32))
            lgtm = ER(nc.sbuf_tensor("l0f_lgtm", [P, NTT, 8], F32))
            m8 = ER(nc.sbuf_tensor("l0f_m8", [P, 1], F32))
            masktm = ER(nc.sbuf_tensor("l0f_masktm", [P, NTT, 8], F32))
            maskfm = ER(nc.sbuf_tensor("l0f_maskfm", [8, C], F32))
            rankfm = ER(nc.sbuf_tensor("l0f_rankfm", [8, C], F32))
            ranktm = ER(nc.sbuf_tensor("l0f_ranktm", [P, NTT, 8], F32))
            tcol = ER(nc.sbuf_tensor("l0f_tcol", [P, 4], F32))
            ones8 = ER(nc.sbuf_tensor("l0f_ones8", [8, C], F32))
            carry = ER(nc.sbuf_tensor("l0f_carry", [8, 1], F32))
            ln = LnFm(p, stR, "l0fln", C)
            pbR = BankedPsum(stR, nc, "l0f_pbr", 2, P, 512, F32)
            ps_lg = pbR.t[0][0:8, 0:C]
            with nc.Block("l0f_rinit") as blk:
                @blk.vector
                def _(v):
                    v = EA(v)
                    v.memset(ones8[:, :], 1.0)
                    v.memset(carry[:, :], 0.0)

            for i in range(NCHK):
                tp0, ev0, ab0 = i * (NTT * KD + 3 * NTT), i * (3 * NTT + 2), \
                    i * (NTT * KD)
                with nc.Block(f"l0f_r{i}") as blk:
                    @blk.sync
                    def _(sy, i=i):
                        sy.dma_start(xc[:, :, :],
                                     x_in_v[:, :, i * C:(i + 1) * C]
                                     ).then_inc(s_x, 16)
                        sy.wait_ge(s_x, 16 * (i + 1))
                with nc.Block(f"l0f_r2_{i}") as blk:
                    @blk.scalar
                    def _(act, i=i, tp0=tp0, ab0=ab0):
                        act = EA(act)
                        ln.act_emit(act, i, xc[:, :, :])
                        for j in range(NTT * KD):
                            tt, k = divmod(j, KD)
                            act.wait_ge(s_tp, tp0 + j + 1)
                            act.activation(lntm[:, tt, k * P:(k + 1) * P],
                                           ps_tb[:, (j % 2) * P:(j % 2 + 1) * P],
                                           AF.Copy).then_inc(s_ab, 1)

                    @blk.tensor
                    def _(te, i=i, tp0=tp0, ev0=ev0, ab0=ab0):
                        ln.pe_emit(te, i, p.sb["ones_col_bf"][:, :],
                                   p.sb["ones_row_f32"][:, :])
                        te.wait_ge(ln.sv, 2 * i + 2)
                        last = None
                        for k in range(KD):
                            last = te.matmul(ps_lg[:, :], rwT[:, k, :],
                                             ln32[:, k, :], start=(k == 0),
                                             stop=(k == KD - 1))
                        last.then_inc(s_lg, 1)
                        for j in range(NTT * KD):
                            tt, k = divmod(j, KD)
                            if j >= 2:
                                te.wait_ge(s_ab, ab0 + j - 1)
                            te.transpose(ps_tb[:, (j % 2) * P:(j % 2 + 1) * P],
                                         lnb[:, k, tt * P:(tt + 1) * P],
                                         p.sb["ident_bf"][:, :]).then_inc(s_tp, 1)
                        te.wait_ge(s_ev, ev0 + 1)
                        for tt in range(NTT):
                            if tt >= 2:
                                te.wait_ge(s_ev, ev0 + tt)
                            te.transpose(pbR.t[tt % 2][0:P, 0:8],
                                         lg_sb[:, tt * P:(tt + 1) * P],
                                         p.sb["ident_f32"][0:8, 0:8]
                                         ).then_inc(s_tp, 1)
                        te.wait_ge(s_ev, ev0 + 1 + NTT)
                        for tt in range(NTT):
                            if tt >= 2:
                                te.wait_ge(s_ev, ev0 + NTT + tt)
                            te.transpose(pbR.t[tt % 2][0:8, 0:P],
                                         masktm[:, tt, :],
                                         p.sb["ident_f32"][:, :]).then_inc(s_tp, 1)
                        te.wait_ge(s_ev, ev0 + 2 * NTT + 2)
                        for tt in range(NTT):
                            if tt >= 2:
                                te.wait_ge(s_ev, ev0 + 2 * NTT + tt + 1)
                            te.transpose(pbR.t[tt % 2][0:P, 0:8],
                                         rankfm[:, tt * P:(tt + 1) * P],
                                         p.sb["ident_f32"][0:8, 0:8]
                                         ).then_inc(s_tp, 1)

                    @blk.vector
                    def _(v, i=i, tp0=tp0, ev0=ev0):
                        v = EA(v)
                        ln.v_emit(v, i, xc[:, :, :], lnb, out2_ap=ln32)
                        v.wait_ge(s_lg, i + 1)
                        v.tensor_copy(lg_sb[:, :], ps_lg[:, :]).then_inc(s_ev, 1)
                        for tt in range(NTT):
                            v.wait_ge(s_tp, tp0 + NTT * KD + tt + 1)
                            v.tensor_copy(lgtm[:, tt, :],
                                          pbR.t[tt % 2][0:P, 0:8])
                            v.reduce_max(m8[:, :], lgtm[:, tt, :], axis=AX.X)
                            v.tensor_scalar(masktm[:, tt, :], lgtm[:, tt, :],
                                            m8[:, :], None,
                                            ALU.is_ge).then_inc(s_ev, 1)
                        for tt in range(NTT):
                            v.wait_ge(s_tp, tp0 + NTT * KD + NTT + tt + 1)
                            v.tensor_copy(maskfm[:, tt * P:(tt + 1) * P],
                                          pbR.t[tt % 2][0:8, 0:P]
                                          ).then_inc(s_ev, 1)
                        v.tensor_tensor_scan(rankfm[:, :], ones8[:, :],
                                             maskfm[:, :], carry[:, :],
                                             ALU.mult, ALU.add)
                        v.tensor_copy(carry[:, :], rankfm[:, C - 1:C])
                        v.tensor_tensor(rankfm[:, :], rankfm[:, :], maskfm[:, :],
                                        ALU.subtract).then_inc(s_ev, 1)
                        for tt in range(NTT):
                            v.wait_ge(s_tp, tp0 + NTT * KD + 2 * NTT + tt + 1)
                            v.tensor_copy(ranktm[:, tt, :],
                                          pbR.t[tt % 2][0:P, 0:8])
                            v.tensor_tensor(ranktm[:, tt, :], ranktm[:, tt, :],
                                            masktm[:, tt, :], ALU.mult)
                            v.reduce_sum(tcol[:, 0:1], ranktm[:, tt, :], axis=AX.X)
                            v.tensor_tensor(lgtm[:, tt, :], masktm[:, tt, :],
                                            iota_e8[:, :], ALU.mult)
                            v.reduce_sum(tcol[:, 1:2], lgtm[:, tt, :], axis=AX.X)
                            v.scalar_tensor_tensor(tcol[:, 2:3], tcol[:, 1:2],
                                                   float(CAPE), tcol[:, 0:1],
                                                   ALU.mult, ALU.add)
                            v.tensor_scalar(tcol[:, 3:4], tcol[:, 2:3],
                                            float(NSLOT), None, ALU.min)
                            v.tensor_copy(dest_all[:, i * NTT + tt:
                                                   i * NTT + tt + 1],
                                          tcol[:, 3:4]).then_inc(s_ev, 1)

                    @blk.gpsimd
                    def _(g, i=i, ev0=ev0, ab0=ab0):
                        for tt in range(NTT):
                            g.wait_ge(s_ev, ev0 + 2 * NTT + 2 + tt + 1)
                            g.wait_ge(s_ab, ab0 + (tt + 1) * KD)
                            g.indirect_dma_start(
                                xe_hbm[:, :],
                                bass.IndirectOffsetOnAxis(
                                    dest_all[:, i * NTT + tt:i * NTT + tt + 1],
                                    0),
                                lntm[:, tt, :], None).then_inc(s_sc, 16)
                        g.wait_ge(s_sc, 16 * NTT * (i + 1))

        s_wp = nc.alloc_semaphore("l0f_swp")
        s_wx = nc.alloc_semaphore("l0f_swx")
        s_wf = [nc.alloc_semaphore("l0f_swf0"), nc.alloc_semaphore("l0f_swf1")]
        s_t2e = nc.alloc_semaphore("l0f_st2")
        s_e2 = nc.alloc_semaphore("l0f_se2")
        s_hm = nc.alloc_semaphore("l0f_shm")
        s_hv = nc.alloc_semaphore("l0f_shv")
        s_ym = nc.alloc_semaphore("l0f_sym")
        s_ye = nc.alloc_semaphore("l0f_sye")
        s_ys = nc.alloc_semaphore("l0f_sys")
        N_TPE = KD * NHT + NES * KD
        with ExitStack() as stE:
            EE = stE.enter_context
            xet = EE(nc.sbuf_tensor("l0f_xet", [P, 1024], BF16))
            xefm = EE(nc.sbuf_tensor("l0f_xefm", [P, KD, CAPE], BF16))
            fcb = EE(nc.sbuf_tensor("l0f_fcb", [P, 2, KD, 512], BF16))
            prn = EE(nc.sbuf_tensor("l0f_prn", [P, HD], BF16))
            prT = EE(nc.sbuf_tensor("l0f_prT", [P, NHT, 1024], BF16))
            relb = EE(nc.sbuf_tensor("l0f_relb", [P, 512], BF16))
            hfm = EE(nc.sbuf_tensor("l0f_hfm", [P, NHT, CAPE], BF16))
            yet = EE(nc.sbuf_tensor("l0f_yet", [P, 1024], F32))
            ps_h = [EE(nc.psum_tensor("l0f_psh0", [P, 512], F32)),
                    EE(nc.psum_tensor("l0f_psh1", [P, 512], F32))]
            ps_y = [EE(nc.psum_tensor("l0f_psy0", [P, 512], F32)),
                    EE(nc.psum_tensor("l0f_psy1", [P, 512], F32))]
            N_E2 = N_TPE + 2 * NHT
            for e in range(8):
                t2_0 = e * N_TPE
                e2_0 = e * N_E2
                hm_0 = e * (NMB * 8)
                hv_0 = e * (NHT * 2)
                ym_0 = e * (NES * 2)
                ye_0 = e * (NES * 2)
                ys_0 = e * (NES * 16)
                wp_0, wx_0 = e * 16 * KD, e * 16 * NES
                wf_0 = [e * 16 * (NMB - NMB // 2), e * 16 * (NMB // 2)]
                with nc.Block(f"l0f_e{e}") as blk:
                    @blk.gpsimd
                    def _(g, e=e, t2_0=t2_0, hm_0=hm_0, wp_0=wp_0, wx_0=wx_0,
                          wf_0=wf_0):
                        for rg in range(KD):
                            g.dma_start(prn[:, :],
                                        proj_w[e, rg * P:(rg + 1) * P, :]
                                        ).then_inc(s_wp, 16)
                            g.wait_ge(s_t2e, t2_0 + (rg + 1) * NHT)
                        for stt in range(NES):
                            g.dma_start(xet[:, :],
                                        xe_hbm[e * CAPE + stt * P:
                                               e * CAPE + (stt + 1) * P, :]
                                        ).then_inc(s_wx, 16)
                            g.wait_ge(s_t2e, t2_0 + KD * NHT + (stt + 1) * KD)
                        for mb in range(NMB):
                            if mb >= 2:
                                g.wait_ge(s_hm, hm_0 + (mb - 1) * 8)
                            g.dma_start(fcb[:, mb % 2, :, :],
                                        fc_w[e, :, mb * 512:(mb + 1) * 512]
                                        .rearrange("(k p) h -> p k h", p=P)
                                        ).then_inc(s_wf[mb % 2], 16)
                        g.wait_ge(s_wf[0], wf_0[0] + 16 * (NMB - NMB // 2))
                        g.wait_ge(s_wf[1], wf_0[1] + 16 * (NMB // 2))

                    @blk.tensor
                    def _(te, e=e, t2_0=t2_0, e2_0=e2_0, hm_0=hm_0, hv_0=hv_0,
                          ym_0=ym_0, ye_0=ye_0, wp_0=wp_0, wx_0=wx_0,
                          wf_0=wf_0):
                        tpn = t2_0
                        for rg in range(KD):
                            te.wait_ge(s_wp, wp_0 + 16 * (rg + 1))
                            for c in range(NHT):
                                if tpn - t2_0 >= 2:
                                    te.wait_ge(s_e2, e2_0 + (tpn - t2_0) - 1)
                                te.transpose(ps_tb[:, (tpn % 2) * P:
                                                   (tpn % 2 + 1) * P],
                                             prn[:, c * P:(c + 1) * P],
                                             p.sb["ident_bf"][:, :]
                                             ).then_inc(s_t2e, 1)
                                tpn += 1
                        for stt in range(NES):
                            te.wait_ge(s_wx, wx_0 + 16 * (stt + 1))
                            for k in range(KD):
                                if tpn - t2_0 >= 2:
                                    te.wait_ge(s_e2, e2_0 + (tpn - t2_0) - 1)
                                te.transpose(ps_tb[:, (tpn % 2) * P:
                                                   (tpn % 2 + 1) * P],
                                             xet[:, k * P:(k + 1) * P],
                                             p.sb["ident_bf"][:, :]
                                             ).then_inc(s_t2e, 1)
                                tpn += 1
                        te.wait_ge(s_e2, e2_0 + N_TPE)
                        for mb in range(NMB):
                            te.wait_ge(s_wf[mb % 2], wf_0[mb % 2]
                                       + 16 * (mb // 2 + 1))
                            for mm in range(4):
                                m = mb * 4 + mm
                                for nch, (n0, n1) in enumerate(((0, 512),
                                                               (512, CAPE))):
                                    j = m * 2 + nch
                                    if j >= 2:
                                        te.wait_ge(s_hv, hv_0 + j - 1)
                                    last = None
                                    for k in range(KD):
                                        last = te.matmul(
                                            ps_h[j % 2][:, 0:n1 - n0],
                                            fcb[:, mb % 2, k,
                                                mm * P:(mm + 1) * P],
                                            xefm[:, k, n0:n1], start=(k == 0),
                                            stop=(k == KD - 1))
                                    last.then_inc(s_hm, 1)
                        te.wait_ge(s_hv, hv_0 + NHT * 2)
                        for stt in range(NES):
                            for nch in range(2):
                                j = stt * 2 + nch
                                if j >= 2:
                                    te.wait_ge(s_ye, ye_0 + j - 1)
                                last = None
                                for k in range(NHT):
                                    last = te.matmul(
                                        ps_y[j % 2][:, :],
                                        hfm[:, k, stt * P:(stt + 1) * P],
                                        prT[:, k, nch * 512:(nch + 1) * 512],
                                        start=(k == 0), stop=(k == NHT - 1))
                                last.then_inc(s_ym, 1)

                    @blk.scalar
                    def _(act, e=e, t2_0=t2_0, e2_0=e2_0, hm_0=hm_0, hv_0=hv_0):
                        act = EA(act)
                        for j in range(KD * NHT):
                            rg, c = divmod(j, NHT)
                            act.wait_ge(s_t2e, t2_0 + j + 1)
                            act.activation(prT[:, c, rg * P:(rg + 1) * P],
                                           ps_tb[:, (j % 2) * P:(j % 2 + 1) * P],
                                           AF.Copy).then_inc(s_e2, 1)
                        for j2 in range(NES * KD):
                            stt, k = divmod(j2, KD)
                            j = KD * NHT + j2
                            act.wait_ge(s_t2e, t2_0 + j + 1)
                            act.activation(xefm[:, k, stt * P:(stt + 1) * P],
                                           ps_tb[:, (j % 2) * P:(j % 2 + 1) * P],
                                           AF.Copy).then_inc(s_e2, 1)
                        for j in range(NHT * 2):
                            act.wait_ge(s_hm, hm_0 + j + 1)
                            if j >= 1:
                                act.wait_ge(s_hv, hv_0 + j)
                            n = 512 if j % 2 == 0 else CAPE - 512
                            act.activation(relb[:, 0:n], ps_h[j % 2][:, 0:n],
                                           AF.Relu).then_inc(s_e2, 1)

                    @blk.vector
                    def _(v, e=e, t2_0=t2_0, e2_0=e2_0, hv_0=hv_0, ym_0=ym_0,
                          ye_0=ye_0, ys_0=ys_0):
                        v = EA(v)
                        base_e2 = e2_0 + N_TPE
                        for j in range(NHT * 2):
                            m, nch = divmod(j, 2)
                            v.wait_ge(s_e2, base_e2 + j + 1)
                            n0 = 0 if nch == 0 else 512
                            n1 = 512 if nch == 0 else CAPE
                            v.tensor_tensor(hfm[:, m, n0:n1],
                                            relb[:, 0:n1 - n0],
                                            relb[:, 0:n1 - n0],
                                            ALU.mult).then_inc(s_hv, 1)
                        for stt in range(NES):
                            for nch in range(2):
                                j = stt * 2 + nch
                                v.wait_ge(s_ym, ym_0 + j + 1)
                                if stt > 0 and nch == 0:
                                    v.wait_ge(s_ys, ys_0 + 16 * stt)
                                v.tensor_copy(yet[:, nch * 512:(nch + 1) * 512],
                                              ps_y[j % 2][:, :]).then_inc(s_ye, 1)

                    @blk.sync
                    def _(sy, e=e, ye_0=ye_0, ys_0=ys_0):
                        for stt in range(NES):
                            sy.wait_ge(s_ye, ye_0 + stt * 2 + 2)
                            sy.dma_start(ye_hbm[e * CAPE + stt * P:
                                                e * CAPE + (stt + 1) * P, :],
                                         yet[:, :]).then_inc(s_ys, 16)
                        sy.wait_ge(s_ys, ys_0 + 16 * NES)

        s_g = nc.alloc_semaphore("l0f_sg")
        s_gt = nc.alloc_semaphore("l0f_sgt")
        s_ge = nc.alloc_semaphore("l0f_sge")
        s_go = nc.alloc_semaphore("l0f_sgo")
        s_gs = nc.alloc_semaphore("l0f_sgs")
        with ExitStack() as stG:
            EG = stG.enter_context
            xcg = EG(nc.sbuf_tensor("l0f_xcg", [P, KD, C], F32))
            dti = EG(nc.sbuf_tensor("l0f_dti", [P, NTT, 1024], F32))
            dfm = EG(nc.sbuf_tensor("l0f_dfm", [P, KD, C], F32))
            obuf = EG(nc.sbuf_tensor("l0f_og", [P, KD, C], F32))
            ps_gf = BankedPsum(stG, nc, "l0f_psgf", 2, P, P, F32)
            for i in range(NCHK):
                with nc.Block(f"l0f_g{i}") as blk:
                    @blk.sync
                    def _(sy, i=i):
                        sy.dma_start(xcg[:, :, :],
                                     x_in_v[:, :, i * C:(i + 1) * C]
                                     ).then_inc(s_x, 16)
                        sy.wait_ge(s_x, 16 * (NCHK + i + 1))
                    @blk.gpsimd
                    def _(g, i=i):
                        for tt in range(NTT):
                            g.indirect_dma_start(
                                dti[:, tt, :], None, ye_hbm[:, :],
                                bass.IndirectOffsetOnAxis(
                                    dest_all[:, i * NTT + tt:i * NTT + tt + 1],
                                    0)).then_inc(s_g, 16)
                        g.wait_ge(s_g, 16 * NTT * (i + 1))
                with nc.Block(f"l0f_g2_{i}") as blk:
                    @blk.tensor
                    def _(te, i=i):
                        te.wait_ge(s_g, 16 * NTT * (i + 1))
                        for j in range(NTT * KD):
                            tt, k = divmod(j, KD)
                            jj = i * NTT * KD + j
                            if j >= 2:
                                te.wait_ge(s_ge, jj - 1)
                            te.transpose(ps_gf[:, (jj % 2) * P:(jj % 2 + 1) * P],
                                         dti[:, tt, k * P:(k + 1) * P],
                                         p.sb["ident_f32"][:, :]).then_inc(s_gt, 1)
                    @blk.scalar
                    def _(act, i=i):
                        act = EA(act)
                        for j in range(NTT * KD):
                            tt, k = divmod(j, KD)
                            jj = i * NTT * KD + j
                            act.wait_ge(s_gt, jj + 1)
                            act.activation(dfm[:, k, tt * P:(tt + 1) * P],
                                           ps_gf[:, (jj % 2) * P:(jj % 2 + 1) * P],
                                           AF.Copy).then_inc(s_ge, 1)
                    @blk.vector
                    def _(v, i=i):
                        v = EA(v)
                        v.wait_ge(s_ge, (i + 1) * NTT * KD)
                        last = None
                        for k in range(KD):
                            last = v.tensor_tensor(obuf[:, k, :], xcg[:, k, :],
                                                   dfm[:, k, :], ALU.add)
                        last.then_inc(s_go, 1)
                    @blk.sync
                    def _(sy, i=i):
                        sy.wait_ge(s_go, i + 1)
                        sy.dma_start(x_out_v[:, :, i * C:(i + 1) * C],
                                     obuf[:, :, :]).then_inc(s_gs, 16)
                        sy.wait_ge(s_gs, 16 * (i + 1))



def emit_l3_moe(p: Prog, x_in, y_out_tm, wr_w, w1_w, w2_w, xe_hbm, ye_hbm,
                lnstage, cdram, acdram, ltmask, cap, ncores,
                stop_phase=None):
    nc, T = p.nc, p.T

    def _dbg_finish():
        s_f = nc.alloc_semaphore("l3f_dbgf")
        with nc.Block("l3f_dbg") as blk:
            @blk.sync
            def _(sy):
                for z in range(8):
                    r0 = z * (T // 8)
                    r1 = r0 + T // 8
                    sy.dma_start(
                        y_out_tm[r0:r1, :].rearrange("t d -> (t d)"),
                        x_in[:, :].rearrange("d t -> (d t)")[r0 * D:r1 * D]
                    ).then_inc(s_f, 16)
                sy.wait_ge(s_f, 16 * 8)
    C = 512
    NCHK = T // C
    NTT = C // P
    NTTT = T // P
    CAPE = 768
    NSLOT = 8 * CAPE
    HD = 4096
    NHT = HD // P
    NES = CAPE // P
    NMB = HD // 512

    with ExitStack() as st:
        E = st.enter_context
        wr_sb = E(nc.sbuf_tensor("l3f_wr", [P, KD, 8], F32))
        iota_e8 = E(nc.sbuf_tensor("l3f_iotae", [P, 8], F32))
        dest_all = E(nc.sbuf_tensor("l3f_dest", [P, NTTT], I32))
        eid_all = E(nc.sbuf_tensor("l3f_eid", [P, NTTT], F32))
        rnk_all = E(nc.sbuf_tensor("l3f_rnk", [P, NTTT], F32))
        p_all = E(nc.sbuf_tensor("l3f_pall", [P, NTTT], F32))
        base8 = E(nc.sbuf_tensor("l3f_base8", [P, 8], F32))
        counts = E(nc.sbuf_tensor("l3f_counts", [8, 1], F32))
        ps_tb = BankedPsum(st, nc, "l3f_pstb", 2, P, P, BF16)

        with nc.sbuf_tensor("l3f_ioti", [P, 8], I32) as ioti, \
             nc.sbuf_tensor("l3f_zrow", [P, 1024], BF16) as zrow, \
             nc.sbuf_tensor("l3f_zrowf", [1, 1024], F32) as zrowf:
            ch = nc.alloc_semaphore("l3f_ch")
            with nc.Block("l3f_prep1") as blk:
                @blk.sync
                def _(sy):
                    sy.dma_start(wr_sb[:, :, :],
                                 wr_w[:, :].rearrange("(k p) e -> p k e", p=P)
                                 ).then_inc(ch, 16)
                    sy.wait_ge(ch, 16)
                @blk.gpsimd
                def _(g):
                    g.iota(ioti[:, :], [[1, 8]], channel_multiplier=0)
                @blk.vector
                def _(v):
                    v = EA(v)
                    v.memset(zrow[:, :], 0.0)
                    v.memset(zrowf[:, :], 0.0)
            with nc.Block("l3f_prep2") as blk:
                @blk.vector
                def _(v):
                    v.tensor_copy(iota_e8[:, :], ioti[:, :])
                @blk.sync
                def _(sy):
                    nz = (NSLOT + 1 + P - 1) // P
                    for z in range(nz):
                        r0 = z * P
                        r1 = min(r0 + P, NSLOT + 1)
                        sy.dma_start(xe_hbm[r0:r1, :],
                                     zrow[0:r1 - r0, :]).then_inc(ch, 16)
                    sy.dma_start(ye_hbm[NSLOT:NSLOT + 1, :],
                                 zrowf[0:1, :]).then_inc(ch, 16)
                    sy.wait_ge(ch, 32 + 16 * nz)

        if stop_phase == "prep":
            _dbg_finish()
            return

        s_x = nc.alloc_semaphore("l3f_sx")
        s_lg = nc.alloc_semaphore("l3f_slg")
        s_ev = nc.alloc_semaphore("l3f_sev")
        s_tp = nc.alloc_semaphore("l3f_stp")
        s_ab = nc.alloc_semaphore("l3f_sab")
        s_ls = nc.alloc_semaphore("l3f_sls")
        x_in_v = fmv(x_in)
        with ExitStack() as stR:
            ER = stR.enter_context
            xc = ER(nc.sbuf_tensor("l3f_xc", [P, KD, C], F32))
            lnb = ER(nc.sbuf_tensor("l3f_lnb", [P, KD, C], BF16))
            ln32 = ER(nc.sbuf_tensor("l3f_ln32", [P, KD, C], F32))
            lntm = ER(nc.sbuf_tensor("l3f_lntm", [P, NTT, 1024], BF16))
            lg_sb = ER(nc.sbuf_tensor("l3f_lg", [8, C], F32))
            lgtm = ER(nc.sbuf_tensor("l3f_lgtm", [P, NTT, 8], F32))
            etb = ER(nc.sbuf_tensor("l3f_etb", [P, 8], F32))
            m8 = ER(nc.sbuf_tensor("l3f_m8", [P, NTT], F32))
            msum = ER(nc.sbuf_tensor("l3f_msum", [P, NTT], F32))
            masktm = ER(nc.sbuf_tensor("l3f_masktm", [P, NTT, 8], F32))
            maskfm = ER(nc.sbuf_tensor("l3f_maskfm", [8, C], F32))
            rankfm = ER(nc.sbuf_tensor("l3f_rankfm", [8, C], F32))
            ranktm = ER(nc.sbuf_tensor("l3f_ranktm", [P, NTT, 8], F32))
            tcol = ER(nc.sbuf_tensor("l3f_tcol", [P, 4], F32))
            ones8 = ER(nc.sbuf_tensor("l3f_ones8", [8, C], F32))
            carry = ER(nc.sbuf_tensor("l3f_carry", [8, 1], F32))
            ln = LnFm(p, stR, "l3fln", C)
            pbR = BankedPsum(stR, nc, "l3f_pbr", 2, P, 512, F32)
            ps_lg = pbR.t[0][0:8, 0:C]
            with nc.Block("l3f_rinit") as blk:
                @blk.vector
                def _(v):
                    v = EA(v)
                    v.memset(ones8[:, :], 1.0)
                    v.memset(carry[:, :], 0.0)

            for i in range(NCHK):
                ev0, ab0 = i * (3 * NTT + 2), i * (NTT * (KD + 1))
                tp0 = i * (NTT * KD + 3 * NTT)
                with nc.Block(f"l3f_r{i}") as blk:
                    @blk.sync
                    def _(sy, i=i):
                        sy.dma_start(xc[:, :, :],
                                     x_in_v[:, :, i * C:(i + 1) * C]
                                     ).then_inc(s_x, 16)
                        sy.wait_ge(s_x, 16 * (i + 1))
                with nc.Block(f"l3f_r2_{i}") as blk:
                    @blk.scalar
                    def _(act, i=i, tp0=tp0, ab0=ab0, ev0=ev0):
                        act = EA(act)
                        ln.act_emit(act, i, xc[:, :, :])
                        for j in range(NTT * KD):
                            tt, k = divmod(j, KD)
                            act.wait_ge(s_tp, tp0 + j + 1)
                            act.activation(lntm[:, tt, k * P:(k + 1) * P],
                                           ps_tb[:, (j % 2) * P:(j % 2 + 1) * P],
                                           AF.Copy).then_inc(s_ab, 1)
                        for tt in range(NTT):
                            act.wait_ge(s_ev, ev0 + 1 + tt + 1)
                            act.activation(etb[:, :], lgtm[:, tt, :], AF.Exp,
                                           bias=m8[:, tt:tt + 1],
                                           accum_out=msum[:, tt:tt + 1]
                                           ).then_inc(s_ab, 1)

                    @blk.tensor
                    def _(te, i=i, tp0=tp0, ev0=ev0, ab0=ab0):
                        ln.pe_emit(te, i, p.sb["ones_col_bf"][:, :],
                                   p.sb["ones_row_f32"][:, :])
                        te.wait_ge(ln.sv, 2 * i + 2)
                        last = None
                        for k in range(KD):
                            last = te.matmul(ps_lg[:, :], wr_sb[:, k, :],
                                             ln32[:, k, :], start=(k == 0),
                                             stop=(k == KD - 1))
                        last.then_inc(s_lg, 1)
                        for j in range(NTT * KD):
                            tt, k = divmod(j, KD)
                            if j >= 2:
                                te.wait_ge(s_ab, ab0 + j - 1)
                            te.transpose(ps_tb[:, (j % 2) * P:(j % 2 + 1) * P],
                                         lnb[:, k, tt * P:(tt + 1) * P],
                                         p.sb["ident_bf"][:, :]).then_inc(s_tp, 1)
                        te.wait_ge(s_ev, ev0 + 1)
                        for tt in range(NTT):
                            if tt >= 2:
                                te.wait_ge(s_ev, ev0 + tt)
                            te.transpose(pbR.t[tt % 2][0:P, 0:8],
                                         lg_sb[:, tt * P:(tt + 1) * P],
                                         p.sb["ident_f32"][0:8, 0:8]
                                         ).then_inc(s_tp, 1)
                        te.wait_ge(s_ev, ev0 + 1 + NTT)
                        for tt in range(NTT):
                            if tt >= 2:
                                te.wait_ge(s_ev, ev0 + NTT + tt)
                            te.transpose(pbR.t[tt % 2][0:8, 0:P],
                                         masktm[:, tt, :],
                                         p.sb["ident_f32"][:, :]).then_inc(s_tp, 1)
                        te.wait_ge(s_ev, ev0 + 2 * NTT + 2)
                        for tt in range(NTT):
                            if tt >= 2:
                                te.wait_ge(s_ev, ev0 + 2 * NTT + tt + 1)
                            te.transpose(pbR.t[tt % 2][0:P, 0:8],
                                         rankfm[:, tt * P:(tt + 1) * P],
                                         p.sb["ident_f32"][0:8, 0:8]
                                         ).then_inc(s_tp, 1)

                    @blk.vector
                    def _(v, i=i, tp0=tp0, ev0=ev0, ab0=ab0):
                        v = EA(v)
                        ln.v_emit(v, i, xc[:, :, :], lnb, out2_ap=ln32)
                        v.wait_ge(s_lg, i + 1)
                        v.tensor_copy(lg_sb[:, :], ps_lg[:, :]).then_inc(s_ev, 1)
                        for tt in range(NTT):
                            v.wait_ge(s_tp, tp0 + NTT * KD + tt + 1)
                            v.tensor_copy(lgtm[:, tt, :],
                                          pbR.t[tt % 2][0:P, 0:8])
                            v.reduce_max(m8[:, tt:tt + 1], lgtm[:, tt, :],
                                         axis=AX.X, negate=True)
                            v.tensor_scalar(masktm[:, tt, :], lgtm[:, tt, :],
                                            m8[:, tt:tt + 1], 0.0,
                                            ALU.add, ALU.is_ge).then_inc(s_ev, 1)
                        for tt in range(NTT):
                            v.wait_ge(s_tp, tp0 + NTT * KD + NTT + tt + 1)
                            v.tensor_copy(maskfm[:, tt * P:(tt + 1) * P],
                                          pbR.t[tt % 2][0:8, 0:P]
                                          ).then_inc(s_ev, 1)
                        v.tensor_tensor_scan(rankfm[:, :], ones8[:, :],
                                             maskfm[:, :], carry[:, :],
                                             ALU.mult, ALU.add)
                        v.tensor_copy(carry[:, :], rankfm[:, C - 1:C])
                        v.tensor_tensor(rankfm[:, :], rankfm[:, :], maskfm[:, :],
                                        ALU.subtract).then_inc(s_ev, 1)
                        for tt in range(NTT):
                            g_tt = i * NTT + tt
                            v.wait_ge(s_tp, tp0 + NTT * KD + 2 * NTT + tt + 1)
                            v.wait_ge(s_ab, ab0 + NTT * KD + tt + 1)
                            v.reciprocal(p_all[:, g_tt:g_tt + 1],
                                         msum[:, tt:tt + 1])
                            v.tensor_copy(ranktm[:, tt, :],
                                          pbR.t[tt % 2][0:P, 0:8])
                            v.tensor_tensor(ranktm[:, tt, :], ranktm[:, tt, :],
                                            masktm[:, tt, :], ALU.mult)
                            v.reduce_sum(rnk_all[:, g_tt:g_tt + 1],
                                         ranktm[:, tt, :], axis=AX.X)
                            v.tensor_tensor(lgtm[:, tt, :], masktm[:, tt, :],
                                            iota_e8[:, :], ALU.mult)
                            v.reduce_sum(eid_all[:, g_tt:g_tt + 1],
                                         lgtm[:, tt, :], axis=AX.X
                                         ).then_inc(s_ev, 1)

                    @blk.sync
                    def _(sy, i=i, ab0=ab0):
                        sy.wait_ge(s_ab, ab0 + NTT * KD)
                        sy.dma_start(
                            lnstage[:, :].rearrange(
                                "(n p) d -> p n d", p=P)[:, i * NTT:(i + 1) * NTT,
                                                         :],
                            lntm[:, :, :]).then_inc(s_ls, 16)
                        sy.wait_ge(s_ls, 16 * (i + 1))

            with nc.Block("l3f_cnt") as blk:
                @blk.vector
                def _(v):
                    v.tensor_copy(counts[:, :], carry[:, :])

        if stop_phase == "R":
            _dbg_finish()
            return

        s_cc = nc.alloc_semaphore("l3f_scc")
        s_cd = nc.alloc_semaphore("l3f_scd")
        s_cg = nc.alloc_semaphore("l3f_scg")
        with nc.sbuf_tensor("l3f_ac", [8, 8], F32) as acsb, \
             nc.sbuf_tensor("l3f_ltm", [8, 1], F32) as ltmsb, \
             nc.sbuf_tensor("l3f_b8r", [1, 8], F32) as b8r, \
             nc.psum_tensor("l3f_psb", [1, 8], F32) as ps_b, \
             nc.psum_tensor("l3f_psb2", [P, 8], F32) as ps_b2:
            with nc.Block("l3f_cc1") as blk:
                @blk.sync
                def _(sy):
                    sy.dma_start(cdram[:, :], counts[:, :]).then_inc(s_cd, 16)
                    sy.dma_start(ltmsb[:, :], ltmask[:, :]).then_inc(s_cd, 16)
                    sy.wait_ge(s_cd, 32)
                @blk.vector
                def _(v):
                    v.memset(acsb[:, :], 0.0)
            with nc.Block("l3f_cc2") as blk:
                @blk.gpsimd
                def _(g):
                    if ncores > 1:
                        g.collective_compute(
                            "AllGather", ALU.bypass,
                            replica_groups=[list(range(ncores))],
                            ins=[cdram.ap().opt()],
                            outs=[acdram.ap().opt()]).then_inc(s_cc, 1)
                        g.wait_ge(s_cc, 1)
                    else:
                        g.dma_start(acdram[0:1, :], cdram[:, :]).then_inc(s_cg, 16)
                        g.wait_ge(s_cg, 16)
            with nc.Block("l3f_cc3") as blk:
                @blk.sync
                def _(sy):
                    sy.dma_start(acsb[0:ncores, :],
                                 acdram[0:ncores, :]).then_inc(s_cd, 16)
                    sy.wait_ge(s_cd, 48)
            with nc.Block("l3f_cc4") as blk:
                @blk.tensor
                def _(te):
                    te.matmul(ps_b[:, :], ltmsb[0:8, :], acsb[:, :],
                              start=True, stop=True).then_inc(s_cc, 1)
                @blk.vector
                def _(v):
                    v.wait_ge(s_cc, 2 if ncores > 1 else 1)
                    v.tensor_copy(b8r[:, :], ps_b[:, :]).then_inc(s_cc, 1)
            with nc.Block("l3f_cc5") as blk:
                @blk.tensor
                def _(te):
                    te.matmul(ps_b2[:, :], p.sb["ones_row_f32"][:, :],
                              b8r[:, :], start=True, stop=True).then_inc(s_cc, 1)
                @blk.vector
                def _(v):
                    v.wait_ge(s_cc, 4 if ncores > 1 else 3)
                    v.tensor_copy(base8[:, :], ps_b2[:, :])

        if stop_phase == "CC":
            _dbg_finish()
            return

        s_r2 = nc.alloc_semaphore("l3f_sr2")
        s_sc = [nc.alloc_semaphore("l3f_ssc0"), nc.alloc_semaphore("l3f_ssc1")]
        s_lld = [nc.alloc_semaphore("l3f_slld0"), nc.alloc_semaphore("l3f_slld1")]
        with nc.sbuf_tensor("l3f_lnt2", [P, 2, 1024], BF16) as lnt2, \
             nc.sbuf_tensor("l3f_t4", [P, 6], F32) as t4, \
             nc.sbuf_tensor("l3f_eq8", [P, 8], F32) as eq8:
            lnsv = lnstage[:, :].rearrange("(n p) d -> p n d", p=P)
            with nc.Block("l3f_rr2") as blk:
                @blk.sync
                def _(sy):
                    for tt in range(NTTT):
                        if tt >= 2:
                            sy.wait_ge(s_sc[tt % 2], 16 * (tt // 2))
                        sy.dma_start(lnt2[:, tt % 2, :],
                                     lnsv[:, tt, :]).then_inc(s_lld[tt % 2], 16)

                @blk.vector
                def _(v):
                    v = EA(v)
                    for tt in range(NTTT):
                        v.tensor_scalar(eq8[:, :], iota_e8[:, :],
                                        eid_all[:, tt:tt + 1], None,
                                        ALU.is_equal)
                        v.tensor_tensor(eq8[:, :], eq8[:, :], base8[:, :],
                                        ALU.mult)
                        v.reduce_sum(t4[:, 0:1], eq8[:, :], axis=AX.X)
                        v.tensor_tensor(t4[:, 1:2], rnk_all[:, tt:tt + 1],
                                        t4[:, 0:1], ALU.add)
                        v.tensor_scalar(t4[:, 2:3], t4[:, 1:2], float(cap),
                                        None, ALU.is_ge)
                        v.scalar_tensor_tensor(t4[:, 3:4], eid_all[:, tt:tt + 1],
                                               float(CAPE),
                                               rnk_all[:, tt:tt + 1],
                                               ALU.mult, ALU.add)
                        v.scalar_tensor_tensor(t4[:, 4:5], t4[:, 2:3], 1e9,
                                               t4[:, 3:4], ALU.mult, ALU.add)
                        v.tensor_scalar(t4[:, 5:6], t4[:, 4:5], float(NSLOT),
                                        None, ALU.min)
                        v.tensor_copy(dest_all[:, tt:tt + 1],
                                      t4[:, 5:6]).then_inc(s_r2, 1)

                @blk.gpsimd
                def _(g):
                    for tt in range(NTTT):
                        g.wait_ge(s_r2, tt + 1)
                        g.wait_ge(s_lld[tt % 2], 16 * (tt // 2 + 1))
                        g.indirect_dma_start(
                            xe_hbm[:, :],
                            bass.IndirectOffsetOnAxis(
                                dest_all[:, tt:tt + 1], 0),
                            lnt2[:, tt % 2, :], None,
                            bounds_check=NSLOT - 1,
                            oob_is_err=False).then_inc(s_sc[tt % 2], 16)
                    g.wait_ge(s_sc[0], 16 * ((NTTT + 1) // 2))
                    g.wait_ge(s_sc[1], 16 * (NTTT // 2))

        if stop_phase == "R2":
            _dbg_finish()
            return

        s_wx = nc.alloc_semaphore("l3f_swx")
        s_w2 = nc.alloc_semaphore("l3f_sw2")
        s_wf = [nc.alloc_semaphore("l3f_swf0"), nc.alloc_semaphore("l3f_swf1")]
        s_t2e = nc.alloc_semaphore("l3f_st2")
        s_e2 = nc.alloc_semaphore("l3f_se2")
        s_hm = nc.alloc_semaphore("l3f_shm")
        s_ym = nc.alloc_semaphore("l3f_sym")
        s_ye = nc.alloc_semaphore("l3f_sye")
        s_ys = nc.alloc_semaphore("l3f_sys")
        N_TPE = NES * KD
        N_E2 = N_TPE + 2 * NHT
        with ExitStack() as stE:
            EE = stE.enter_context
            xet = EE(nc.sbuf_tensor("l3f_xet", [P, 1024], BF16))
            xefm = EE(nc.sbuf_tensor("l3f_xefm", [P, KD, CAPE], BF16))
            fcb = EE(nc.sbuf_tensor("l3f_fcb", [P, 2, KD, 512], BF16))
            w2sb = EE(nc.sbuf_tensor("l3f_w2", [P, NHT, 1024], BF16))
            hfm = EE(nc.sbuf_tensor("l3f_hfm", [P, NHT, CAPE], BF16))
            yet = EE(nc.sbuf_tensor("l3f_yet", [P, 1024], F32))
            ps_h = [EE(nc.psum_tensor("l3f_psh0", [P, 512], F32)),
                    EE(nc.psum_tensor("l3f_psh1", [P, 512], F32))]
            ps_y = [EE(nc.psum_tensor("l3f_psy0", [P, 512], F32)),
                    EE(nc.psum_tensor("l3f_psy1", [P, 512], F32))]
            for e in range(8):
                t2_0 = e * N_TPE
                e2_0 = e * N_E2
                hm_0 = e * (NMB * 8)
                ym_0 = e * (NES * 2)
                ye_0 = e * (NES * 2)
                ys_0 = e * (NES * 16)
                wx_0 = e * 16 * NES
                w2_0 = e * 16
                wf_0 = [e * 16 * (NMB - NMB // 2), e * 16 * (NMB // 2)]
                with nc.Block(f"l3f_e{e}") as blk:
                    @blk.gpsimd
                    def _(g, e=e, t2_0=t2_0, hm_0=hm_0, wx_0=wx_0, w2_0=w2_0,
                          wf_0=wf_0):
                        for stt in range(NES):
                            g.dma_start(xet[:, :],
                                        xe_hbm[e * CAPE + stt * P:
                                               e * CAPE + (stt + 1) * P, :]
                                        ).then_inc(s_wx, 16)
                            g.wait_ge(s_t2e, t2_0 + (stt + 1) * KD)
                        g.dma_start(w2sb[:, :, :],
                                    w2_w[e, :, :].rearrange(
                                        "(k p) d -> p k d", p=P)
                                    ).then_inc(s_w2, 16)
                        for mb in range(NMB):
                            if mb >= 2:
                                g.wait_ge(s_hm, hm_0 + (mb - 1) * 8)
                            g.dma_start(fcb[:, mb % 2, :, :],
                                        w1_w[e, :, mb * 512:(mb + 1) * 512]
                                        .rearrange("(k p) h -> p k h", p=P)
                                        ).then_inc(s_wf[mb % 2], 16)
                        g.wait_ge(s_wf[0], wf_0[0] + 16 * (NMB - NMB // 2))
                        g.wait_ge(s_wf[1], wf_0[1] + 16 * (NMB // 2))
                        g.wait_ge(s_w2, w2_0 + 16)

                    @blk.tensor
                    def _(te, e=e, t2_0=t2_0, e2_0=e2_0, hm_0=hm_0, ym_0=ym_0,
                          ye_0=ye_0, wx_0=wx_0, w2_0=w2_0, wf_0=wf_0):
                        tpn = 0
                        for stt in range(NES):
                            te.wait_ge(s_wx, wx_0 + 16 * (stt + 1))
                            for k in range(KD):
                                if tpn >= 2:
                                    te.wait_ge(s_e2, e2_0 + tpn - 1)
                                te.transpose(ps_tb[:, (tpn % 2) * P:
                                                   (tpn % 2 + 1) * P],
                                             xet[:, k * P:(k + 1) * P],
                                             p.sb["ident_bf"][:, :]
                                             ).then_inc(s_t2e, 1)
                                tpn += 1
                        te.wait_ge(s_e2, e2_0 + N_TPE)
                        for mb in range(NMB):
                            te.wait_ge(s_wf[mb % 2], wf_0[mb % 2]
                                       + 16 * (mb // 2 + 1))
                            for mm in range(4):
                                m = mb * 4 + mm
                                for nch, (n0, n1) in enumerate(((0, 512),
                                                               (512, CAPE))):
                                    j = m * 2 + nch
                                    if j >= 2:
                                        te.wait_ge(s_e2, e2_0 + N_TPE + j - 1)
                                    last = None
                                    for k in range(KD):
                                        last = te.matmul(
                                            ps_h[j % 2][:, 0:n1 - n0],
                                            fcb[:, mb % 2, k,
                                                mm * P:(mm + 1) * P],
                                            xefm[:, k, n0:n1], start=(k == 0),
                                            stop=(k == KD - 1))
                                    last.then_inc(s_hm, 1)
                        te.wait_ge(s_e2, e2_0 + N_E2)
                        te.wait_ge(s_w2, w2_0 + 16)
                        for stt in range(NES):
                            for nch in range(2):
                                j = stt * 2 + nch
                                if j >= 2:
                                    te.wait_ge(s_ye, ye_0 + j - 1)
                                last = None
                                for k in range(NHT):
                                    last = te.matmul(
                                        ps_y[j % 2][:, :],
                                        hfm[:, k, stt * P:(stt + 1) * P],
                                        w2sb[:, k, nch * 512:(nch + 1) * 512],
                                        start=(k == 0), stop=(k == NHT - 1))
                                last.then_inc(s_ym, 1)

                    @blk.scalar
                    def _(act, e=e, t2_0=t2_0, e2_0=e2_0, hm_0=hm_0):
                        act = EA(act)
                        for j in range(N_TPE):
                            stt, k = divmod(j, KD)
                            act.wait_ge(s_t2e, t2_0 + j + 1)
                            act.activation(xefm[:, k, stt * P:(stt + 1) * P],
                                           ps_tb[:, (j % 2) * P:(j % 2 + 1) * P],
                                           AF.Copy).then_inc(s_e2, 1)
                        for j in range(NHT * 2):
                            m, nch = divmod(j, 2)
                            act.wait_ge(s_hm, hm_0 + j + 1)
                            n0 = 0 if nch == 0 else 512
                            n1 = 512 if nch == 0 else CAPE
                            act.activation(hfm[:, m, n0:n1],
                                           ps_h[j % 2][:, 0:n1 - n0],
                                           AF.Relu).then_inc(s_e2, 1)

                    @blk.vector
                    def _(v, e=e, ym_0=ym_0, ye_0=ye_0, ys_0=ys_0):
                        v = EA(v)
                        for stt in range(NES):
                            for nch in range(2):
                                j = stt * 2 + nch
                                v.wait_ge(s_ym, ym_0 + j + 1)
                                if stt > 0 and nch == 0:
                                    v.wait_ge(s_ys, ys_0 + 16 * stt)
                                v.tensor_copy(yet[:, nch * 512:(nch + 1) * 512],
                                              ps_y[j % 2][:, :]).then_inc(s_ye, 1)

                    @blk.sync
                    def _(sy, e=e, ye_0=ye_0, ys_0=ys_0):
                        for stt in range(NES):
                            sy.wait_ge(s_ye, ye_0 + stt * 2 + 2)
                            sy.dma_start(ye_hbm[e * CAPE + stt * P:
                                                e * CAPE + (stt + 1) * P, :],
                                         yet[:, :]).then_inc(s_ys, 16)
                        sy.wait_ge(s_ys, ys_0 + 16 * NES)

        if stop_phase == "E":
            _dbg_finish()
            return

        s_g = nc.alloc_semaphore("l3f_sg")
        s_gt = nc.alloc_semaphore("l3f_sgt")
        s_ge = nc.alloc_semaphore("l3f_sge")
        s_go = nc.alloc_semaphore("l3f_sgo")
        s_gs = nc.alloc_semaphore("l3f_sgs")
        s_x2 = nc.alloc_semaphore("l3f_sx2")
        with ExitStack() as stG:
            EG = stG.enter_context
            xcg = EG(nc.sbuf_tensor("l3f_xcg", [P, KD, C], F32))
            dti = EG(nc.sbuf_tensor("l3f_dti", [P, NTT, 1024], F32))
            xtm = EG(nc.sbuf_tensor("l3f_xtm", [P, 1024], F32))
            ytile = EG(nc.sbuf_tensor("l3f_yt", [P, 1024], F32))
            ps_gf = BankedPsum(stG, nc, "l3f_psgf", 2, P, P, F32)
            for i in range(NCHK):
                with nc.Block(f"l3f_g{i}") as blk:
                    @blk.sync
                    def _(sy, i=i):
                        sy.dma_start(xcg[:, :, :],
                                     x_in_v[:, :, i * C:(i + 1) * C]
                                     ).then_inc(s_x2, 16)
                        sy.wait_ge(s_x2, 16 * (i + 1))
                    @blk.gpsimd
                    def _(g, i=i):
                        for tt in range(NTT):
                            g.indirect_dma_start(
                                dti[:, tt, :], None, ye_hbm[:, :],
                                bass.IndirectOffsetOnAxis(
                                    dest_all[:, i * NTT + tt:i * NTT + tt + 1],
                                    0)).then_inc(s_g, 16)
                        g.wait_ge(s_g, 16 * NTT * (i + 1))
                with nc.Block(f"l3f_g2_{i}") as blk:
                    @blk.tensor
                    def _(te, i=i):
                        for j in range(NTT * KD):
                            tt, k = divmod(j, KD)
                            if j >= 2:
                                te.wait_ge(s_ge, i * NTT * KD + j - 1)
                            te.transpose(ps_gf[:, (j % 2) * P:(j % 2 + 1) * P],
                                         xcg[:, k, tt * P:(tt + 1) * P],
                                         p.sb["ident_f32"][:, :]).then_inc(s_gt, 1)
                    @blk.scalar
                    def _(act, i=i):
                        act = EA(act)
                        for j in range(NTT * KD):
                            tt, k = divmod(j, KD)
                            act.wait_ge(s_gt, i * NTT * KD + j + 1)
                            if k == 0 and tt > 0:
                                act.wait_ge(s_go, i * NTT + tt)
                            act.activation(xtm[:, k * P:(k + 1) * P],
                                           ps_gf[:, (j % 2) * P:(j % 2 + 1) * P],
                                           AF.Copy).then_inc(s_ge, 1)
                    @blk.vector
                    def _(v, i=i):
                        v = EA(v)
                        for tt in range(NTT):
                            g_tt = i * NTT + tt
                            v.wait_ge(s_ge, g_tt * KD + KD)
                            if g_tt > 0:
                                v.wait_ge(s_gs, 16 * g_tt)
                            v.scalar_tensor_tensor(ytile[:, :], dti[:, tt, :],
                                                   p_all[:, g_tt:g_tt + 1],
                                                   xtm[:, :], ALU.mult,
                                                   ALU.add).then_inc(s_go, 1)
                    @blk.sync
                    def _(sy, i=i):
                        for tt in range(NTT):
                            g_tt = i * NTT + tt
                            sy.wait_ge(s_go, g_tt + 1)
                            sy.dma_start(y_out_tm[g_tt * P:(g_tt + 1) * P, :],
                                         ytile[:, :]).then_inc(s_gs, 16)
                        sy.wait_ge(s_gs, 16 * NTT * (i + 1))



def _emit_rope_consts(p, st, tag):
    nc = p.nc
    E = st.enter_context
    invf = E(nc.sbuf_tensor(f"{tag}_invf", [P, 1], F32))
    sgn = E(nc.sbuf_tensor(f"{tag}_sgn", [P, 1], F32))
    with nc.sbuf_tensor(f"{tag}_it", [P, 1], I32) as it, \
         nc.sbuf_tensor(f"{tag}_rtmp", [P, 3], F32) as rt:
        with nc.Block(f"{tag}_rc1") as blk:
            @blk.gpsimd
            def _(g):
                g.iota(it[:, :], [[0, 1]], channel_multiplier=1)
        with nc.Block(f"{tag}_rc2") as blk:
            @blk.vector
            def _(v):
                v = EA(v)
                v.tensor_copy(invf[:, :], it[:, :])
                for q in range(3):
                    v.tensor_scalar(rt[:, q:q + 1], invf[:, :],
                                    32.0 * (q + 1), None, ALU.is_ge)
                v.tensor_tensor(sgn[:, :], rt[:, 0:1], rt[:, 1:2],
                                ALU.subtract)
                v.tensor_tensor(sgn[:, :], sgn[:, :], rt[:, 2:3], ALU.add)
                v.tensor_scalar(sgn[:, :], sgn[:, :], 2.0, -1.0, ALU.mult,
                                ALU.add)
                v.tensor_tensor(rt[:, 0:1], rt[:, 0:1], rt[:, 1:2], ALU.add)
                v.tensor_tensor(rt[:, 0:1], rt[:, 0:1], rt[:, 2:3], ALU.add)
                v.tensor_scalar(rt[:, 0:1], rt[:, 0:1], -32.0, None, ALU.mult)
                v.tensor_tensor(invf[:, :], invf[:, :], rt[:, 0:1], ALU.add)
                v.tensor_scalar(invf[:, :], invf[:, :],
                                -math.log(10000.0) / 32.0, None, ALU.mult)
        with nc.Block(f"{tag}_rc3") as blk:
            @blk.scalar
            def _(act):
                act.activation(invf[:, :], invf[:, :], AF.Exp)
    return invf, sgn


def _emit_wT_prep(p, tg, jobs):
    nc = p.nc
    with nc.sbuf_tensor(f"{tg}_twrow", [P, 2, 1024], BF16) as wrow, \
         nc.sbuf_tensor(f"{tg}_twts", [P, KD, P], BF16) as wTs, \
         ExitStack() as _tpst3:
        ptp = BankedPsum(_tpst3, nc, f"{tg}_twps", 2, P, P, BF16)
        s_l = [nc.alloc_semaphore(f"{tg}_twl0"), nc.alloc_semaphore(f"{tg}_twl1")]
        s_t = nc.alloc_semaphore(f"{tg}_twt")
        s_e = nc.alloc_semaphore(f"{tg}_twe")
        s_o = nc.alloc_semaphore(f"{tg}_two")
        ngrp_tot = sum(j[1] for j in jobs)
        with nc.Block(f"{tg}_tw") as blk:
            @blk.gpsimd
            def _(g):
                gi = 0
                for (wd, ngrp, dst) in jobs:
                    for rg in range(ngrp):
                        if gi >= 2:
                            g.wait_ge(s_t, (gi - 1) * KD)
                        g.dma_start(wrow[:, gi % 2, :],
                                    wd[rg * P:(rg + 1) * P, :]
                                    ).then_inc(s_l[gi % 2], 16)
                        gi += 1
                g.wait_ge(s_l[0], 16 * ((gi + 1) // 2))
                g.wait_ge(s_l[1], 16 * (gi // 2))
            @blk.tensor
            def _(te):
                gi = 0
                for (wd, ngrp, dst) in jobs:
                    for rg in range(ngrp):
                        te.wait_ge(s_l[gi % 2], 16 * (gi // 2 + 1))
                        for c in range(KD):
                            j = gi * KD + c
                            if j >= 2:
                                te.wait_ge(s_e, j - 1)
                            te.transpose(ptp[:, (j % 2) * P:(j % 2 + 1) * P],
                                         wrow[:, gi % 2, c * P:(c + 1) * P],
                                         p.sb["ident_bf"][:, :]).then_inc(s_t, 1)
                        gi += 1
            @blk.scalar
            def _(act):
                act = EA(act)
                gi = 0
                for (wd, ngrp, dst) in jobs:
                    for rg in range(ngrp):
                        for c in range(KD):
                            j = gi * KD + c
                            act.wait_ge(s_t, j + 1)
                            if gi >= 1:
                                act.wait_ge(s_o, 16 * gi)
                            act.activation(wTs[:, c, :],
                                           ptp[:, (j % 2) * P:(j % 2 + 1) * P],
                                           AF.Copy).then_inc(s_e, 1)
                        gi += 1
            @blk.sync
            def _(sy):
                gi = 0
                for (wd, ngrp, dst) in jobs:
                    dv = dst[:, :].rearrange("(k p) c -> p k c", p=P)
                    for rg in range(ngrp):
                        sy.wait_ge(s_e, (gi + 1) * KD)
                        sy.dma_start(dv[:, :, rg * P:(rg + 1) * P],
                                     wTs[:, :, :]).then_inc(s_o, 16)
                        gi += 1
                sy.wait_ge(s_o, 16 * gi)


def emit_attn(p: Prog, layer, x_in, x_out, w, scratch):
    nc, T = p.nc, p.T
    C1 = 128
    NTC = C1 // P
    NC1 = T // C1
    SB = 512 if T % 512 == 0 else 256
    NSB = T // SB
    NKT = T // P
    NPASS = 8 if NKT >= 8 else (4 if NKT >= 4 else 2)
    NH = 16
    x_in_v = fmv(x_in)
    x_out_v = fmv(x_out)
    is2 = layer == "l2"
    tg = layer + "a"
    cnt = {}

    def sem(name):
        if name not in cnt:
            cnt[name] = [nc.alloc_semaphore(f"{tg}_{name}"), 0]
        return cnt[name][0]

    def val(name):
        sem(name)
        return cnt[name][1]

    def bump(name, n):
        sem(name)
        cnt[name][1] += n
        return cnt[name][1]

    with ExitStack() as st:
        E = st.enter_context
        k_fm = E(nc.sbuf_tensor(f"{tg}_kfm", [P, KD, T], BF16))
        v_aug = E(nc.sbuf_tensor(f"{tg}_vaug", [P, NKT, NH * 65], BF16))
        k2m = E(nc.sbuf_tensor(f"{tg}_k2m", [2, KD], F32))
        esel = E(nc.sbuf_tensor(f"{tg}_esel", [2, 2, P], F32))
        bcol = E(nc.sbuf_tensor(f"{tg}_bcol", [P, KD, 4], F32))
        sqk = E(nc.sbuf_tensor(f"{tg}_sqk", [2, KD], F32))
        if is2:
            invf, sgn = _emit_rope_consts(p, st, tg)
            wnorm = None
        else:
            wnorm = E(nc.sbuf_tensor(f"{tg}_wn", [P, 2], F32))
            invf = sgn = None
        pb = [E(nc.psum_tensor(f"{tg}_pb{z}", [P, 512], F32)) for z in range(4)]

        with nc.sbuf_tensor(f"{tg}_ii", [2, 2, P], I32) as ii, \
             nc.sbuf_tensor(f"{tg}_ij", [2, 2, P], I32) as ij:
            with nc.Block(f"{tg}_pc1") as blk:
                @blk.gpsimd
                def _(g):
                    g.iota(ii[:, :, :], [[1, 2], [0, P]], channel_multiplier=0)
                    g.iota(ij[:, :, :], [[0, 2], [0, P]], channel_multiplier=1)
                @blk.vector
                def _(v):
                    v = EA(v)
                    v.memset(v_aug[:, :, :], 0.0)
                    v.memset(k2m[:, :], 0.0)
                @blk.sync
                def _(sy):
                    spr = sem("pr")
                    if not is2:
                        for half in range(2):
                            sy.dma_start(wnorm[half * 64:(half + 1) * 64, 0:1],
                                         w["q_norm_w"][:]).then_inc(spr, 16)
                            sy.dma_start(wnorm[half * 64:(half + 1) * 64, 1:2],
                                         w["k_norm_w"][:]).then_inc(spr, 16)
                        bump("pr", 64)
                        for m in range(KD):
                            sy.dma_start(bcol[:, m, 2:3],
                                         w["proj_b"][m * P:(m + 1) * P]
                                         ).then_inc(spr, 16)
                        bump("pr", 16 * KD)
                    else:
                        for m in range(KD):
                            for ci, nm in ((0, "bk"), (1, "bv"), (2, "bo"),
                                           (3, "bq")):
                                sy.dma_start(bcol[:, m, ci:ci + 1],
                                             w[nm][m * P:(m + 1) * P]
                                             ).then_inc(spr, 16)
                        bump("pr", 16 * KD * 4)
                    sy.wait_ge(spr, val("pr"))
            with nc.Block(f"{tg}_pc2") as blk:
                @blk.vector
                def _(v):
                    v = EA(v)
                    v.tensor_tensor(esel[:, :, :], ii[:, :, :], ij[:, :, :],
                                    ALU.is_equal)
                    v.memset(v_aug[:, :, :].rearrange(
                        "p n (h c) -> p n h c", c=65)[:, :, :, 64:65], 1.0)

        if not is2:
            _emit_wT_prep(p, tg, [(w["qkv_w"], 24, scratch["qkvT"]),
                                  (w["proj_w"], 8, scratch["pwT"])])

        def trig_build_v(v, trig_t, iot_t, itf_t):
            v.tensor_copy(itf_t[:, :], iot_t[:, :])
            v.tensor_scalar(trig_t[:, 0, :], itf_t[:, :], invf[:, :],
                            None, ALU.mult)
            v.tensor_scalar(trig_t[:, 0, :], trig_t[:, 0, :],
                            1.0 / (2 * math.pi), None, ALU.mult)
            v.tensor_scalar(trig_t[:, 1, :], trig_t[:, 0, :], 0.25,
                            None, ALU.add)
            for z in range(2):
                v.tensor_copy(iot_t[:, :], trig_t[:, z, :])
                v.tensor_copy(itf_t[:, :], iot_t[:, :])
                v.tensor_tensor(trig_t[:, z, :], trig_t[:, z, :],
                                itf_t[:, :], ALU.subtract)
                v.drain()
                v.tensor_scalar(trig_t[:, z, :], trig_t[:, z, :], 0.5, None,
                                ALU.subtract)

        def trig_build_act(act, trig_t, cos_t, sin_t):
            act.activation(sin_t[:, :], trig_t[:, 0, :], AF.Sin,
                           scale=2 * math.pi)
            act.activation(cos_t[:, :], trig_t[:, 1, :], AF.Sin,
                           scale=2 * math.pi)

        def emit_proj_norm(CW, NCW, get_iot, get_trig, get_cos, get_sin,
                           wten, woff, bias_ci, wn_ci, out_fm, out_range_of,
                           nmax, lnb_t, raw_t, sq_t, rr_t, rot_t, n2t,
                           blkname, wcol_of=None, pe_gate=None,
                           sync_body=None, gp_body=None):
            if wcol_of is None:
                wcol_of = lambda m: woff + m * P
            with nc.Block(blkname + "p") as blk:
                @blk.tensor
                def _(te):
                    for m in range(KD):
                        if pe_gate is not None:
                            pe_gate(te, m)
                        if m >= 2:
                            te.wait_ge(sem("ke"), val("ke") + m - 1)
                        last = None
                        for kk in range(KD):
                            c0 = wcol_of(m)
                            last = te.matmul(
                                pb[m % 2][:, 0:CW],
                                wten[:, kk, c0:c0 + P],
                                lnb_t[:, kk, 0:CW], start=(kk == 0),
                                stop=(kk == KD - 1))
                        last.then_inc(sem("km"), 1)
                if sync_body is not None:
                    @blk.sync
                    def _(sy):
                        sync_body(sy)
                if gp_body is not None:
                    @blk.gpsimd
                    def _(g):
                        gp_body(g)
                @blk.scalar
                def _(act):
                    act = EA(act)
                    for m in range(KD):
                        act.wait_ge(sem("km"), val("km") + m + 1)
                        kw = {}
                        fn = AF.Copy
                        if is2 and bias_ci is not None:
                            kw["bias"] = bcol[:, m, bias_ci:bias_ci + 1]
                            fn = AF.Identity
                        act.activation(raw_t[:, m, 0:CW],
                                       pb[m % 2][:, 0:CW], fn,
                                       **kw).then_inc(sem("ke"), 1)
                bump("km", KD)
                bump("ke", KD)

            with nc.Block(blkname + "r") as blk:
                @blk.scalar
                def _(act):
                    act = EA(act)
                    for m in range(KD):
                        if not is2:
                            if m > 0:
                                act.wait_ge(sem("nm"), val("nm") + 3 * m)
                            act.activation(sq_t[:, 0:CW], raw_t[:, m, 0:CW],
                                           AF.Square).then_inc(sem("na"), 1)
                            act.wait_ge(sem("nv"), val("nv") + 3 * m + 1)
                            if m > 0:
                                act.wait_ge(sem("nm"), val("nm") + 3 * m - 1)
                            act.activation(rr_t[:, 0:CW], rr_t[:, CW:2 * CW],
                                           AF.Sqrt).then_inc(sem("na"), 1)
                        else:
                            if m > 0:
                                act.wait_ge(sem("nm"), val("nm") + 2 * m)
                        act.wait_ge(sem("nv"),
                                    val("nv") + (2 if is2 else 3) * m
                                    + (1 if is2 else 2))
                        act.activation(sq_t[:, 0:CW], out_range_of(m),
                                       AF.Square).then_inc(sem("na"), 1)
                @blk.tensor
                def _(te):
                    for m in range(KD):
                        if is2:
                            if m > 0:
                                te.wait_ge(sem("nv"), val("nv") + 2 * m - 1)
                            te.matmul(pb[2][:, 0:CW], p.sb["swap_bf"][:, :],
                                      raw_t[:, m, 0:CW], start=True,
                                      stop=True).then_inc(sem("nm"), 1)
                            te.wait_ge(sem("na"), val("na") + m + 1)
                            if m > 0:
                                te.wait_ge(sem("nv"), val("nv") + 2 * m)
                            te.matmul(pb[3][0:2, 0:CW], p.sb["bones_bf"][:, :],
                                      sq_t[:, 0:CW], start=True,
                                      stop=True).then_inc(sem("nm"), 1)
                        else:
                            te.wait_ge(sem("na"), val("na") + 3 * m + 1)
                            if m > 0:
                                te.wait_ge(sem("nv"), val("nv") + 3 * m)
                            te.matmul(pb[3][0:2, 0:CW], p.sb["bones_bf"][:, :],
                                      sq_t[:, 0:CW], start=True,
                                      stop=True).then_inc(sem("nm"), 1)
                            te.wait_ge(sem("na"), val("na") + 3 * m + 2)
                            if m > 0:
                                te.wait_ge(sem("nv"), val("nv") + 3 * m + 1)
                            te.matmul(pb[2][:, 0:CW], p.sb["bones2_f32"][:, :],
                                      rr_t[0:2, 0:CW], start=True,
                                      stop=True).then_inc(sem("nm"), 1)
                            te.wait_ge(sem("na"), val("na") + 3 * m + 3)
                            te.wait_ge(sem("nv"), val("nv") + 3 * m + 2)
                            te.matmul(pb[3][0:2, 0:CW], p.sb["bones_bf"][:, :],
                                      sq_t[:, 0:CW], start=True,
                                      stop=True).then_inc(sem("nm"), 1)
                @blk.vector
                def _(v):
                    v = EA(v)
                    for m in range(KD):
                        if is2:
                            v.wait_ge(sem("nm"), val("nm") + 2 * m + 1)
                            v.tensor_tensor(rot_t[:, 0:CW], pb[2][:, 0:CW],
                                            get_sin()[:, 0:CW], ALU.mult)
                            v.tensor_scalar(rot_t[:, 0:CW], rot_t[:, 0:CW],
                                            sgn[:, :], None, ALU.mult)
                            v.tensor_tensor(out_range_of(m), raw_t[:, m, 0:CW],
                                            get_cos()[:, 0:CW], ALU.mult)
                            v.drain()
                            v.scalar_tensor_tensor(
                                out_range_of(m), out_range_of(m), -1.0,
                                rot_t[:, 0:CW], ALU.mult,
                                ALU.subtract).then_inc(sem("nv"), 1)
                            v.wait_ge(sem("nm"), val("nm") + 2 * m + 2)
                            v.reduce_max(n2t[:, 0:1], pb[3][0:2, 0:CW],
                                         axis=AX.X)
                            v.tensor_tensor(nmax[:, m:m + 1], nmax[:, m:m + 1],
                                            n2t[:, 0:1],
                                            ALU.max).then_inc(sem("nv"), 1)
                        else:
                            v.wait_ge(sem("nm"), val("nm") + 3 * m + 1)
                            v.tensor_scalar(rr_t[:, CW:2 * CW],
                                            pb[3][0:2, 0:CW], 1.0 / 64.0,
                                            1e-6, ALU.mult, ALU.add)
                            v.drain()
                            v.reciprocal(rr_t[:, CW:2 * CW],
                                         rr_t[:, CW:2 * CW]
                                         ).then_inc(sem("nv"), 1)
                            v.wait_ge(sem("nm"), val("nm") + 3 * m + 2)
                            v.scalar_tensor_tensor(
                                out_range_of(m), raw_t[:, m, 0:CW],
                                wnorm[:, wn_ci:wn_ci + 1], pb[2][:, 0:CW],
                                ALU.mult, ALU.mult).then_inc(sem("nv"), 1)
                            v.wait_ge(sem("nm"), val("nm") + 3 * m + 3)
                            v.reduce_max(n2t[:, 0:1], pb[3][0:2, 0:CW],
                                         axis=AX.X)
                            v.tensor_tensor(nmax[:, m:m + 1], nmax[:, m:m + 1],
                                            n2t[:, 0:1],
                                            ALU.max).then_inc(sem("nv"), 1)
                bump("nm", (2 if is2 else 3) * KD)
                bump("na", (1 if is2 else 3) * KD)
                bump("nv", (2 if is2 else 3) * KD)

        with ExitStack() as st1:
            E1 = st1.enter_context
            kvw = E1(nc.sbuf_tensor(f"{tg}_kvw", [P, KD, 2048], BF16))
            xc = E1(nc.sbuf_tensor(f"{tg}_xc1", [P, KD, C1], F32))
            lnb = E1(nc.sbuf_tensor(f"{tg}_lnb1", [P, KD, C1], BF16))
            kraw = E1(nc.sbuf_tensor(f"{tg}_kraw", [P, KD, C1], BF16))
            sqb = E1(nc.sbuf_tensor(f"{tg}_sqb", [P, C1], BF16))
            rrb = E1(nc.sbuf_tensor(f"{tg}_rrb", [2, 2 * C1], F32))
            vtmp = E1(nc.sbuf_tensor(f"{tg}_vtmp", [P, C1], BF16))
            k2t = E1(nc.sbuf_tensor(f"{tg}_k2t", [2, 1], F32))
            rot1 = E1(nc.sbuf_tensor(f"{tg}_rot1", [P, C1], F32))
            if is2:
                cosb = E1(nc.sbuf_tensor(f"{tg}_cosb", [P, C1], F32))
                sinb = E1(nc.sbuf_tensor(f"{tg}_sinb", [P, C1], F32))
                trig = E1(nc.sbuf_tensor(f"{tg}_trig", [P, 2, C1], F32))
                iot = E1(nc.sbuf_tensor(f"{tg}_iot", [P, C1], I32))
                itf = E1(nc.sbuf_tensor(f"{tg}_itf", [P, C1], F32))
            else:
                cosb = sinb = trig = iot = itf = None
            ln = LnFm(p, st1, f"{tg}ln1", C1)
            ps_tp = BankedPsum(st1, nc, f"{tg}_pstp", 2, P, P, BF16)

            with nc.Block(f"{tg}_p1w") as blk:
                @blk.gpsimd
                def _(g):
                    if is2:
                        g.dma_start(kvw[:, :, 0:1024],
                                    w["Wk"][:, :].rearrange(
                                        "(k p) c -> p k c", p=P)
                                    ).then_inc(sem("wg"), 16)
                        g.dma_start(kvw[:, :, 1024:2048],
                                    w["Wv"][:, :].rearrange(
                                        "(k p) c -> p k c", p=P)
                                    ).then_inc(sem("wg"), 16)
                        g.wait_ge(sem("wg"), bump("wg", 32))
                @blk.sync
                def _(sy):
                    if not is2:
                        sy.dma_start(kvw[:, :, :],
                                     scratch["qkvT"][:, 1024:3072].rearrange(
                                         "(k p) c -> p k c", p=P)
                                     ).then_inc(sem("wl"), 16)
                        sy.wait_ge(sem("wl"), bump("wl", 16))

            for i in range(NC1):
                with nc.Block(f"{tg}_p1l{i}") as blk:
                    @blk.sync
                    def _(sy, i=i):
                        sy.dma_start(xc[:, :, :],
                                     x_in_v[:, :, i * C1:(i + 1) * C1]
                                     ).then_inc(sem("x1"), 16)
                        sy.wait_ge(sem("x1"), bump("x1", 16))
                    if is2:
                        @blk.gpsimd
                        def _(g, i=i):
                            g.iota(iot[:, :], [[1, C1]], base=i * C1,
                                   channel_multiplier=0)
                with nc.Block(f"{tg}_p1n{i}") as blk:
                    @blk.vector
                    def _(v, i=i):
                        v = EA(v)
                        if is2:
                            trig_build_v(v, trig, iot, itf)
                            v.engine_nop().then_inc(sem("tv"), 1)
                        ln.v_emit(v, i, xc[:, :, :], lnb)
                    @blk.scalar
                    def _(act, i=i):
                        act = EA(act)
                        if is2:
                            act.wait_ge(sem("tv"), val("tv") + 1)
                            trig_build_act(act, trig, cosb, sinb)
                        ln.act_emit(act, i, xc[:, :, :])
                    @blk.tensor
                    def _(te, i=i):
                        ln.pe_emit(te, i, p.sb["ones_col_bf"][:, :],
                                   p.sb["ones_row_f32"][:, :])
                if is2:
                    bump("tv", 1)

                emit_proj_norm(
                    C1, NC1, None, None, (lambda: cosb), (lambda: sinb),
                    kvw, 0, 0, 1, k_fm,
                    (lambda m, i=i: k_fm[:, m, i * C1:(i + 1) * C1]),
                    k2m, lnb, kraw, sqb, rrb, rot1, k2t, f"{tg}_p1k{i}")

                with nc.Block(f"{tg}_p1v{i}") as blk:
                    @blk.tensor
                    def _(te, i=i):
                        tpj = 0
                        for m in range(KD):
                            if m >= 1:
                                te.wait_ge(sem("ve"),
                                           val("ve") + (1 + NTC) * m - NTC)
                            last = None
                            for kk in range(KD):
                                last = te.matmul(
                                    pb[m % 2][:, 0:C1],
                                    kvw[:, kk, 1024 + m * P:1024 + (m + 1) * P],
                                    lnb[:, kk, :], start=(kk == 0),
                                    stop=(kk == KD - 1))
                            last.then_inc(sem("vm"), 1)
                            for tc in range(NTC):
                                j = NTC * m + tc
                                te.wait_ge(sem("ve"),
                                           val("ve") + (1 + NTC) * m + 1)
                                if j >= 2:
                                    te.wait_ge(sem("ve"),
                                               val("ve")
                                               + (1 + NTC) * ((j - 2) // NTC)
                                               + 2 + ((j - 2) % NTC))
                                te.transpose(
                                    ps_tp[:, (j % 2) * P:(j % 2 + 1) * P],
                                    vtmp[:, tc * P:(tc + 1) * P],
                                    p.sb["ident_bf"][:, :]).then_inc(sem("vt"), 1)
                    @blk.scalar
                    def _(act, i=i):
                        act = EA(act)
                        for m in range(KD):
                            act.wait_ge(sem("vm"), val("vm") + m + 1)
                            kw = {}
                            fn = AF.Copy
                            if is2:
                                kw["bias"] = bcol[:, m, 1:2]
                                fn = AF.Identity
                            act.activation(vtmp[:, :], pb[m % 2][:, 0:C1],
                                           fn, **kw).then_inc(sem("ve"), 1)
                            for tc in range(NTC):
                                j = NTC * m + tc
                                act.wait_ge(sem("vt"), val("vt") + j + 1)
                                sl = ps_tp[:, (j % 2) * P:(j % 2 + 1) * P]
                                act.activation(
                                    v_aug[:, i * NTC + tc,
                                          (2 * m) * 65:(2 * m) * 65 + 64],
                                    sl[:, 0:64], AF.Copy)
                                act.activation(
                                    v_aug[:, i * NTC + tc,
                                          (2 * m + 1) * 65:(2 * m + 1) * 65 + 64],
                                    sl[:, 64:128], AF.Copy).then_inc(sem("ve"), 1)
                    bump("vm", KD)
                    bump("vt", NTC * KD)
                    bump("ve", (1 + NTC) * KD)

        with nc.Block(f"{tg}_kb") as blk:
            @blk.scalar
            def _(act):
                act.activation(sqk[:, :], k2m[:, :], AF.Sqrt)

        with ExitStack() as st2:
            E2 = st2.enter_context
            qw = E2(nc.sbuf_tensor(f"{tg}_qw", [P, KD, 512], BF16))
            ow = qw
            xc2 = E2(nc.sbuf_tensor(f"{tg}_xc2", [P, KD, SB], F32))
            lnb2 = E2(nc.sbuf_tensor(f"{tg}_lnb2", [P, KD, SB], BF16))
            qraw = E2(nc.sbuf_tensor(f"{tg}_qraw", [P, KD, SB], BF16))
            q_fm = qraw
            sqb2 = E2(nc.sbuf_tensor(f"{tg}_sqb2", [P, SB], BF16))
            rrb2 = (None if is2 else
                    E2(nc.sbuf_tensor(f"{tg}_rrb2", [2, 2 * SB], F32)))
            rot2 = E2(nc.sbuf_tensor(f"{tg}_rot2", [P, SB], F32))
            q2t = E2(nc.sbuf_tensor(f"{tg}_q2t", [2, 1], F32))
            q2m = E2(nc.sbuf_tensor(f"{tg}_q2m", [2, KD], F32))
            nbv = E2(nc.sbuf_tensor(f"{tg}_nbv", [2, KD], F32))
            nbc = E2(nc.sbuf_tensor(f"{tg}_nbc", [P, 1], F32))
            probsT = E2(nc.sbuf_tensor(f"{tg}_pT",
                                       [P, max(NKT // NPASS, 1), SB], BF16))
            ctx_fm = lnb2
            rrow = E2(nc.sbuf_tensor(f"{tg}_rrow", [1, SB], F32))
            obuf = xc2
            if is2:
                cosq = E2(nc.sbuf_tensor(f"{tg}_cosq", [P, SB], F32))
                sinq = E2(nc.sbuf_tensor(f"{tg}_sinq", [P, SB], F32))
                trigq = E2(nc.sbuf_tensor(f"{tg}_trigq", [P, 2, SB], F32))
                iotq = E2(nc.sbuf_tensor(f"{tg}_iotq", [P, SB], I32))
                itfq = rot2
            else:
                cosq = sinq = trigq = iotq = itfq = None
            pb.append(E2(nc.psum_tensor(f"{tg}_pb4", [P, 512], F32)))
            ln2 = LnFm(p, st2, f"{tg}ln2", C1)
            NC2 = SB // C1

            for sbi in range(NSB):
                qb = val("qh")
                with nc.Block(f"{tg}_p2l{sbi}") as blk:
                    @blk.sync
                    def _(sy, sbi=sbi):
                        sy.dma_start(xc2[:, :, :],
                                     x_in_v[:, :, sbi * SB:(sbi + 1) * SB]
                                     ).then_inc(sem("x2"), 16)
                        sy.wait_ge(sem("x2"), bump("x2", 16))
                        if not is2:
                            sy.dma_start(qw[:, :, :],
                                         scratch["qkvT"][:, 0:512].rearrange(
                                             "(k p) c -> p k c", p=P)
                                         ).then_inc(sem("qh"), 16)
                    @blk.gpsimd
                    def _(g, sbi=sbi):
                        if is2:
                            g.dma_start(qw[:, :, :],
                                        w["Wq"][:, 0:512].rearrange(
                                            "(k p) c -> p k c", p=P)
                                        ).then_inc(sem("qh"), 16)
                            g.iota(iotq[:, :], [[1, SB]], base=sbi * SB,
                                   channel_multiplier=0)
                    @blk.vector
                    def _(v, sbi=sbi):
                        v.memset(q2m[:, :], 0.0)

                with nc.Block(f"{tg}_p2n{sbi}") as blk:
                    @blk.vector
                    def _(v, sbi=sbi):
                        v = EA(v)
                        if is2:
                            trig_build_v(v, trigq, iotq, itfq)
                            v.engine_nop().then_inc(sem("tv"), 1)
                        for cc in range(NC2):
                            ln2.v_emit(v, sbi * NC2 + cc,
                                       xc2[:, :, cc * C1:(cc + 1) * C1],
                                       lnb2[:, :, cc * C1:(cc + 1) * C1])
                    @blk.scalar
                    def _(act, sbi=sbi):
                        act = EA(act)
                        if is2:
                            act.wait_ge(sem("tv"), val("tv") + 1)
                            trig_build_act(act, trigq, cosq, sinq)
                        for cc in range(NC2):
                            ln2.act_emit(act, sbi * NC2 + cc,
                                         xc2[:, :, cc * C1:(cc + 1) * C1])
                    @blk.tensor
                    def _(te, sbi=sbi):
                        for cc in range(NC2):
                            ln2.pe_emit(te, sbi * NC2 + cc,
                                        p.sb["ones_col_bf"][:, :],
                                        p.sb["ones_row_f32"][:, :])
                if is2:
                    bump("tv", 1)

                def _q_gate(te, m, qb=qb):
                    if m == 0:
                        te.wait_ge(sem("qh"), qb + 16)
                    elif m == 4:
                        te.wait_ge(sem("qh"), qb + 32)

                def _q_half1(eng, qb=qb):
                    eng.wait_ge(sem("km"), val("km") + 4)
                    src = (w["Wq"][:, 512:1024] if is2
                           else scratch["qkvT"][:, 512:1024])
                    eng.dma_start(qw[:, :, :],
                                  src.rearrange("(k p) c -> p k c", p=P)
                                  ).then_inc(sem("qh"), 16)

                emit_proj_norm(
                    SB, NSB, None, None, (lambda: cosq), (lambda: sinq),
                    qw, 0, 3, 0, q_fm,
                    (lambda m: q_fm[:, m, :]),
                    q2m, lnb2, qraw, sqb2, rrb2, rot2, q2t, f"{tg}_p2q{sbi}",
                    wcol_of=(lambda m: (m % 4) * P), pe_gate=_q_gate,
                    sync_body=(None if is2 else _q_half1),
                    gp_body=(_q_half1 if is2 else None))

                with nc.Block(f"{tg}_p2b{sbi}") as blk:
                    @blk.scalar
                    def _(act, sbi=sbi):
                        act.activation(nbv[:, :], q2m[:, :], AF.Sqrt)
                with nc.Block(f"{tg}_p2b2{sbi}") as blk:
                    @blk.vector
                    def _(v, sbi=sbi):
                        v = EA(v)
                        v.tensor_tensor(nbv[:, :], nbv[:, :], sqk[:, :],
                                        ALU.mult)
                        v.drain()
                        v.tensor_scalar(nbv[:, :], nbv[:, :], -0.125, None,
                                        ALU.mult)

                NKH = NKT // NPASS
                with nc.Block(f"{tg}_p2f{sbi}") as blk:
                    @blk.tensor
                    def _(te, sbi=sbi):
                        for h in range(NH):
                            hp, hm = h % 2, h // 2
                            if h > 0:
                                te.wait_ge(sem("fa"), val("fa") + 2 * h)
                            te.matmul(pb[2][:, 0:1], esel[:, hp, :],
                                      nbv[:, hm:hm + 1], start=True,
                                      stop=True).then_inc(sem("fm"), 1)
                            for half in range(NPASS):
                                for g2 in range(NKH):
                                    kt = half * NKH + g2
                                    if kt >= 2:
                                        te.wait_ge(sem("fe"),
                                                   val("fe") + h * NKT
                                                   + kt - 1)
                                    if half > 0 and g2 == 0:
                                        te.wait_ge(sem("fc"),
                                                   val("fc") + h * NKT
                                                   + half * NKH)
                                    te.matmul(
                                        pb[kt % 2][:, 0:SB],
                                        k_fm[hp * 64:hp * 64 + 64, hm,
                                             kt * P:(kt + 1) * P],
                                        q_fm[hp * 64:hp * 64 + 64, hm, :],
                                        start=True,
                                        stop=True).then_inc(sem("fm"), 1)
                                te.wait_ge(sem("fe"),
                                           val("fe") + h * NKT
                                           + half * NKH + NKH)
                                if half == 0 and h > 0:
                                    te.wait_ge(sem("fv"), val("fv") + 2 * h - 1)
                                for kt2 in range(NKH):
                                    kt = half * NKH + kt2
                                    te.matmul(
                                        pb[4][0:65, 0:SB],
                                        v_aug[:, kt, h * 65:(h + 1) * 65],
                                        probsT[:, kt2, :],
                                        start=(half == 0 and kt2 == 0),
                                        stop=(half == NPASS - 1
                                              and kt2 == NKH - 1)
                                        ).then_inc(sem("fc"), 1)
                            te.wait_ge(sem("fv"), val("fv") + 2 * h + 1)
                            te.matmul(pb[3][0:64, 0:SB],
                                      p.sb["ones_row_f32"][0:1, 0:64],
                                      rrow[:, :], start=True,
                                      stop=True).then_inc(sem("fm"), 1)
                    @blk.scalar
                    def _(act, sbi=sbi):
                        act = EA(act)
                        for h in range(NH):
                            act.wait_ge(sem("fm"), val("fm") + h * (NKT + 2) + 1)
                            act.activation(nbc[:, :], pb[2][:, 0:1],
                                           AF.Copy).then_inc(sem("fa"), 1)
                            for half in range(NPASS):
                                for g2 in range(NKH):
                                    kt = half * NKH + g2
                                    act.wait_ge(sem("fm"),
                                                val("fm") + h * (NKT + 2) + 1
                                                + kt + 1)
                                    act.activation(
                                        probsT[:, g2, :], pb[kt % 2][:, 0:SB],
                                        AF.Exp, bias=nbc[:, :],
                                        scale=0.125).then_inc(sem("fe"), 1)
                            act.wait_ge(sem("fc"), val("fc") + (h + 1) * NKT)
                            act.activation(
                                ctx_fm[(h % 2) * 64:(h % 2) * 64 + 64,
                                       h // 2, :],
                                pb[4][0:64, 0:SB], AF.Copy).then_inc(sem("fa"), 1)
                    @blk.vector
                    def _(v, sbi=sbi):
                        v = EA(v)
                        for h in range(NH):
                            v.wait_ge(sem("fc"), val("fc") + (h + 1) * NKT)
                            v.tensor_scalar(rrow[:, :], pb[4][64:65, 0:SB],
                                            1e-30, None, ALU.add)
                            v.drain()
                            v.reciprocal(rrow[:, :],
                                         rrow[:, :]).then_inc(sem("fv"), 1)
                            v.wait_ge(sem("fa"), val("fa") + 2 * h + 2)
                            v.wait_ge(sem("fm"), val("fm") + (h + 1) * (NKT + 2))
                            v.tensor_tensor(
                                ctx_fm[(h % 2) * 64:(h % 2) * 64 + 64,
                                       h // 2, :],
                                ctx_fm[(h % 2) * 64:(h % 2) * 64 + 64,
                                       h // 2, :],
                                pb[3][0:64, 0:SB], ALU.mult).then_inc(sem("fv"), 1)
                    bump("fm", NH * (NKT + 2))
                    bump("fe", NH * NKT)
                    bump("fc", NH * NKT)
                    bump("fa", NH * 2)
                    bump("fv", NH * 2)

                with nc.Block(f"{tg}_p2ow{sbi}") as blk:
                    @blk.gpsimd
                    def _(g, sbi=sbi):
                        if is2:
                            g.dma_start(ow[:, :, :],
                                        w["Wo"][:, 0:512].rearrange(
                                            "(k p) c -> p k c", p=P)
                                        ).then_inc(sem("qh"), 16)
                    @blk.sync
                    def _(sy, sbi=sbi):
                        if not is2:
                            sy.dma_start(ow[:, :, :],
                                         scratch["pwT"][:, 0:512].rearrange(
                                             "(k p) c -> p k c", p=P)
                                         ).then_inc(sem("qh"), 16)
                with nc.Block(f"{tg}_p2o{sbi}") as blk:
                    @blk.tensor
                    def _(te, sbi=sbi, qb=qb):
                        for m in range(KD):
                            if m == 0:
                                te.wait_ge(sem("qh"), qb + 48)
                            elif m == 4:
                                te.wait_ge(sem("qh"), qb + 64)
                            if m >= 2:
                                te.wait_ge(sem("oe"), val("oe") + m - 1)
                            last = None
                            for kk in range(KD):
                                last = te.matmul(
                                    pb[m % 2][:, 0:SB],
                                    ow[:, kk, (m % 4) * P:(m % 4 + 1) * P],
                                    ctx_fm[:, kk, :], start=(kk == 0),
                                    stop=(kk == KD - 1))
                            last.then_inc(sem("om"), 1)
                    @blk.vector
                    def _(v, sbi=sbi):
                        v = EA(v)
                        for m in range(KD):
                            v.wait_ge(sem("om"), val("om") + m + 1)
                            v.scalar_tensor_tensor(
                                obuf[:, m, :], pb[m % 2][:, 0:SB],
                                bcol[:, m, 2:3], xc2[:, m, :], ALU.add,
                                ALU.add).then_inc(sem("oe"), 1)
                    if is2:
                        @blk.gpsimd
                        def _(g, sbi=sbi):
                            g.wait_ge(sem("om"), val("om") + 4)
                            g.dma_start(ow[:, :, :],
                                        w["Wo"][:, 512:1024].rearrange(
                                            "(k p) c -> p k c", p=P)
                                        ).then_inc(sem("qh"), 16)
                    @blk.sync
                    def _(sy, sbi=sbi):
                        if not is2:
                            sy.wait_ge(sem("om"), val("om") + 4)
                            sy.dma_start(ow[:, :, :],
                                         scratch["pwT"][:, 512:1024].rearrange(
                                             "(k p) c -> p k c", p=P)
                                         ).then_inc(sem("qh"), 16)
                        sy.wait_ge(sem("oe"), val("oe") + KD)
                        sy.dma_start(x_out_v[:, :, sbi * SB:(sbi + 1) * SB],
                                     obuf[:, :, :]).then_inc(sem("os"), 16)
                        sy.wait_ge(sem("os"), bump("os", 16))
                    bump("om", KD)
                    bump("oe", KD)
                    bump("qh", 64)





def emit_l1_attn(p: Prog, x_in, x_out, w_qkv, w_out, b_out):
    nc, T = p.nc, p.T
    CW = 512
    NCH = T // CW
    tg = "l1a"
    x_in_v = fmv(x_in)
    x_out_v = fmv(x_out)
    cnt = {}

    def sem(name):
        if name not in cnt:
            cnt[name] = [nc.alloc_semaphore(f"{tg}_{name}"), 0]
        return cnt[name][0]

    def val(name):
        sem(name)
        return cnt[name][1]

    def bump(name, n):
        sem(name)
        cnt[name][1] += n
        return cnt[name][1]

    PAIRS = [(i, j) for i in range(4) for j in range(4)]

    with ExitStack() as st:
        E = st.enter_context
        wqkv = E(nc.sbuf_tensor(f"{tg}_wqkv", [P, KD, 1024], BF16))
        wo = E(nc.sbuf_tensor(f"{tg}_wo", [P, KD, 1024], BF16))
        bor = E(nc.sbuf_tensor(f"{tg}_bor", [1, 1024], F32))
        cntr = E(nc.sbuf_tensor(f"{tg}_cntr", [1, 512], F32))
        bsel = E(nc.sbuf_tensor(f"{tg}_bsel", [P, KD, 16], BF16))
        bselT = E(nc.sbuf_tensor(f"{tg}_bselT", [16, KD, P], BF16))
        carry = E(nc.sbuf_tensor(f"{tg}_carry", [P, KD, 2], F32))

        with nc.sbuf_tensor(f"{tg}_ia", [P, KD, 16], I32) as ia, \
             nc.sbuf_tensor(f"{tg}_ib", [P, KD, 16], I32) as ib, \
             nc.sbuf_tensor(f"{tg}_ic", [16, KD, P], I32) as ic, \
             nc.sbuf_tensor(f"{tg}_id", [16, KD, P], I32) as idd, \
             nc.sbuf_tensor(f"{tg}_ih", [16, KD, P], I32) as ih, \
             nc.sbuf_tensor(f"{tg}_ip", [P, 1], I32) as ip, \
             nc.sbuf_tensor(f"{tg}_fa", [P, KD, 16], F32) as fa, \
             nc.sbuf_tensor(f"{tg}_fb", [P, KD, 16], F32) as fb, \
             nc.sbuf_tensor(f"{tg}_fp", [P, 1], F32) as fp, \
             nc.sbuf_tensor(f"{tg}_fc", [16, KD, P], F32) as fc, \
             nc.sbuf_tensor(f"{tg}_fd", [16, KD, P], F32) as fd, \
             nc.sbuf_tensor(f"{tg}_fh", [16, KD, P], F32) as fh:
            with nc.Block(f"{tg}_pp1") as blk:
                @blk.gpsimd
                def _(g):
                    g.iota(ia[:, :, :], [[2, KD], [0, 16]], channel_multiplier=0)
                    g.iota(ib[:, :, :], [[0, KD], [1, 16]], channel_multiplier=0)
                    g.iota(ic[:, :, :], [[2, KD], [0, P]], channel_multiplier=0)
                    g.iota(idd[:, :, :], [[0, KD], [1, P]], channel_multiplier=0)
                    g.iota(ih[:, :, :], [[0, KD], [0, P]], channel_multiplier=1)
                    g.iota(ip[:, :], [[0, 1]], channel_multiplier=1)
                    g.dma_start(wo[:, :, :],
                                w_out[:, :].rearrange("(k p) c -> p k c", p=P)
                                ).then_inc(sem("wg"), 16)
                    g.wait_ge(sem("wg"), bump("wg", 16))
                @blk.sync
                def _(sy):
                    sy.dma_start(bor[:, :], b_out[:]).then_inc(sem("wl"), 16)
                    sy.wait_ge(sem("wl"), bump("wl", 16))
            with nc.Block(f"{tg}_pp2") as blk:
                @blk.vector
                def _(v):
                    v = EA(v)
                    v.tensor_copy(fa[:, :, :], ia[:, :, :])
                    v.tensor_copy(fb[:, :, :], ib[:, :, :])
                    v.tensor_copy(fp[:, :], ip[:, :])
                    v.tensor_scalar(fp[:, :], fp[:, :], 64.0, None, ALU.is_ge)
                    v.tensor_scalar(fa[:, :, :], fa[:, :, :], fp[:, :], None,
                                    ALU.add)
                    v.tensor_tensor(fa[:, :, :], fa[:, :, :], fb[:, :, :],
                                    ALU.is_equal)
                    v.tensor_copy(bsel[:, :, :], fa[:, :, :])
                    v.tensor_copy(fc[:, :, :], ic[:, :, :])
                    v.tensor_copy(fd[:, :, :], idd[:, :, :])
                    v.tensor_copy(fh[:, :, :], ih[:, :, :])
                    v.tensor_scalar(fd[:, :, :], fd[:, :, :], 64.0, None,
                                    ALU.is_ge)
                    v.drain()
                    v.tensor_tensor(fc[:, :, :], fc[:, :, :], fd[:, :, :],
                                    ALU.add)
                    v.drain()
                    v.tensor_tensor(fc[:, :, :], fc[:, :, :], fh[:, :, :],
                                    ALU.is_equal)
                    v.tensor_copy(bselT[:, :, :], fc[:, :, :])
                    v.memset(carry[:, :, :], 0.0)

        with ExitStack() as st1:
            E1 = st1.enter_context
            TOKMAX = CW + 128
            xc = E1(nc.sbuf_tensor(f"{tg}_xc", [P, KD, TOKMAX], F32))
            lnb = E1(nc.sbuf_tensor(f"{tg}_lnb", [P, KD, TOKMAX], BF16))
            qf = E1(nc.sbuf_tensor(f"{tg}_qf", [P, KD, TOKMAX], BF16))
            kf = E1(nc.sbuf_tensor(f"{tg}_kf", [P, KD, TOKMAX], BF16))
            vf = E1(nc.sbuf_tensor(f"{tg}_vf", [P, KD, TOKMAX], BF16))
            prod = E1(nc.sbuf_tensor(f"{tg}_prod", [P, 2, CW // 2], BF16))
            dots = E1(nc.sbuf_tensor(f"{tg}_dots", [16, 16, CW // 2], F32))
            mx = E1(nc.sbuf_tensor(f"{tg}_mx", [16, CW // 2], F32))
            smx = E1(nc.sbuf_tensor(f"{tg}_smx", [16, 2, CW // 2], F32))
            probs = E1(nc.sbuf_tensor(f"{tg}_probs", [16, 16, CW // 2], BF16))
            acc = E1(nc.sbuf_tensor(f"{tg}_acc", [P, KD, CW + 2], F32))
            accb = lnb
            obuf = xc
            tmpv = E1(nc.sbuf_tensor(f"{tg}_tmpv", [P, CW // 2], F32))
            ln = LnFm(p, st1, f"{tg}ln", 128)
            pba = E1(nc.psum_tensor(f"{tg}_pba", [P, 512], F32))
            pbb = E1(nc.psum_tensor(f"{tg}_pbb", [P, 512], F32))
            pdots = [E1(nc.psum_tensor(f"{tg}_pd0", [16, CW // 2], F32)),
                     E1(nc.psum_tensor(f"{tg}_pd1", [16, CW // 2], F32))]

            for ci in range(NCH):
                TOK = TOKMAX if ci < NCH - 1 else CW
                NLN = TOK // 128
                WN = CW // 2 if ci < NCH - 1 else CW // 2 - 1
                t0 = ci * CW

                with nc.Block(f"{tg}_c{ci}l") as blk:
                    @blk.sync
                    def _(sy, ci=ci, TOK=TOK, t0=t0):
                        sy.dma_start(xc[:, :, 0:TOK],
                                     x_in_v[:, :, t0:t0 + TOK]
                                     ).then_inc(sem("x"), 16)
                        sy.wait_ge(sem("x"), bump("x", 16))
                    @blk.vector
                    def _(v, ci=ci):
                        v = EA(v)
                        v.memset(acc[:, :, :], 0.0)
                        v.drain()
                        v.tensor_copy(acc[:, :, 0:2], carry[:, :, :])
                        v.memset(cntr[:, :], 2.0)
                        if ci == 0:
                            v.drain()
                            v.memset(cntr[:, 0:2], 1.0)
                        if ci == NCH - 1:
                            v.drain()
                            v.memset(cntr[:, CW - 2:CW], 1.0)

                with nc.Block(f"{tg}_c{ci}n") as blk:
                    @blk.vector
                    def _(v, ci=ci, NLN=NLN):
                        v = EA(v)
                        for cc in range(NLN):
                            ln.v_emit(v, val("lnc") + cc,
                                      xc[:, :, cc * 128:(cc + 1) * 128],
                                      lnb[:, :, cc * 128:(cc + 1) * 128])
                    @blk.scalar
                    def _(act, ci=ci, NLN=NLN):
                        act = EA(act)
                        for cc in range(NLN):
                            ln.act_emit(act, val("lnc") + cc,
                                        xc[:, :, cc * 128:(cc + 1) * 128])
                    @blk.tensor
                    def _(te, ci=ci, NLN=NLN):
                        for cc in range(NLN):
                            ln.pe_emit(te, val("lnc") + cc,
                                       p.sb["ones_col_bf"][:, :],
                                       p.sb["ones_row_f32"][:, :])
                bump("lnc", NLN)

                NSEG = (TOK + 511) // 512
                with nc.Block(f"{tg}_c{ci}q") as blk:
                    @blk.gpsimd
                    def _(g, ci=ci, TOK=TOK, NSEG=NSEG):
                        for which in range(3):
                            if which > 0:
                                g.wait_ge(sem("qm"),
                                          val("qm") + which * KD * NSEG)
                            g.dma_start(
                                wqkv[:, :, :],
                                w_qkv[:, which * 1024:(which + 1) * 1024]
                                .rearrange("(k p) c -> p k c", p=P)
                                ).then_inc(sem("wq"), 16)
                        g.wait_ge(sem("wq"), bump("wq", 48))
                    @blk.tensor
                    def _(te, ci=ci, TOK=TOK, NSEG=NSEG):
                        nn = 0
                        for which in range(3):
                            te.wait_ge(sem("wq"), val("wq") - 48 + 16 * (which + 1))
                            for m in range(KD):
                                for n0 in range(0, TOK, 512):
                                    n1 = min(n0 + 512, TOK)
                                    if nn >= 2:
                                        te.wait_ge(sem("qe"), val("qe") + nn - 1)
                                    last = None
                                    for kk in range(KD):
                                        last = te.matmul(
                                            (pba if nn % 2 == 0 else pbb)[:, 0:n1 - n0],
                                            wqkv[:, kk, m * P:(m + 1) * P],
                                            lnb[:, kk, n0:n1], start=(kk == 0),
                                            stop=(kk == KD - 1))
                                    last.then_inc(sem("qm"), 1)
                                    nn += 1
                    @blk.scalar
                    def _(act, ci=ci, TOK=TOK):
                        act = EA(act)
                        nn = 0
                        for which in range(3):
                            dst = (qf, kf, vf)[which]
                            for m in range(KD):
                                for n0 in range(0, TOK, 512):
                                    n1 = min(n0 + 512, TOK)
                                    act.wait_ge(sem("qm"), val("qm") + nn + 1)
                                    act.activation(dst[:, m, n0:n1],
                                                   (pba if nn % 2 == 0 else pbb)[:, 0:n1 - n0],
                                                   AF.Copy).then_inc(sem("qe"), 1)
                                    nn += 1
                    nqkv = 3 * KD * ((TOK + 511) // 512)
                    bump("qm", nqkv)
                    bump("qe", nqkv)

                with nc.Block(f"{tg}_c{ci}d") as blk:
                    @blk.vector
                    def _(v, ci=ci, WN=WN):
                        v = EA(v)
                        for pi, (i, j) in enumerate(PAIRS):
                            for m in range(KD):
                                jj = pi * KD + m
                                if jj >= 2:
                                    v.wait_ge(sem("dm"), val("dm") + jj - 1)
                                qs = qf[:, m, :].rearrange(
                                    "p (w two) -> p w two", two=2)[
                                    :, i // 2:i // 2 + WN, i % 2]
                                ks = kf[:, m, :].rearrange(
                                    "p (w two) -> p w two", two=2)[
                                    :, j // 2:j // 2 + WN, j % 2]
                                v.tensor_tensor(prod[:, jj % 2, 0:WN], qs, ks,
                                                ALU.mult).then_inc(sem("dv"), 1)
                    @blk.tensor
                    def _(te, ci=ci, WN=WN):
                        for pi, (i, j) in enumerate(PAIRS):
                            for m in range(KD):
                                jj = pi * KD + m
                                te.wait_ge(sem("dv"), val("dv") + jj + 1)
                                if m == 0 and pi >= 2:
                                    te.wait_ge(sem("de"), val("de") + pi - 1)
                                te.matmul(pdots[pi % 2][:, 0:WN],
                                          bsel[:, m, :],
                                          prod[:, jj % 2, 0:WN],
                                          start=(m == 0),
                                          stop=(m == KD - 1)
                                          ).then_inc(sem("dm"), 1)
                    @blk.scalar
                    def _(act, ci=ci, WN=WN):
                        act = EA(act)
                        for pi in range(16):
                            act.wait_ge(sem("dm"), val("dm") + (pi + 1) * KD)
                            act.activation(dots[:, pi, 0:WN],
                                           pdots[pi % 2][:, 0:WN],
                                           AF.Copy).then_inc(sem("de"), 1)
                    bump("dv", 16 * KD)
                    bump("dm", 16 * KD)
                    bump("de", 16)

                with nc.Block(f"{tg}_c{ci}s") as blk:
                    @blk.vector
                    def _(v, ci=ci, WN=WN):
                        v = EA(v)
                        for i in range(4):
                            v.tensor_tensor(mx[:, 0:WN], dots[:, 4 * i, 0:WN],
                                            dots[:, 4 * i + 1, 0:WN], ALU.max)
                            v.drain()
                            v.tensor_tensor(mx[:, 0:WN], mx[:, 0:WN],
                                            dots[:, 4 * i + 2, 0:WN], ALU.max)
                            v.drain()
                            v.tensor_tensor(mx[:, 0:WN], mx[:, 0:WN],
                                            dots[:, 4 * i + 3, 0:WN], ALU.max)
                            for j in range(4):
                                v.drain()
                                v.tensor_tensor(dots[:, 4 * i + j, 0:WN],
                                                dots[:, 4 * i + j, 0:WN],
                                                mx[:, 0:WN], ALU.subtract)
                            v.engine_nop().then_inc(sem("sv"), 1)
                            v.wait_ge(sem("se"), val("se") + i + 1)
                            v.tensor_tensor(smx[:, 0, 0:WN],
                                            smx[:, 1, 0:WN]
                                            if False else dots[:, 4 * i, 0:WN],
                                            dots[:, 4 * i + 1, 0:WN], ALU.add)
                            v.drain()
                            v.tensor_tensor(smx[:, 0, 0:WN], smx[:, 0, 0:WN],
                                            dots[:, 4 * i + 2, 0:WN], ALU.add)
                            v.drain()
                            v.tensor_tensor(smx[:, 0, 0:WN], smx[:, 0, 0:WN],
                                            dots[:, 4 * i + 3, 0:WN], ALU.add)
                            v.drain()
                            v.reciprocal(smx[:, 1, 0:WN], smx[:, 0, 0:WN])
                            v.drain()
                            for j in range(4):
                                v.tensor_tensor(probs[:, 4 * i + j, 0:WN],
                                                dots[:, 4 * i + j, 0:WN],
                                                smx[:, 1, 0:WN],
                                                ALU.mult)
                            v.engine_nop().then_inc(sem("sv"), 1)
                    @blk.scalar
                    def _(act, ci=ci, WN=WN):
                        act = EA(act)
                        for i in range(4):
                            act.wait_ge(sem("sv"), val("sv") + 2 * i + 1)
                            act.activation(dots[:, 4 * i:4 * i + 4, 0:WN]
                                           if WN == 512 else
                                           dots[:, 4 * i:4 * i + 4, 0:WN],
                                           dots[:, 4 * i:4 * i + 4, 0:WN],
                                           AF.Exp,
                                           scale=1.0 / 32.0).then_inc(sem("se"), 1)
                    bump("sv", 8)
                    bump("se", 4)

                with nc.Block(f"{tg}_c{ci}a") as blk:
                    @blk.tensor
                    def _(te, ci=ci, WN=WN):
                        for pi, (i, j) in enumerate(PAIRS):
                            for m in range(KD):
                                jj = pi * KD + m
                                if jj >= 2:
                                    te.wait_ge(sem("av"), val("av") + 2 * (jj - 1))
                                te.matmul((pba if jj % 2 == 0 else pbb)[:, 0:WN],
                                          bselT[:, m, :], probs[:, pi, 0:WN],
                                          start=True,
                                          stop=True).then_inc(sem("am"), 1)
                    @blk.vector
                    def _(v, ci=ci, WN=WN):
                        v = EA(v)
                        for pi, (i, j) in enumerate(PAIRS):
                            for m in range(KD):
                                jj = pi * KD + m
                                v.wait_ge(sem("am"), val("am") + jj + 1)
                                vs = vf[:, m, :].rearrange(
                                    "p (w two) -> p w two", two=2)[
                                    :, j // 2:j // 2 + WN, j % 2]
                                v.tensor_tensor(tmpv[:, 0:WN],
                                                (pba if jj % 2 == 0 else pbb)[:, 0:WN],
                                                vs, ALU.mult).then_inc(sem("av"), 1)
                                asl = acc[:, m, :].rearrange(
                                    "p (w two) -> p w two", two=2)[
                                    :, i // 2:i // 2 + WN, i % 2]
                                v.drain()
                                v.tensor_tensor(asl, asl, tmpv[:, 0:WN],
                                                ALU.add).then_inc(sem("av"), 1)
                    bump("am", 16 * KD)
                    bump("av", 2 * 16 * KD)

                with nc.Block(f"{tg}_c{ci}o") as blk:
                    @blk.vector
                    def _(v, ci=ci):
                        v = EA(v)
                        v.tensor_copy(carry[:, :, :], acc[:, :, CW:CW + 2])
                        v.tensor_copy(accb[:, :, 0:CW], acc[:, :, 0:CW])
                        v.engine_nop().then_inc(sem("ov"), 1)
                        for m in range(KD):
                            oo = m
                            v.wait_ge(sem("om"), val("om") + oo + 1)
                            v.scalar_tensor_tensor(
                                xc[:, m, 0:CW],
                                (pba if oo % 2 == 0 else pbb)[:, 0:CW],
                                1.0, xc[:, m, 0:CW],
                                ALU.mult, ALU.add).then_inc(sem("oe"), 1)
                    @blk.tensor
                    def _(te, ci=ci, t0=t0):
                        te.wait_ge(sem("ov"), val("ov") + 1)
                        for m in range(KD):
                            oo = m
                            if oo >= 2:
                                te.wait_ge(sem("oe"), val("oe") + oo - 1)
                            ps = (pba if oo % 2 == 0 else pbb)
                            last = None
                            for kk in range(KD):
                                last = te.matmul(
                                    ps[:, 0:CW], wo[:, kk, m * P:(m + 1) * P],
                                    accb[:, kk, 0:CW],
                                    start=(kk == 0), stop=False)
                            te.matmul(ps[:, 0:CW],
                                      bor[:, m * P:(m + 1) * P],
                                      cntr[:, 0:CW],
                                      start=False,
                                      stop=True).then_inc(sem("om"), 1)
                    @blk.sync
                    def _(sy, ci=ci, t0=t0):
                        sy.wait_ge(sem("oe"), val("oe") + KD)
                        sy.dma_start(x_out_v[:, :, t0:t0 + CW],
                                     xc[:, :, 0:CW]).then_inc(sem("os"), 16)
                        sy.wait_ge(sem("os"), bump("os", 16))
                    bump("ov", 1)
                    bump("om", KD)
                    bump("oe", KD)



def emit_l2_ffn(p: Prog, x_in, x_out, g1r, g2r, g1i, g2i, br, bi, wtmp_dram):
    nc, T = p.nc, p.T
    C = 512
    NCHK = T // C

    with ExitStack() as st:
        E = st.enter_context
        wrT = E(nc.sbuf_tensor("l2f_wrT", [P, 4, 512], BF16))
        wiT = E(nc.sbuf_tensor("l2f_wiT", [P, 4, 512], BF16))
        wiTn = E(nc.sbuf_tensor("l2f_wiTn", [P, 4, 512], BF16))
        bias = E(nc.sbuf_tensor("l2f_bias", [P, 8], F32))

        with nc.sbuf_tensor("l2f_g1", [4, 1024], F32) as g1sb, \
             nc.sbuf_tensor("l2f_g2", [4, 256], F32) as g2sb, \
             nc.sbuf_tensor("l2f_m2", [P, 8, 256], F32) as m2sb, \
             nc.psum_tensor("l2f_ps_m1", [P, 256], F32) as ps_m1:
            sW = nc.alloc_semaphore("l2f_sW")
            s_ld = nc.alloc_semaphore("l2f_sld")
            s_gd = nc.alloc_semaphore("l2f_sgd")
            ldv = [0]
            gdv = [0]
            swv = [0]
            for wi, (g1d, g2d) in enumerate(((g1r, g2r), (g1i, g2i))):
                dstT = wrT if wi == 0 else wiT
                with nc.Block(f"l2f_wg{wi}") as blk:
                    @blk.sync
                    def _(sy, g1d=g1d, g2d=g2d):
                        sy.dma_start(g1sb[:, :],
                                     g1d[:, :, :, :].rearrange("a b c d -> (a c) b d")
                                     ).then_inc(s_ld, 16)
                        sy.dma_start(g2sb[:, :],
                                     g2d[:, :, :, :].rearrange("c e a g -> (a c) e g")
                                     ).then_inc(s_ld, 16)
                        ldv[0] += 32
                        sy.wait_ge(s_ld, ldv[0])
                base = swv[0]
                with nc.Block(f"l2f_wm{wi}") as blk:
                    @blk.tensor
                    def _(te, base=base):
                        for m in range(8):
                            if m > 0:
                                te.wait_ge(sW, base + 2 * m)
                            te.matmul(ps_m1[:, :], g1sb[:, m * P:(m + 1) * P],
                                      g2sb[:, :], start=True,
                                      stop=True).then_inc(sW, 1)
                    @blk.vector
                    def _(v, base=base):
                        v = EA(v)
                        for m in range(8):
                            v.wait_ge(sW, base + 2 * m + 1)
                            dst = m2sb[:, m, :].rearrange("p (g e) -> p e g", g=16)
                            v.tensor_copy(dst, ps_m1[:, :]).then_inc(sW, 1)
                swv[0] = base + 16
                with nc.Block(f"l2f_ww{wi}") as blk:
                    @blk.sync
                    def _(sy):
                        sy.dma_start(wtmp_dram[:, :].rearrange(
                            "(m p) n -> p m n", p=P), m2sb[:, :, :]).then_inc(sW, 16)
                        sy.wait_ge(sW, swv[0] + 16)
                swv[0] += 16
                with nc.Block(f"l2f_wr{wi}") as blk:
                    @blk.gpsimd
                    def _(g, dstT=dstT):
                        flat = wtmp_dram[:, :].rearrange("a b -> (a b)")
                        for q in range(4):
                            srcap = bass.AP(flat.tensor, q * 8 * 256,
                                            [[256, 8], [16, 16], [8192, 32], [1, 16]])
                            g.dma_start(dstT[:, q, :], srcap).then_inc(s_gd, 16)
                        gdv[0] += 64
                        g.wait_ge(s_gd, gdv[0])
        with nc.Block("l2f_neg") as blk:
            @blk.sync
            def _(sy):
                for m in range(4):
                    sy.dma_start(bias[:, m:m + 1],
                                 br[m * P:(m + 1) * P]).then_inc(s_ld, 16)
                    sy.dma_start(bias[:, 4 + m:5 + m],
                                 bi[m * P:(m + 1) * P]).then_inc(s_ld, 16)
                ldv[0] += 128
                sy.wait_ge(s_ld, ldv[0])
            @blk.vector
            def _(v):
                v.tensor_scalar(wiTn[:, :, :], wiT[:, :, :], -1.0, None, ALU.mult)

        xc = E(nc.sbuf_tensor("l2f_xc", [P, KD, C], F32))
        lnb = E(nc.sbuf_tensor("l2f_lnb", [P, KD, C], BF16))
        obuf = E(nc.sbuf_tensor("l2f_o", [P, KD, C], F32))
        ln = LnFm(p, st, "l2fln", C)
        ps_o = [E(nc.psum_tensor("l2f_ps_o0", [P, C], F32)),
                E(nc.psum_tensor("l2f_ps_o1", [P, C], F32))]
        s_x = nc.alloc_semaphore("l2f_sx")
        s_mm = nc.alloc_semaphore("l2f_smm")
        s_o = nc.alloc_semaphore("l2f_so")
        s_st = nc.alloc_semaphore("l2f_sst")
        x_in_v = fmv(x_in)
        x_out_v = fmv(x_out)

        def mm_groups(i):
            out = []
            for g in range(8):
                m = g % 4
                if g < 4:
                    ops = [(wrT, 0), (wiTn, 4)]
                else:
                    ops = [(wiT, 0), (wrT, 4)]
                out.append((g, m, ops))
            return out

        with nc.Block("l2f_main") as blk:
            @blk.sync
            def _(sy):
                sy.dma_start(xc[:, :, :], x_in_v[:, :, 0:C]).then_inc(s_x, 16)
                for i in range(NCHK):
                    if i + 1 < NCHK:
                        sy.wait_ge(s_o, (i + 1) * 8)
                        sy.dma_start(xc[:, :, :],
                                     x_in_v[:, :, (i + 1) * C:(i + 2) * C]
                                     ).then_inc(s_x, 16)
                    for g in range(8):
                        sy.wait_ge(s_o, i * 8 + g + 1)
                        sy.dma_start(x_out_v[:, g, i * C:(i + 1) * C],
                                     obuf[:, g, :]).then_inc(s_st, 16)
                sy.wait_ge(s_st, 16 * NCHK * 8)

            @blk.scalar
            def _(act):
                act = EA(act)
                for i in range(NCHK):
                    act.wait_ge(s_x, 16 * (i + 1))
                    ln.act_emit(act, i, xc[:, :, :])

            @blk.tensor
            def _(te):
                for i in range(NCHK):
                    ln.pe_emit(te, i, p.sb["ones_col_bf"][:, :],
                               p.sb["ones_row_f32"][:, :])
                    te.wait_ge(ln.sv, 2 * i + 2)
                    for g, m, ops in mm_groups(i):
                        gi = i * 8 + g
                        if gi >= 2:
                            te.wait_ge(s_o, gi - 1)
                        last = None
                        for (wT, koff) in ops:
                            for q in range(4):
                                first = (wT is ops[0][0]) and q == 0
                                last = te.matmul(ps_o[g % 2][:, :],
                                                 wT[:, q, m * P:(m + 1) * P],
                                                 lnb[:, koff + q, :],
                                                 start=(koff == ops[0][1] and q == 0),
                                                 stop=(koff == ops[1][1] and q == 3))
                        last.then_inc(s_mm, 1)

            @blk.vector
            def _(v):
                v = EA(v)
                for i in range(NCHK):
                    if i > 0:
                        v.wait_ge(s_mm, 8 * i)
                    ln.v_emit(v, i, xc[:, :, :], lnb)
                    for g in range(8):
                        gi = i * 8 + g
                        v.wait_ge(s_mm, gi + 1)
                        if i > 0:
                            v.wait_ge(s_st, 16 * i * 8)
                        v.scalar_tensor_tensor(obuf[:, g, :], ps_o[g % 2][:, :],
                                               bias[:, g:g + 1],
                                               xc[:, g, :], ALU.add,
                                               ALU.add).then_inc(s_o, 1)



TT = 4096
NCORES = 8
SUBLAYERS = ["l0a", "l0f", "l1a", "l1f", "l2a", "l2f", "l3a", "l3f"]

_WSPECS = {
    "l0a": [("L0_attn_W_x", [96, 1024], F32), ("L0_attn_W_dt", [1024, 64], F32),
            ("L0_attn_b_dt", [1024], F32), ("L0_attn_log_A", [1024, 16], F32),
            ("L0_attn_D_param", [1024], F32)],
    "l0f": [("L0_ffn_router_w", [8, 1024], F32),
            ("L0_ffn_expert_fc", [8, 1024, 4096], F32),
            ("L0_ffn_expert_proj", [8, 1024, 4096], F32)],
    "l1a": [("L1_attn_w_qkv", [1024, 3072], F32),
            ("L1_attn_w_out", [1024, 1024], F32),
            ("L1_attn_b_out", [1024], F32)],
    "l1f": [("L1_ffn_w_gate", [3072, 1024], I32),
            ("L1_ffn_w_up", [3072, 1024], I32),
            ("L1_ffn_w_down", [1024, 3072], I32)],
    "l2a": [("L2_attn_Wq", [1024, 1024], F32), ("L2_attn_bq", [1024], F32),
            ("L2_attn_Wk", [1024, 1024], F32), ("L2_attn_bk", [1024], F32),
            ("L2_attn_Wv", [1024, 1024], F32), ("L2_attn_bv", [1024], F32),
            ("L2_attn_Wo", [1024, 1024], F32), ("L2_attn_bo", [1024], F32)],
    "l2f": [("L2_ffn_G1_R", [1, 32, 4, 32], F32), ("L2_ffn_G2_R", [4, 16, 1, 16], F32),
            ("L2_ffn_G1_I", [1, 32, 4, 32], F32), ("L2_ffn_G2_I", [4, 16, 1, 16], F32),
            ("L2_ffn_bias_real", [512], F32), ("L2_ffn_bias_imag", [512], F32)],
    "l3a": [("L3_attn_qkv_w", [3072, 1024], F32),
            ("L3_attn_proj_w", [1024, 1024], F32),
            ("L3_attn_proj_b", [1024], F32),
            ("L3_attn_q_norm_w", [64], F32), ("L3_attn_k_norm_w", [64], F32)],
    "l3f": [("L3_ffn_Wr", [1024, 8], F32), ("L3_ffn_W1", [8, 1024, 4096], F32),
            ("L3_ffn_W2", [8, 4096, 1024], F32), ("core_lt_mask", [8, 1], F32)],
}


def build_program(which, T=TT, ncores=NCORES):
    nc = bass.Bass("TRN2", target_bir_lowering=False)
    p = Prog(nc, T)
    ins = {}
    for s in which:
        for (nm, shp, dt) in _WSPECS[s]:
            ins[nm] = nc.dram_tensor(nm, list(shp), dt, kind="ExternalInput")
    x_in = nc.dram_tensor("xT", [D, T], F32, kind="ExternalInput")
    last = which[-1]
    if last == "l3f":
        y_out = nc.dram_tensor("y", [T, D], F32, kind="ExternalOutput")
    else:
        y_out = nc.dram_tensor("y", [D, T], F32, kind="ExternalOutput")
    xb = [nc.dram_tensor("xb0", [D, T], F32, kind="Internal"),
          nc.dram_tensor("xb1", [D, T], F32, kind="Internal")]

    emit_consts(p)

    def _emit_one(s, cur, nxt):
        if s == "l0a":
            emit_l0_mamba(p, cur, nxt, ins["L0_attn_W_x"], ins["L0_attn_W_dt"],
                          ins["L0_attn_b_dt"], ins["L0_attn_log_A"],
                          ins["L0_attn_D_param"])
        elif s == "l0f":
            xe = nc.dram_tensor("l0f_xe", [8 * 768 + 1, 1024], BF16, kind="Internal")
            ye = nc.dram_tensor("l0f_ye", [8 * 768 + 1, 1024], F32, kind="Internal")
            emit_l0_moe(p, cur, nxt, ins["L0_ffn_router_w"],
                        ins["L0_ffn_expert_fc"], ins["L0_ffn_expert_proj"],
                        xe, ye)
        elif s == "l1a":
            emit_l1_attn(p, cur, nxt, ins["L1_attn_w_qkv"],
                         ins["L1_attn_w_out"], ins["L1_attn_b_out"])
        elif s == "l1f":
            emit_l1_ffn(p, cur, nxt, ins["L1_ffn_w_gate"], ins["L1_ffn_w_up"],
                        ins["L1_ffn_w_down"])
        elif s == "l2a":
            emit_attn(p, "l2", cur, nxt,
                      {k: ins["L2_attn_" + k] for k in
                       ("Wq", "bq", "Wk", "bk", "Wv", "bv", "Wo", "bo")}, {})
        elif s == "l2f":
            wtmp = nc.dram_tensor("l2f_wtmp", [1024, 256], F32, kind="Internal")
            emit_l2_ffn(p, cur, nxt, ins["L2_ffn_G1_R"], ins["L2_ffn_G2_R"],
                        ins["L2_ffn_G1_I"], ins["L2_ffn_G2_I"],
                        ins["L2_ffn_bias_real"], ins["L2_ffn_bias_imag"], wtmp)
        elif s == "l3a":
            scr = {"qkvT": nc.dram_tensor("l3a_qkvT", [1024, 3072], BF16,
                                          kind="Internal"),
                   "pwT": nc.dram_tensor("l3a_pwT", [1024, 1024], BF16,
                                         kind="Internal")}
            emit_attn(p, "l3", cur, nxt,
                      {"qkv_w": ins["L3_attn_qkv_w"],
                       "proj_w": ins["L3_attn_proj_w"],
                       "proj_b": ins["L3_attn_proj_b"],
                       "q_norm_w": ins["L3_attn_q_norm_w"],
                       "k_norm_w": ins["L3_attn_k_norm_w"]}, scr)
        elif s == "l3f":
            xe = nc.dram_tensor("l3f_xe", [8 * 768 + 1, 1024], BF16, kind="Internal")
            ye = nc.dram_tensor("l3f_ye", [8 * 768 + 1, 1024], F32, kind="Internal")
            lns = nc.dram_tensor("l3f_lns", [T, 1024], BF16, kind="Internal")
            cd = nc.dram_tensor("l3f_cd", [1, 8], F32, kind="Internal")
            acd = nc.dram_tensor("l3f_acd", [8, 8], F32, kind="Internal")
            cap = (T * ncores) // 8
            emit_l3_moe(p, cur, nxt, ins["L3_ffn_Wr"], ins["L3_ffn_W1"],
                        ins["L3_ffn_W2"], xe, ye, lns, cd, acd,
                        ins["core_lt_mask"], cap, ncores)

    cur = x_in
    for si, s in enumerate(which):
        is_last = si == len(which) - 1
        nxt = y_out if is_last else xb[si % 2]
        with nc.cleanup_on_exit():
            _emit_one(s, cur, nxt)
        cur = nxt
    nc.finalize()
    return nc


GROUPS = [SUBLAYERS]

_PROG_CACHE = {}


def _get_prog(which):
    key = tuple(which)
    if key not in _PROG_CACHE:
        _PROG_CACHE[key] = build_program(list(which))
    return _PROG_CACHE[key]


def kernel(**inputs):
    x = np.ascontiguousarray(inputs["L0_attn_u"], dtype=np.float32)
    B = x.shape[0]
    cur = [np.ascontiguousarray(x[c].T) for c in range(B)]

    for which in GROUPS:
        t0 = time.time()
        nc = _get_prog(which)
        t1 = time.time()
        in_maps = []
        for c in range(B):
            m = {"xT": cur[c]}
            for s in which:
                for (nm, shp, dt) in _WSPECS[s]:
                    if nm == "core_lt_mask":
                        m[nm] = (np.arange(8) < c).astype(
                            np.float32).reshape(8, 1)
                    else:
                        m[nm] = np.ascontiguousarray(inputs[nm])
            in_maps.append(m)
        t2 = time.time()
        res = run_bass_kernel_spmd(nc, in_maps, core_ids=list(range(B)))
        cur = [np.asarray(res.results[c]["y"]) for c in range(B)]
        print(f"[kernel] {'+'.join(which)}: build {t1 - t0:.1f}s "
              f"inmaps {t2 - t1:.1f}s run {time.time() - t2:.1f}s",
              file=sys.stderr, flush=True)

    return np.stack(cur, axis=0)
